# version 17
# speedup vs baseline: 1.1348x; 1.0101x over previous
from contextlib import ExitStack

import numpy as np
import concourse.bass as bass
import concourse.mybir as mybir
from concourse.bass_utils import run_bass_kernel_spmd
from concourse.bass_core_rust import IndirectOffsetOnAxis

F32 = mybir.dt.float32
BF16 = mybir.dt.bfloat16
I32 = mybir.dt.int32
RSQRT_MAGIC = 0x5F3759DF
AF = mybir.ActivationFunctionType
ALU = mybir.AluOpType
AX = mybir.AxisListType
f32 = np.float32

B, S, D = 16, 2048, 1024
NCORES = 8
SEQ_PER_CORE = B // NCORES
TOK = SEQ_PER_CORE * S
NT = TOK // 128
NEXP = 16
NPE = 18
ST = 2048
NT_ST = ST // 128
NST = TOK // ST
QT = S // 128
EPS = f32(1e-5)
KC = 8
CAP = 384
CAPT = CAP // 128
ZROW = 16 * CAP


class Sched:
    ENGS = ("t", "v", "s", "g", "y")

    def __init__(self):
        self.recs = []
        self.counts = {}
        self.barrier_level = {}

    def _chk(self, deps):
        waits = {}
        for d in deps:
            if d is None:
                continue
            sem, val = d
            assert val <= self.counts.get(sem, 0), (sem, val, self.counts.get(sem, 0))
            if waits.get(sem, 0) < val:
                waits[sem] = val
        for sem in [s for s, v in waits.items()
                    if s not in Sched.ENGS and v <= self.barrier_level.get(s, -1)]:
            del waits[sem]
        for sem, val in waits.items():
            if sem not in Sched.ENGS:
                assert val == self.counts.get(sem, 0), (
                    "unsafe intermediate DMA wait", sem, val, self.counts.get(sem, 0))
        return list(waits.items())

    def op(self, eng, fn, deps=()):
        waits = self._chk(deps)
        self.counts[eng] = self.counts.get(eng, 0) + 1
        tok = (eng, self.counts[eng])
        self.recs.append((eng, waits, fn, 1, eng))
        return tok

    def dma(self, q, fn, deps=(), ch="dy"):
        pass
        assert q in ("y", "g", "s")
        waits = self._chk(deps)
        sem = ch
        self.counts[sem] = self.counts.get(sem, 0) + 16
        tok = (sem, self.counts[sem])
        self.recs.append((q, waits, fn, 16, sem))
        return tok

    def wait_only(self, eng, deps):
        waits = self._chk(deps)
        if waits:
            self.recs.append((eng, waits, None, 0, None))

    def barrier(self):
        toks = [(s, v) for s, v in self.counts.items()]
        for e in self.ENGS:
            self.wait_only(e, toks)
        for s, v in self.counts.items():
            if s not in Sched.ENGS:
                self.barrier_level[s] = v


class Buf:

    def __init__(self):
        self.w = None
        self.r = {}

    def write_deps(self):
        return list(self.r.values()) or ([self.w] if self.w else [])

    def read_deps(self):
        return [self.w] if self.w else []

    def wrote(self, tok):
        self.w = tok
        self.r = {}

    def read(self, tok):
        self.r[tok[0]] = tok


def build(stage="full"):
    nc = bass.Bass("TRN2", target_bir_lowering=False)
    _uid = [0]

    def nm(base):
        _uid[0] += 1
        return f"{base}_{_uid[0]}"

    x_in = nc.dram_tensor("x", [TOK, D], F32, kind="ExternalInput").ap()
    wq = nc.dram_tensor("Wq", [D, D], F32, kind="ExternalInput").ap()
    bq = nc.dram_tensor("bq", [1, D], F32, kind="ExternalInput").ap()
    wk = nc.dram_tensor("Wk", [D, 256], F32, kind="ExternalInput").ap()
    bk = nc.dram_tensor("bk", [1, 256], F32, kind="ExternalInput").ap()
    wv = nc.dram_tensor("Wv", [D, 256], F32, kind="ExternalInput").ap()
    bv = nc.dram_tensor("bv", [1, 256], F32, kind="ExternalInput").ap()
    ln_g = nc.dram_tensor("ln_g", [1, D], F32, kind="ExternalInput").ap()
    ln_b = nc.dram_tensor("ln_b", [1, D], F32, kind="ExternalInput").ap()
    wo = nc.dram_tensor("Wo", [D, D], F32, kind="ExternalInput").ap()
    bo = nc.dram_tensor("bo", [1, D], F32, kind="ExternalInput").ap()
    wg = nc.dram_tensor("Wg", [NEXP, D], F32, kind="ExternalInput").ap()
    w1 = nc.dram_tensor("W1", [NEXP, D, 2 * 512], F32, kind="ExternalInput").ap()
    w2 = nc.dram_tensor("W2", [NEXP, 512, D], F32, kind="ExternalInput").ap()
    ws1 = nc.dram_tensor("Ws1", [D, 2048], F32, kind="ExternalInput").ap()
    ws2 = nc.dram_tensor("Ws2", [1024, D], F32, kind="ExternalInput").ap()
    out = nc.dram_tensor("out", [TOK, D], F32, kind="ExternalOutput").ap()
    x1 = nc.dram_tensor("x1", [TOK, D], F32, kind="Internal").ap()
    x2 = nc.dram_tensor("x2", [TOK, D], F32, kind="Internal").ap()
    xhat_dram = nc.dram_tensor("xhat_d", [TOK, D], BF16, kind="Internal").ap()
    xg_dram = [nc.dram_tensor(f"xg_d{i}", [ZROW + 1, D], BF16, kind="Internal").ap()
               for i in range(2)]
    zg_dram = [nc.dram_tensor(f"zg_d{i}", [ZROW + 1, D], F32, kind="Internal").ap()
               for i in range(2)]
    wbf1 = nc.dram_tensor("wbf1", [NPE, 128, KC * 1024], BF16, kind="Internal").ap()
    wbf2 = nc.dram_tensor("wbf2", [NPE, 128, 4 * 1024], BF16, kind="Internal").ap()

    sch = Sched()

    with ExitStack() as top:
        E = top.enter_context
        ones_row = E(nc.sbuf_tensor(nm("ones_row"), [1, 128], F32))
        ident = E(nc.sbuf_tensor(nm("ident"), [128, 128], BF16))
        maskdT = E(nc.sbuf_tensor(nm("maskdT"), [128, 128], F32))
        ident_f = E(nc.sbuf_tensor(nm("ident_f"), [128, 128], F32))
        wgT = E(nc.sbuf_tensor(nm("wgT"), [128, 128], BF16))
        ltri_bf = E(nc.sbuf_tensor(nm("ltri_bf"), [128, 128], BF16))
        ones_col_bf = E(nc.sbuf_tensor(nm("ones_col_bf"), [128, 1], BF16))
        ecb = E(nc.sbuf_tensor(nm("ecb"), [128, 16], F32))
        iotae = E(nc.sbuf_tensor(nm("iotae"), [128, 16], F32))


        with nc.psum_tensor(nm("set_ps"), [128, 512], F32) as set_ps, \
             nc.psum_tensor(nm("set_ps2"), [128, 128], BF16) as set_ps2, \
             nc.sbuf_tensor(nm("rows_sb"), [1, D], F32) as rows_sb, \
             nc.sbuf_tensor(nm("zero_row"), [1, D], F32) as zero_row:
            t0 = sch.op("g", lambda g: g.memset(ones_row[:], 1.0))
            t1 = sch.op("g", lambda g: g.memset(ident[:], 0.0))
            t2 = sch.op("g", lambda g: g.affine_select(
                out=ident[:], in_=ident[:], compare_op=ALU.not_equal, fill=1.0,
                base=0, pattern=[[-1, 128]], channel_multiplier=1), deps=[t1])
            t5 = sch.op("g", lambda g: g.memset(maskdT[:], 0.0))
            t6 = sch.op("g", lambda g: g.affine_select(
                out=maskdT[:], in_=maskdT[:], compare_op=ALU.is_ge, fill=-1e10,
                base=0, pattern=[[1, 128]], channel_multiplier=-1), deps=[t5])
            t7 = sch.op("g", lambda g: g.memset(ident_f[:], 0.0))
            t8 = sch.op("g", lambda g: g.affine_select(
                out=ident_f[:], in_=ident_f[:], compare_op=ALU.not_equal, fill=1.0,
                base=0, pattern=[[-1, 128]], channel_multiplier=1), deps=[t7])
            t9 = sch.op("g", lambda g: g.memset(ltri_bf[:], 0.0))
            t10 = sch.op("g", lambda g: g.affine_select(
                out=ltri_bf[:], in_=ltri_bf[:], compare_op=ALU.is_ge, fill=1.0,
                base=0, pattern=[[-1, 128]], channel_multiplier=1), deps=[t9])
            t11 = sch.op("g", lambda g: g.memset(ones_col_bf[:], 1.0))
            t12 = sch.op("g", lambda g: g.iota(
                ecb[:], pattern=[[CAP, 16]], base=0, channel_multiplier=0,
                allow_small_or_imprecise_dtypes=True))
            t13 = sch.op("g", lambda g: g.iota(
                iotae[:], pattern=[[1, 16]], base=0, channel_multiplier=0,
                allow_small_or_imprecise_dtypes=True))
            tok_const = [t0, t2, t6, t8, t10, t11, t12, t13]
            tz = sch.op("g", lambda g: g.memset(zero_row[:1, :], 0.0))
            for i in range(2):
                sch.dma("y", (lambda y, i=i: y.dma_start(
                    zg_dram[i][ZROW:ZROW + 1, :], zero_row[:1, :])),
                    deps=[tz], ch=f"zsink{i}")

            b_rows = Buf()
            b_setps = Buf()

            def _unused_bcast(dst, row_dram, n, scale=None):
                dld = sch.dma("y", (lambda y, n=n, row_dram=row_dram: y.dma_start(
                    rows_sb[:1, 0:n], row_dram)), deps=b_rows.write_deps(), ch="drw")
                b_rows.wrote(dld)
                if scale is not None:
                    dsc = sch.op("s", (lambda s, n=n, scale=scale: s.activation(
                        rows_sb[:1, 0:n], rows_sb[:1, 0:n], AF.Identity,
                        bias=f32(0.0), scale=f32(scale))), deps=[dld])
                    b_rows.wrote(dsc)
                    src_tok = dsc
                else:
                    src_tok = dld
                nj = (n + 511) // 512
                last = None
                for j in range(nj):
                    w = min(512, n - j * 512)
                    tmm = sch.op("t", (lambda t, j=j, w=w: t.matmul(
                        set_ps[:, 0:w], ones_row[:1, :],
                        rows_sb[:1, j * 512:j * 512 + w], start=True, stop=True)),
                        deps=[src_tok, t0] + b_setps.write_deps())
                    b_rows.read(tmm)
                    b_setps.wrote(tmm)
                    tev = sch.op("v", (lambda v, j=j, w=w, dst=dst: v.tensor_copy(
                        dst[:, j * 512:j * 512 + w], set_ps[:, 0:w])), deps=[tmm])
                    b_setps.read(tev)
                    last = tev
                return last

            tok_bcast = []

            with nc.sbuf_tensor(nm("wg_tmp"), [16, D], BF16) as wg_tmp:
                dwg = sch.dma("g", lambda g: g.dma_start(wg_tmp[:], wg), ch="dwg")
                tprev = None
                for c in range(KC):
                    ttr = sch.op("t", (lambda t, c=c: t.transpose(
                        set_ps2[:, 0:16], wg_tmp[:16, c * 128:(c + 1) * 128],
                        ident[0:16, 0:16])), deps=[dwg, t2] + ([tprev] if tprev else []))
                    tev = sch.op("v", (lambda v, c=c: v.tensor_copy(
                        wgT[:, c * 16:(c + 1) * 16], set_ps2[:, 0:16])), deps=[ttr])
                    tprev = tev
                tok_wgT = tprev
            sch.barrier()

        lnst = [E(nc.sbuf_tensor(nm(f"lnst{i}"), [128, 12], F32)) for i in range(2)]
        lnag = [E(nc.sbuf_tensor(nm(f"lnag{i}"), [128, 2], F32)) for i in range(2)]
        lnr = [E(nc.sbuf_tensor(nm(f"lnr{i}"), [128, 4], F32)) for i in range(2)]

        def emit_rsqrt(x_ap, xe_ap, y_ap, u_ap, eps, deps, iters=2):
            tk = sch.op("v", (lambda v: v.tensor_scalar_add(xe_ap, x_ap, eps)), deps=deps)
            tk = sch.op("v", (lambda v: v.tensor_scalar(
                out=y_ap.bitcast(I32), in0=xe_ap.bitcast(I32), scalar1=1,
                scalar2=None, op0=ALU.arith_shift_right)), deps=[tk])
            tk = sch.op("v", (lambda v: v.tensor_scalar(
                out=y_ap.bitcast(I32), in0=y_ap.bitcast(I32), scalar1=-1,
                scalar2=RSQRT_MAGIC, op0=ALU.mult, op1=ALU.add)), deps=[tk])
            for _ in range(iters):
                tk = sch.op("v", (lambda v: v.scalar_tensor_tensor(
                    out=u_ap, in0=y_ap, scalar=1.0, in1=y_ap,
                    op0=ALU.bypass, op1=ALU.mult)), deps=[tk])
                tk = sch.op("v", (lambda v: v.scalar_tensor_tensor(
                    out=u_ap, in0=u_ap, scalar=-0.5, in1=xe_ap,
                    op0=ALU.mult, op1=ALU.mult)), deps=[tk])
                tk = sch.op("v", (lambda v: v.scalar_tensor_tensor(
                    out=y_ap, in0=u_ap, scalar=1.5, in1=y_ap,
                    op0=ALU.add, op1=ALU.mult)), deps=[tk])
            return tk

        def emit_ln(par, src_ap, dst_ap, deps, scratch_r=None):
            stt = lnst[par]
            agg = lnag[par]
            rr = lnr[par] if scratch_r is None else scratch_r
            t_a = sch.op("v", lambda v: v.bn_stats(stt[:, 0:6], src_ap[:, 0:512]), deps=deps)
            t_b = sch.op("v", lambda v: v.bn_stats(stt[:, 6:12], src_ap[:, 512:1024]), deps=deps)
            t_c = sch.op("v", lambda v: v.bn_aggr(agg[:], stt[:]), deps=[t_a, t_b])
            t_e = emit_rsqrt(agg[:, 1:2], rr[:, 2:3], rr[:, 0:1], rr[:, 3:4],
                             float(EPS), [t_c], iters=2)
            t_f = sch.op("v", lambda v: v.scalar_tensor_tensor(
                out=rr[:, 1:2], in0=agg[:, 0:1], scalar=-1.0, in1=rr[:, 0:1],
                op0=ALU.mult, op1=ALU.mult), deps=[t_e, t_c])
            t_g = sch.op("s", lambda s: s.activation(
                dst_ap, src_ap, AF.Identity, bias=rr[:, 1:2], scale=rr[:, 0:1]),
                deps=[t_f, t_e] + list(deps))
            return t_g, (rr[:, 0:1], rr[:, 1:2]), [t_a, t_b, t_c, t_e, t_f, t_g]

        store_toks = {"x1": [None] * NT, "x2": [None] * NT, "out": [None] * NT}
        b_wbf1 = [Buf() for _ in range(NPE)]
        b_wbf2 = [Buf() for _ in range(NPE)]

        def emit_ffn(src_dram, dst_dram, dst_key, src_toks, first, sparse):
            with ExitStack() as ffn:
                FE = ffn.enter_context
                zacc = FE(nc.sbuf_tensor(nm("zacc"), [128, NT_ST * D], F32))
                cw_all = [FE(nc.sbuf_tensor(nm(f"cw_all{i}"), [128, NT_ST * 16], F32)) for i in range(2)]
                glog_all = cw_all
                w1s = [FE(nc.sbuf_tensor(nm(f"w1s{i}"), [128, KC * 1024], BF16)) for i in range(2)]
                w2s = [FE(nc.sbuf_tensor(nm(f"w2s{i}"), [128, 4 * 1024], BF16)) for i in range(2)]
                xtile = [FE(nc.sbuf_tensor(nm(f"xtile{i}"), [128, D], F32)) for i in range(2)]
                xhat_tm = [FE(nc.sbuf_tensor(nm(f"xhat_tm{i}"), [128, D], BF16)) for i in range(2)]
                if sparse:
                    xg_sb = [FE(nc.sbuf_tensor(nm(f"xg_sb{i}"), [128, D], BF16)) for i in range(3)]
                    xgT = [FE(nc.sbuf_tensor(nm(f"xgT{i}"), [128, KC * 128], BF16)) for i in range(2)]
                    _xfm = FE(nc.sbuf_tensor(nm("xhat_fm"), [128, KC * ST], BF16))
                    xhat_fm = [_xfm, _xfm]
                else:
                    xhat_fm = [FE(nc.sbuf_tensor(nm(f"xhat_fm{i}"), [128, KC * ST], BF16)) for i in range(2)]
                sg = [FE(nc.sbuf_tensor(nm(f"sg{i}"), [128, 512], BF16)) for i in range(2)]
                h_bf = [FE(nc.sbuf_tensor(nm(f"h_bf{i}"), [128, 512], BF16)) for i in range(2)]
                hT_sb = [FE(nc.sbuf_tensor(nm(f"hT_sb{i}"), [128, 512], BF16)) for i in range(3)]
                zg_sb = [FE(nc.sbuf_tensor(nm(f"zg_sb{i}"), [128, D], F32)) for i in range(2)] if sparse else None
                if sparse:
                    base_row = FE(nc.sbuf_tensor(nm("base_row"), [1, 16], F32))
                    maskf = [FE(nc.sbuf_tensor(nm(f"maskf{i}"), [128, 16], F32)) for i in range(2)]
                    mask_bf = [FE(nc.sbuf_tensor(nm(f"mask_bf{i}"), [128, 16], BF16)) for i in range(2)]
                    mask_all = [FE(nc.sbuf_tensor(nm(f"mask_all{i}"), [128, NT_ST * 16], BF16)) for i in range(2)]
                    rsc = FE(nc.sbuf_tensor(nm("rsc"), [128, 6 * 16], F32))
                    dk_all = [FE(nc.sbuf_tensor(nm(f"dk_all{i}"), [128, NT_ST * 2], I32)) for i in range(2)]
                    dkf = FE(nc.sbuf_tensor(nm("dkf"), [128, 2], F32))
                    cwk_all = [FE(nc.sbuf_tensor(nm(f"cwk_all{i}"), [128, NT_ST * 2], F32)) for i in range(2)]
                else:
                    base_row = maskf = mask_bf = mask_all = rsc = dk_all = dkf = cwk_all = None
                _gp = FE(nc.sbuf_tensor(nm("gp16"), [128, 16], F32))
                gp16 = [_gp, _gp]
                _s16 = FE(nc.sbuf_tensor(nm("s16"), [128, 16], F32))
                s16 = [_s16, _s16]
                _m8 = FE(nc.sbuf_tensor(nm("m8"), [128, 8], F32))
                m8 = [_m8, _m8]
                _gsc = FE(nc.sbuf_tensor(nm("gsc"), [128, 4], F32))
                gsc = [_gsc, _gsc]
                xctile_s = FE(nc.sbuf_tensor(nm("xctile"), [128, D], F32))
                xctile = [xctile_s, xctile_s]
                zk_sb = [FE(nc.sbuf_tensor(nm(f"zk_sb{i}"), [128, D], F32)) for i in range(2)] if sparse else None
                xnew_s = FE(nc.sbuf_tensor(nm("xnew"), [128, D], F32))
                xnew = [xnew_s, xnew_s]

                y_ps = [FE(nc.psum_tensor(nm(f"y_ps{i}"), [128, 1024], F32)) for i in range(2)]
                hT_ps = [FE(nc.psum_tensor(nm(f"hT_ps{i}"), [128, 1024], BF16)) for i in range(2)]
                w2o = FE(nc.psum_tensor(nm("w2o"), [128, 1024], F32))
                glog_ps = w2o[:, 0:16]
                pfx_ps = w2o[:, 16:32]
                tot_ps = w2o[0:1, 32:48]
                baseb_ps = w2o[:, 48:64]

                b_xt = [Buf(), Buf()]
                b_xhtm = [Buf(), Buf()]
                b_xg = [Buf(), Buf(), Buf()]
                if sparse:
                    _bxf = [Buf() for _ in range(NT_ST)]
                    b_xfm = [_bxf, _bxf]
                else:
                    b_xfm = [[Buf() for _ in range(NT_ST)] for _ in range(2)]
                b_xgT = [Buf(), Buf()]
                b_cw = [[Buf() for _ in range(NT_ST)] for _ in range(2)]
                b_mall = [[Buf() for _ in range(NT_ST)] for _ in range(2)]
                b_z = [Buf() for _ in range(NT_ST)]
                b_w1 = [Buf(), Buf()]
                b_w2 = [Buf(), Buf()]
                b_yps = [Buf(), Buf()]
                b_hTps = [Buf(), Buf()]
                b_w2o = Buf()
                b_sg = [Buf(), Buf()]
                b_h = [Buf(), Buf()]
                b_hTsb = [Buf(), Buf(), Buf()]
                _bg = Buf()
                b_g16 = [_bg, _bg]
                b_zg = [Buf(), Buf()]
                b_base = Buf()
                b_baser = Buf()
                b_rsc = Buf()
                b_dk = [[Buf() for _ in range(NT_ST)] for _ in range(2)]
                b_cwk = [[Buf() for _ in range(NT_ST)] for _ in range(2)]
                b_xgd = [Buf(), Buf()]
                b_zgd = [Buf(), Buf()]
                scat_last = [[None, None], [None, None]]
                zg_last = [[None, None], [None, None]]
                _bxc = Buf()
                b_xc = [_bxc, _bxc]
                b_zk = [Buf(), Buf()]
                _bxnf = Buf()
                b_xn = [_bxnf, _bxnf]

                def wdma(e, cast):
                    sl = e % 2
                    d1 = b_w1[sl].write_deps()
                    d2 = b_w2[sl].write_deps()
                    if cast:
                        if e < NEXP:
                            tk1 = sch.dma("g", (lambda g, e=e, sl=sl: g.dma_start(
                                w1s[sl][:].rearrange("p (c n) -> p c n", c=KC),
                                w1[e].rearrange("(c p) n -> p c n", p=128))), deps=d1, ch=f"w1{sl}")
                            tk2 = sch.dma("g", (lambda g, e=e, sl=sl: g.dma_start(
                                w2s[sl][:].rearrange("p (c n) -> p c n", c=4),
                                w2[e].rearrange("(c p) n -> p c n", p=128))), deps=d2, ch=f"w2{sl}")
                        else:
                            j = e - NEXP
                            tk1a = sch.dma("g", (lambda g, j=j, sl=sl: g.dma_start(
                                w1s[sl][:].rearrange("p (c n) -> p c n", c=KC)[:, :, 0:512],
                                ws1.rearrange("(c p) n -> p c n", p=128)[:, :, j * 512:(j + 1) * 512])),
                                deps=d1, ch=f"w1{sl}")
                            tk1 = sch.dma("g", (lambda g, j=j, sl=sl: g.dma_start(
                                w1s[sl][:].rearrange("p (c n) -> p c n", c=KC)[:, :, 512:1024],
                                ws1.rearrange("(c p) n -> p c n", p=128)[:, :, 1024 + j * 512:1024 + (j + 1) * 512])),
                                deps=[tk1a], ch=f"w1{sl}")
                            tk2 = sch.dma("g", (lambda g, j=j, sl=sl: g.dma_start(
                                w2s[sl][:].rearrange("p (c n) -> p c n", c=4),
                                ws2[j * 512:(j + 1) * 512, :].rearrange("(c p) n -> p c n", p=128))),
                                deps=d2, ch=f"w2{sl}")
                        ts1 = sch.dma("s", (lambda y, e=e, sl=sl: y.dma_start(
                            wbf1[e], w1s[sl][:])), deps=[tk1], ch=f"wb1{e}")
                        ts2 = sch.dma("s", (lambda y, e=e, sl=sl: y.dma_start(
                            wbf2[e], w2s[sl][:])), deps=[tk2], ch=f"wb2{e}")
                        b_w1[sl].wrote(tk1)
                        b_w1[sl].read(ts1)
                        b_w2[sl].wrote(tk2)
                        b_w2[sl].read(ts2)
                        b_wbf1[e].wrote(ts1)
                        b_wbf2[e].wrote(ts2)
                    else:
                        tk1 = sch.dma("y", (lambda y, e=e, sl=sl: y.dma_start(
                            w1s[sl][:], wbf1[e])), deps=d1 + b_wbf1[e].read_deps(),
                            ch=f"wB1{sl}")
                        tk2 = sch.dma("y", (lambda y, e=e, sl=sl: y.dma_start(
                            w2s[sl][:], wbf2[e])), deps=d2 + b_wbf2[e].read_deps(),
                            ch=f"wB2{sl}")
                        b_wbf1[e].read(tk1)
                        b_wbf2[e].read(tk2)
                        b_w1[sl].wrote(tk1)
                        b_w2[sl].wrote(tk2)

                tln_pend = {}

                def passA_front(st_i, tt):
                    par = tt % 2
                    r0 = st_i * ST + tt * 128
                    dld = sch.dma("y", (lambda y, r0=r0, par=par: y.dma_start(
                        xtile[par][:], src_dram[r0:r0 + 128, :])),
                        deps=b_xt[par].write_deps() + ([src_toks[(r0 // 128)]] if src_toks else []),
                        ch=f"xa{par}")
                    b_xt[par].wrote(dld)
                    tln, _, lntoks = emit_ln(par, xtile[par][:], xhat_tm[par][:],
                                             [dld] + b_xhtm[par].write_deps())
                    for tk in lntoks[:3]:
                        b_xt[par].read(tk)
                    b_xt[par].read(tln)
                    b_xhtm[par].wrote(tln)
                    tln_pend[(st_i, tt)] = tln

                def passA_back(st_i, tt):
                    sp = st_i % 2
                    par = tt % 2
                    tln = tln_pend.pop((st_i, tt))
                    r0 = st_i * ST + tt * 128
                    if sparse and tt == 0 and st_i > 0:
                        trst = sch.op("v", (lambda v: v.memset(base_row[:1, :], 0.0)),
                                      deps=b_baser.read_deps() + b_baser.write_deps())
                        b_baser.wrote(trst)
                    ttr = sch.op("t", (lambda t, par=par: [
                        t.transpose(hT_ps[par][:, c * 128:(c + 1) * 128],
                                    xhat_tm[par][:, c * 128:(c + 1) * 128],
                                    ident[:]) for c in range(KC)][-1]),
                        deps=[tln] + b_hTps[par].write_deps() + tok_const)
                    b_xhtm[par].read(ttr)
                    b_hTps[par].wrote(ttr)
                    tev = sch.op("v", (lambda v, sp=sp, par=par, tt=tt: v.tensor_copy(
                        xhat_fm[sp][:].rearrange("p (c n) -> p c n", c=KC)[:, :, tt * 128:(tt + 1) * 128],
                        hT_ps[par][:].rearrange("p (c n) -> p c n", c=KC))),
                        deps=[ttr] + b_xfm[sp][tt].write_deps())
                    b_hTps[par].read(tev)
                    b_xfm[sp][tt].wrote(tev)
                    gate_lhsT = (lambda c, sp=sp, tt=tt: xhat_fm[sp][:, c * ST + tt * 128:c * ST + (tt + 1) * 128])
                    tgm_deps = [tev]
                    tgm = sch.op("t", (lambda t, gate_lhsT=gate_lhsT: [
                        t.matmul(glog_ps,
                                 gate_lhsT(c),
                                 wgT[:, c * 16:(c + 1) * 16],
                                 start=(c == 0), stop=(c == KC - 1)) for c in range(KC)][-1]),
                        deps=tgm_deps + [tok_wgT] + b_w2o.write_deps())
                    b_xfm[sp][tt].read(tgm)
                    b_w2o.wrote(tgm)
                    tge = sch.op("v", (lambda v, sp=sp, tt=tt: v.tensor_copy(
                        glog_all[sp][:, tt * 16:(tt + 1) * 16], glog_ps)),
                        deps=[tgm] + b_cw[sp][tt].write_deps())
                    b_w2o.read(tge)
                    b_cw[sp][tt].wrote(tge)
                    if not sparse:
                        return
                    tm8 = sch.op("v", (lambda v, par=par, sp=sp, tt=tt: v.max(
                        out=m8[par][:], in_=glog_all[sp][:, tt * 16:(tt + 1) * 16])),
                        deps=[tge] + b_g16[par].write_deps())
                    b_cw[sp][tt].read(tm8)
                    tmk = sch.op("v", (lambda v, par=par, sp=sp, tt=tt: v.tensor_scalar(
                        out=maskf[par][:], in0=glog_all[sp][:, tt * 16:(tt + 1) * 16],
                        scalar1=m8[par][:, 1:2], scalar2=None, op0=ALU.is_ge)),
                        deps=[tm8])
                    b_cw[sp][tt].read(tmk)
                    b_g16[par].wrote(tmk)
                    tmb = sch.op("v", (lambda v, par=par: v.tensor_copy(
                        mask_bf[par][:], maskf[par][:])), deps=[tmk])
                    tma = sch.op("v", (lambda v, par=par, sp=sp, tt=tt: v.tensor_copy(
                        mask_all[sp][:, tt * 16:(tt + 1) * 16], maskf[par][:])),
                        deps=[tmk] + b_mall[sp][tt].write_deps())
                    b_mall[sp][tt].wrote(tma)
                    tpf = sch.op("t", (lambda t, par=par: t.matmul(
                        pfx_ps, ltri_bf[:], mask_bf[par][:], start=True, stop=True)),
                        deps=[tmb, tge] + b_w2o.write_deps() + tok_const)
                    b_w2o.wrote(tpf)
                    ttot = sch.op("t", (lambda t, par=par: t.matmul(
                        tot_ps, ones_col_bf[:, 0:1], mask_bf[par][:], start=True, stop=True)),
                        deps=[tmb, tpf])
                    b_w2o.wrote(ttot)
                    tbb = sch.op("t", (lambda t: t.matmul(
                        baseb_ps, ones_row[:1, :], base_row[:1, :], start=True, stop=True)),
                        deps=[tpf] + b_baser.read_deps() + b_baser.write_deps() + [tok_const[0]])
                    b_baser.read(tbb)
                    b_w2o.wrote(tbb)
                    R = rsc
                    t_s = sch.op("v", (lambda v: v.tensor_copy(R[:, 80:96], baseb_ps)),
                                 deps=[tbb] + b_rsc.write_deps())
                    b_w2o.read(t_s)
                    t_s2 = sch.op("v", (lambda v: v.tensor_add(R[:, 0:16], pfx_ps, R[:, 80:96])),
                                  deps=[t_s, tpf])
                    b_w2o.read(t_s2)
                    t_bu = sch.op("v", (lambda v: v.tensor_add(
                        base_row[:1, :], base_row[:1, :], tot_ps[:1, :])),
                        deps=[ttot, tbb])
                    b_baser.wrote(t_bu)
                    b_w2o.read(t_bu)
                    t_ok = sch.op("v", (lambda v: v.tensor_scalar(
                        out=R[:, 16:32], in0=R[:, 0:16], scalar1=float(CAP),
                        scalar2=None, op0=ALU.is_lt)), deps=[t_s2])
                    t_t1 = sch.op("v", (lambda v: v.tensor_add(R[:, 32:48], R[:, 0:16], ecb[:])),
                                  deps=[t_s2, tok_const[0]])
                    t_c1 = sch.op("v", (lambda v: v.tensor_scalar(
                        out=R[:, 32:48], in0=R[:, 32:48], scalar1=float(ZROW),
                        scalar2=None, op0=ALU.subtract)), deps=[t_t1])
                    t_c2 = sch.op("v", (lambda v: v.tensor_mul(
                        R[:, 32:48], R[:, 32:48], R[:, 16:32])), deps=[t_c1, t_ok])
                    t_c3 = sch.op("v", (lambda v: v.tensor_scalar(
                        out=R[:, 32:48], in0=R[:, 32:48], scalar1=float(ZROW),
                        scalar2=None, op0=ALU.add)), deps=[t_c2])
                    t_m1 = sch.op("v", (lambda v: v.tensor_scalar(
                        out=R[:, 48:64], in0=R[:, 32:48], scalar1=float(ZROW),
                        scalar2=None, op0=ALU.subtract)), deps=[t_c3])
                    t_m2 = sch.op("v", (lambda v, par=par: v.tensor_mul(
                        R[:, 48:64], R[:, 48:64], maskf[par][:])), deps=[t_m1, tmk])
                    t_m3 = sch.op("v", (lambda v: v.tensor_scalar(
                        out=R[:, 48:64], in0=R[:, 48:64], scalar1=float(ZROW),
                        scalar2=None, op0=ALU.add)), deps=[t_m2])
                    t_x1 = sch.op("v", (lambda v: v.tensor_scalar(
                        out=R[:, 64:80], in0=R[:, 32:48], scalar1=1.0,
                        scalar2=None, op0=ALU.add)), deps=[t_c3])
                    t_x2 = sch.op("v", (lambda v, par=par: v.tensor_mul(
                        R[:, 64:80], R[:, 64:80], maskf[par][:])), deps=[t_x1, tmk])
                    t_x3 = sch.op("v", (lambda v: v.tensor_scalar(
                        out=R[:, 64:80], in0=R[:, 64:80], scalar1=1.0,
                        scalar2=None, op0=ALU.subtract)), deps=[t_x2])
                    t_d0 = sch.op("v", (lambda v: v.tensor_reduce(
                        dkf[:, 0:1], R[:, 48:64], axis=AX.X, op=ALU.min)),
                        deps=[t_m3])
                    t_d1 = sch.op("v", (lambda v: v.reduce_max(
                        dkf[:, 1:2], R[:, 64:80], axis=AX.X)), deps=[t_x3])
                    b_rsc.wrote(t_d1)
                    tci = sch.op("v", (lambda v, sp=sp, tt=tt: v.tensor_copy(
                        dk_all[sp][:, 2 * tt:2 * tt + 2], dkf[:, 0:2])),
                        deps=[t_d0, t_d1] + b_dk[sp][tt].write_deps())
                    b_dk[sp][tt].wrote(tci)
                    for k in range(2):
                        tsc_ = sch.dma("g", (lambda g, sp=sp, par=par, tt=tt, k=k: g.indirect_dma_start(
                            out=xg_dram[sp], out_offset=IndirectOffsetOnAxis(
                                dk_all[sp][:, 2 * tt + k:2 * tt + k + 1], 0),
                            in_=xhat_tm[par][:], in_offset=None)),
                            deps=[tci, tln] + (b_xgd[sp].write_deps() if tt == 0 and k == 0 else [])
                                 + ([b_xgd[sp].w] if b_xgd[sp].w else []),
                            ch=f"sx{sp}{par}")
                        b_xhtm[par].read(tsc_)
                        b_dk[sp][tt].read(tsc_)
                        b_xgd[sp].wrote(tsc_)
                        scat_last[sp][par] = tsc_

                def gate_burst(st_i):
                    sp = st_i % 2
                    for tt in range(NT_ST):
                        par = tt % 2
                        tmx = sch.op("v", (lambda v, sp=sp, par=par, tt=tt: v.reduce_max(
                            gsc[par][:, 0:1], glog_all[sp][:, tt * 16:(tt + 1) * 16],
                            axis=AX.X, negate=True)),
                            deps=b_cw[sp][tt].read_deps() + b_g16[par].write_deps())
                        b_cw[sp][tt].read(tmx)
                        texp = sch.op("s", (lambda s, sp=sp, par=par, tt=tt: s.activation(
                            gp16[par][:], glog_all[sp][:, tt * 16:(tt + 1) * 16],
                            AF.Exp, bias=gsc[par][:, 0:1],
                            accum_out=gsc[par][:, 1:2])), deps=[tmx])
                        b_cw[sp][tt].read(texp)
                        trc = sch.op("v", (lambda v, par=par: v.reciprocal(
                            gsc[par][:, 0:1], gsc[par][:, 1:2])), deps=[texp])
                        tsc = sch.op("v", (lambda v, par=par: v.tensor_scalar(
                            out=s16[par][:], in0=gp16[par][:], scalar1=gsc[par][:, 0:1],
                            scalar2=None, op0=ALU.mult)), deps=[trc])
                        if not sparse:
                            tdm = sch.op("v", (lambda v, par=par: v.max(
                                out=m8[par][:], in_=s16[par][:])), deps=[tsc])
                            tcw = sch.op("v", (lambda v, sp=sp, par=par, tt=tt: v.scalar_tensor_tensor(
                                out=cw_all[sp][:, tt * 16:(tt + 1) * 16], in0=s16[par][:],
                                scalar=m8[par][:, 1:2], in1=s16[par][:],
                                op0=ALU.is_ge, op1=ALU.mult)),
                                deps=[tdm] + b_cw[sp][tt].write_deps())
                            b_cw[sp][tt].wrote(tcw)
                            b_g16[par].wrote(tcw)
                            continue
                        ms = (lambda sp=sp, tt=tt: mask_all[sp][:, tt * 16:(tt + 1) * 16])
                        te0 = sch.op("v", (lambda v, par=par: v.tensor_scalar(
                            out=gp16[par][:], in0=iotae[:], scalar1=16.0,
                            scalar2=None, op0=ALU.subtract)),
                            deps=[tsc, tok_const[0]])
                        te0b = sch.op("v", (lambda v, sp=sp, par=par, tt=tt: v.tensor_mul(
                            gp16[par][:], gp16[par][:],
                            mask_all[sp][:, tt * 16:(tt + 1) * 16])),
                            deps=[te0] + b_mall[sp][tt].read_deps())
                        b_mall[sp][tt].read(te0b)
                        te0c = sch.op("v", (lambda v, par=par: v.tensor_scalar(
                            out=gp16[par][:], in0=gp16[par][:], scalar1=16.0,
                            scalar2=None, op0=ALU.add)), deps=[te0b])
                        te1 = sch.op("v", (lambda v, par=par: v.tensor_reduce(
                            gsc[par][:, 2:3], gp16[par][:], axis=AX.X, op=ALU.min)),
                            deps=[te0c])
                        te2 = sch.op("v", (lambda v, par=par: v.tensor_scalar(
                            out=gp16[par][:], in0=iotae[:], scalar1=1.0,
                            scalar2=None, op0=ALU.add)), deps=[te1])
                        te2b = sch.op("v", (lambda v, sp=sp, par=par, tt=tt: v.tensor_mul(
                            gp16[par][:], gp16[par][:],
                            mask_all[sp][:, tt * 16:(tt + 1) * 16])),
                            deps=[te2])
                        b_mall[sp][tt].read(te2b)
                        te2c = sch.op("v", (lambda v, par=par: v.tensor_scalar(
                            out=gp16[par][:], in0=gp16[par][:], scalar1=1.0,
                            scalar2=None, op0=ALU.subtract)), deps=[te2b])
                        te3 = sch.op("v", (lambda v, par=par: v.reduce_max(
                            gsc[par][:, 3:4], gp16[par][:], axis=AX.X)), deps=[te2c])
                        tq1 = None
                        for k in range(2):
                            tq0 = sch.op("v", (lambda v, par=par, k=k: v.tensor_scalar(
                                out=gp16[par][:], in0=iotae[:],
                                scalar1=gsc[par][:, 2 + k:3 + k], scalar2=None,
                                op0=ALU.is_equal)),
                                deps=[te3] + ([tq1] if tq1 else []))
                            tq1 = sch.op("v", (lambda v, sp=sp, par=par, tt=tt, k=k: v.scalar_tensor_tensor(
                                out=m8[par][:, 0:8].to_broadcast([128, 8]) if False else gp16[par][:],
                                in0=gp16[par][:], scalar=1.0, in1=s16[par][:],
                                op0=ALU.bypass, op1=ALU.mult,
                                accum_out=cwk_all[sp][:, 2 * tt + k:2 * tt + k + 1])),
                                deps=[tq0] + b_cwk[sp][tt].write_deps())
                            b_cwk[sp][tt].wrote(tq1)
                        b_g16[par].wrote(tq1)

                if sparse:
                    tbi = sch.op("v", lambda v: v.memset(base_row[:1, :], 0.0))
                    b_baser.wrote(tbi)
                    passA_front(0, 0)
                    passA_front(0, 1)
                    passA_back(0, 0)
                    passA_front(0, 2)
                    passA_back(0, 1)
                    passA_front(0, 3)
                else:
                    for tt in range(NT_ST):
                        passA_front(0, tt)
                        passA_back(0, tt)
                    gate_burst(0)

                for st_i in range(NST):
                    row0 = st_i * ST
                    sp = st_i % 2
                    if st_i == 0:
                        wdma(16 if sparse else 0, cast=first)
                        wdma(17 if sparse else 1, cast=first)

                    if sparse:
                        steps = ([(pe, tt) for pe in (16, 17) for tt in range(NT_ST)]
                                 + [(e, sl) for e in range(NEXP) for sl in range(CAPT)])
                    else:
                        steps = [(e, tt) for e in range(NPE) for tt in range(NT_ST)]
                    nsteps = len(steps)
                    tok_ld = {}
                    tok_trx = {}
                    tok_w1mm = {}
                    tok_hstt = {}
                    tok_hev = {}
                    tok_w2mm = {}

                    def src_rows(i):
                        e, sl = steps[i]
                        return xg_dram[sp][e * CAP + sl * 128: e * CAP + (sl + 1) * 128, :]

                    def emit_ldx(i):
                        e, sl = steps[i]
                        p = i % 3
                        deps = b_xg[p].write_deps() + [t for t in scat_last[sp] if t]
                        rows_ap = src_rows(i)
                        tk = sch.dma("y", (lambda y, p=p, rows_ap=rows_ap: y.dma_start(
                            xg_sb[p][:], rows_ap)), deps=deps, ch=f"xg{p}")
                        b_xg[p].wrote(tk)
                        b_xgd[sp].read(tk)
                        tok_ld[i] = tk

                    def emit_trx(i):
                        e, sl = steps[i]
                        p3 = i % 3
                        p = i % 2
                        tk = sch.op("t", (lambda t, p=p, p3=p3: [
                            t.transpose(hT_ps[p][:, c * 128:(c + 1) * 128],
                                        xg_sb[p3][:, c * 128:(c + 1) * 128],
                                        ident[:]) for c in range(KC)][-1]),
                            deps=[tok_ld[i]] + b_hTps[p].write_deps() + tok_const)
                        b_xg[p3].read(tk)
                        b_hTps[p].wrote(tk)
                        tev = sch.op("v", (lambda v, p=p: v.tensor_copy(
                            xgT[p][:], hT_ps[p][:])),
                            deps=[tk] + b_xgT[p].write_deps())
                        b_hTps[p].read(tev)
                        b_xgT[p].wrote(tev)
                        tok_trx[i] = tev

                    def emit_w1mm(i):
                        e, sl = steps[i]
                        p = i % 2
                        sl_w = e % 2
                        if sparse and e < NEXP:
                            deps = (b_yps[p].write_deps() + b_w1[sl_w].read_deps() + [tok_trx[i]])
                            tk = sch.op("t", (lambda t, p=p, sl_w=sl_w: [
                                t.matmul(y_ps[p][:, n * 512:(n + 1) * 512],
                                         xgT[p][:, k * 128:(k + 1) * 128],
                                         w1s[sl_w][:, k * 1024 + n * 512:k * 1024 + (n + 1) * 512],
                                         start=(k == 0), stop=(k == KC - 1))
                                for k in range(KC) for n in range(2)][-1]), deps=deps)
                            b_xgT[p].read(tk)
                        else:
                            tt = sl
                            deps = (b_yps[p].write_deps() + b_w1[sl_w].read_deps()
                                    + b_xfm[sp][tt].read_deps())
                            tk = sch.op("t", (lambda t, p=p, sl_w=sl_w, sp=sp, tt=tt: [
                                t.matmul(y_ps[p][:, n * 512:(n + 1) * 512],
                                         xhat_fm[sp][:, k * ST + tt * 128:k * ST + (tt + 1) * 128],
                                         w1s[sl_w][:, k * 1024 + n * 512:k * 1024 + (n + 1) * 512],
                                         start=(k == 0), stop=(k == KC - 1))
                                for k in range(KC) for n in range(2)][-1]), deps=deps)
                            b_xfm[sp][tt].read(tk)
                        tok_w1mm[i] = tk
                        b_yps[p].wrote(tk)
                        b_w1[sl_w].read(tk)

                    def emit_gating(i):
                        e, sl = steps[i]
                        p = i % 2
                        tsil = sch.op("s", (lambda s, p=p: s.activation(
                            sg[p][:], y_ps[p][:, 512:1024], AF.Silu)),
                            deps=[tok_w1mm[i]] + b_sg[p].write_deps())
                        b_yps[p].read(tsil)
                        b_sg[p].wrote(tsil)
                        use_cw = (not sparse) and e < NEXP
                        th = sch.op("v", (lambda v, p=p, sp=sp, e=e, sl=sl, use_cw=use_cw: v.scalar_tensor_tensor(
                            out=h_bf[p][:], in0=y_ps[p][:, 0:512],
                            scalar=(cw_all[sp][:, sl * 16 + e:sl * 16 + e + 1] if use_cw else 1.0),
                            in1=sg[p][:], op0=ALU.mult, op1=ALU.mult)),
                            deps=[tsil, tok_w1mm[i]] + b_h[p].write_deps()
                                 + (b_cw[sp][sl].read_deps() + [b_cw[sp][sl].w] if use_cw else []))
                        b_yps[p].read(th)
                        b_sg[p].read(th)
                        b_h[p].wrote(th)
                        if use_cw:
                            b_cw[sp][sl].read(th)
                        tok_hstt[i] = th

                    def emit_tr(i):
                        e, sl = steps[i]
                        p = i % 2
                        p3 = i % 3
                        tk = sch.op("t", (lambda t, p=p: [
                            t.transpose(hT_ps[p][:, j * 128:(j + 1) * 128],
                                        h_bf[p][:, j * 128:(j + 1) * 128], ident[:])
                            for j in range(4)][-1]),
                            deps=[tok_hstt[i]] + b_hTps[p].write_deps())
                        b_h[p].read(tk)
                        b_hTps[p].wrote(tk)
                        tev = sch.op("v", (lambda v, p=p, p3=p3: v.tensor_copy(
                            hT_sb[p3][:], hT_ps[p][:, 0:512])),
                            deps=[tk] + b_hTsb[p3].write_deps())
                        b_hTps[p].read(tev)
                        b_hTsb[p3].wrote(tev)
                        tok_hev[i] = tev

                    def emit_w2mm(i):
                        e, sl = steps[i]
                        p3 = i % 3
                        sl_w = e % 2
                        deps = ([tok_hev[i]] + b_w2o.write_deps() + b_w2[sl_w].read_deps())
                        tk = sch.op("t", (lambda t, p3=p3, sl_w=sl_w: [
                            t.matmul(w2o[:, n * 512:(n + 1) * 512],
                                     hT_sb[p3][:, k * 128:(k + 1) * 128],
                                     w2s[sl_w][:, k * 1024 + n * 512:k * 1024 + (n + 1) * 512],
                                     start=(k == 0), stop=(k == 3))
                            for k in range(4) for n in range(2)][-1]), deps=deps)
                        tok_w2mm[i] = tk
                        b_hTsb[p3].read(tk)
                        b_w2[sl_w].read(tk)
                        b_w2o.wrote(tk)

                    def emit_sink(i):
                        e, sl = steps[i]
                        p = i % 2
                        if sparse and e < NEXP:
                            tk = sch.op("v", (lambda v, p=p: v.tensor_copy(
                                zg_sb[p][:], w2o[:])),
                                deps=[tok_w2mm[i]] + b_zg[p].write_deps())
                            b_w2o.read(tk)
                            b_zg[p].wrote(tk)
                            td = sch.dma("s", (lambda y, p=p, e=e, sl=sl, sp=sp: y.dma_start(
                                zg_dram[sp][e * CAP + sl * 128:e * CAP + (sl + 1) * 128, :],
                                zg_sb[p][:])),
                                deps=[tk] + (b_zgd[sp].write_deps() if (e == 0 and sl == 0) else []),
                                ch=f"zg{sp}{p}")
                            b_zg[p].read(td)
                            b_zgd[sp].wrote(td)
                            zg_last[sp][p] = td
                        else:
                            tt = sl
                            zslv = zacc[:, tt * D:(tt + 1) * D]
                            if e == (16 if sparse else 0):
                                tk = sch.op("v", (lambda v, zslv=zslv: v.tensor_copy(zslv, w2o[:])),
                                            deps=[tok_w2mm[i]] + b_z[tt].write_deps())
                            else:
                                tk = sch.op("v", (lambda v, zslv=zslv: v.tensor_add(zslv, zslv, w2o[:])),
                                            deps=[tok_w2mm[i]] + b_z[tt].read_deps()
                                                 + b_z[tt].write_deps())
                            b_z[tt].wrote(tk)
                            b_w2o.read(tk)

                    fr_next = [0]
                    bk_next = [0]
                    cur_fr = [4]
                    cur_bk = [2]
                    gb0 = [False]
                    for i in range(nsteps):
                        e, sl = steps[i]
                        if sparse and st_i == 0:
                            if cur_bk[0] < NT_ST:
                                passA_back(0, cur_bk[0])
                                cur_bk[0] += 1
                            elif not gb0[0]:
                                gate_burst(0)
                                gb0[0] = True
                            if cur_fr[0] < NT_ST:
                                passA_front(0, cur_fr[0])
                                cur_fr[0] += 1
                        if st_i + 1 < NST and i >= (16 if sparse else 2):
                            if i % 3 == 0 and fr_next[0] < NT_ST:
                                passA_front(st_i + 1, fr_next[0])
                                fr_next[0] += 1
                            if i % 3 == 2 and bk_next[0] < NT_ST and bk_next[0] <= fr_next[0] - 2:
                                passA_back(st_i + 1, bk_next[0])
                                bk_next[0] += 1
                        if sparse and i + 2 < nsteps and steps[i + 2][0] < NEXP:
                            emit_ldx(i + 2)
                        emit_w1mm(i)
                        if sparse and i + 1 < nsteps and steps[i + 1][0] < NEXP:
                            emit_trx(i + 1)
                        if i >= 1:
                            emit_tr(i - 1)
                        if i >= 2:
                            emit_w2mm(i - 2)
                            emit_sink(i - 2)
                        emit_gating(i)
                        if sl == 1:
                            if sparse:
                                if e == 17:
                                    wdma(0, cast=(first and st_i == 0))
                                elif e < 15:
                                    wdma(e + 1, cast=(first and st_i == 0))
                            elif e >= 1 and e + 1 < NPE:
                                wdma(e + 1, cast=(first and st_i == 0))
                    emit_tr(nsteps - 1)
                    for i in (nsteps - 2, nsteps - 1):
                        emit_w2mm(i)
                        emit_sink(i)
                    if st_i + 1 < NST:
                        while fr_next[0] < NT_ST:
                            passA_front(st_i + 1, fr_next[0])
                            fr_next[0] += 1
                        while bk_next[0] < NT_ST:
                            passA_back(st_i + 1, bk_next[0])
                            bk_next[0] += 1
                        gate_burst(st_i + 1)
                        wdma(16 if sparse else 0, cast=False)
                        wdma(17 if sparse else 1, cast=False)

                    for tt in range(NT_ST):
                        par = tt % 2
                        r0 = row0 + tt * 128
                        gt = r0 // 128
                        dld = sch.dma("y", (lambda y, r0=r0, par=par: y.dma_start(
                            xctile[par][:], src_dram[r0:r0 + 128, :])),
                            deps=b_xc[par].write_deps() + ([src_toks[gt]] if src_toks else []),
                            ch=f"xc{par}")
                        b_xc[par].wrote(dld)
                        if not sparse:
                            tfin = sch.op("v", (lambda v, par=par, tt=tt: v.scalar_tensor_tensor(
                                out=xnew[par][:], in0=zacc[:, tt * D:(tt + 1) * D], scalar=0.5,
                                in1=xctile[par][:], op0=ALU.mult, op1=ALU.add)),
                                deps=[dld] + b_z[tt].read_deps() + b_z[tt].write_deps()
                                     + b_xn[par].write_deps())
                            b_xc[par].read(tfin)
                            b_z[tt].read(tfin)
                            b_xn[par].wrote(tfin)
                            dst = sch.dma("y", (lambda y, r0=r0, par=par: y.dma_start(
                                dst_dram[r0:r0 + 128, :], xnew[par][:])), deps=[tfin],
                                ch=f"so{par}")
                            b_xn[par].read(dst)
                            store_toks[dst_key][gt] = dst
                            continue
                        tg0 = sch.dma("g", (lambda g, sp=sp, tt=tt, par=par: g.indirect_dma_start(
                            out=zk_sb[par][:], out_offset=None,
                            in_=zg_dram[sp], in_offset=IndirectOffsetOnAxis(
                                dk_all[sp][:, 2 * tt:2 * tt + 1], 0))),
                            deps=b_zk[par].write_deps()
                                 + [t for t in zg_last[sp] if t]
                                 + b_dk[sp][tt].read_deps(),
                            ch=f"zk{par}")
                        b_zgd[sp].read(tg0)
                        b_dk[sp][tt].read(tg0)
                        b_zk[par].wrote(tg0)
                        ta0 = sch.op("v", (lambda v, sp=sp, tt=tt, par=par: v.scalar_tensor_tensor(
                            out=xnew[par][:], in0=zk_sb[par][:],
                            scalar=cwk_all[sp][:, 2 * tt:2 * tt + 1],
                            in1=zacc[:, tt * D:(tt + 1) * D], op0=ALU.mult, op1=ALU.add)),
                            deps=[tg0] + b_z[tt].read_deps() + b_z[tt].write_deps()
                                 + b_cwk[sp][tt].read_deps() + [b_cwk[sp][tt].w]
                                 + b_xn[par].write_deps())
                        b_z[tt].read(ta0)
                        b_cwk[sp][tt].read(ta0)
                        b_zk[par].read(ta0)
                        b_xn[par].wrote(ta0)
                        tg1 = sch.dma("g", (lambda g, sp=sp, tt=tt, par=par: g.indirect_dma_start(
                            out=zk_sb[par][:], out_offset=None,
                            in_=zg_dram[sp], in_offset=IndirectOffsetOnAxis(
                                dk_all[sp][:, 2 * tt + 1:2 * tt + 2], 0))),
                            deps=[ta0] + [t for t in zg_last[sp] if t], ch=f"zk{par}")
                        b_zgd[sp].read(tg1)
                        b_zk[par].wrote(tg1)
                        ta1 = sch.op("v", (lambda v, sp=sp, tt=tt, par=par: v.scalar_tensor_tensor(
                            out=xnew[par][:], in0=zk_sb[par][:],
                            scalar=cwk_all[sp][:, 2 * tt + 1:2 * tt + 2],
                            in1=xnew[par][:], op0=ALU.mult, op1=ALU.add)),
                            deps=[tg1, ta0])
                        b_zk[par].read(ta1)
                        b_xn[par].wrote(ta1)
                        tfin = sch.op("v", (lambda v, par=par: v.scalar_tensor_tensor(
                            out=xnew[par][:], in0=xnew[par][:], scalar=0.5,
                            in1=xctile[par][:], op0=ALU.mult, op1=ALU.add)),
                            deps=[ta1, dld])
                        b_xc[par].read(tfin)
                        b_xn[par].wrote(tfin)
                        dst = sch.dma("y", (lambda y, r0=r0, par=par: y.dma_start(
                            dst_dram[r0:r0 + 128, :], xnew[par][:])), deps=[tfin],
                            ch=f"so{par}")
                        b_xn[par].read(dst)
                        store_toks[dst_key][gt] = dst
                sch.barrier()

        def emit_attn(src_dram, dst_dram, dst_key, src_toks):
            with ExitStack() as att:
                AE = att.enter_context
                wq_sb = AE(nc.sbuf_tensor(nm("wq_sb"), [128, KC * 1024], BF16))
                wk_sb = AE(nc.sbuf_tensor(nm("wk_sb"), [128, KC * 256], BF16))
                wv_sb = AE(nc.sbuf_tensor(nm("wv_sb"), [128, KC * 256], BF16))
                wo_sb = AE(nc.sbuf_tensor(nm("wo_sb"), [128, KC * 1024], BF16))
                bqs_b = AE(nc.sbuf_tensor(nm("bqs_b"), [128, D], F32))
                bk_b = AE(nc.sbuf_tensor(nm("bk_b"), [128, 256], F32))
                bv_b = AE(nc.sbuf_tensor(nm("bv_b"), [128, 256], F32))
                bo_b = AE(nc.sbuf_tensor(nm("bo_b"), [128, D], F32))
                g_b = AE(nc.sbuf_tensor(nm("g_b"), [128, D], F32))
                b_b = AE(nc.sbuf_tensor(nm("b_b"), [128, D], F32))
                q_fm = AE(nc.sbuf_tensor(nm("q_fm"), [128, 16 * S], BF16))
                k_fm = AE(nc.sbuf_tensor(nm("k_fm"), [128, 4 * S], BF16))
                v_sb = AE(nc.sbuf_tensor(nm("v_sb"), [128, QT * 260], BF16))
                xtile = [AE(nc.sbuf_tensor(nm(f"axt{i}"), [128, D], F32)) for i in range(2)]
                xhat_tm = [AE(nc.sbuf_tensor(nm(f"axh{i}"), [128, D], BF16)) for i in range(2)]
                xaT = [AE(nc.sbuf_tensor(nm(f"xaT{i}"), [128, KC * 128], BF16)) for i in range(2)]
                q_tm = [AE(nc.sbuf_tensor(nm(f"q_tm{i}"), [128, D], BF16)) for i in range(2)]
                k_tm = [AE(nc.sbuf_tensor(nm(f"k_tm{i}"), [128, 256], BF16)) for i in range(2)]
                qn2_all = AE(nc.sbuf_tensor(nm("qn2_all"), [128, QT], F32))
                kn2_all = AE(nc.sbuf_tensor(nm("kn2_all"), [128, QT * 4], F32))
                qn8_all = AE(nc.sbuf_tensor(nm("qn8_all"), [128, QT], F32))
                qn8row = AE(nc.sbuf_tensor(nm("qn8row"), [1, S], BF16))
                knsc = AE(nc.sbuf_tensor(nm("knsc"), [128, 4], F32))
                pT_sb = [AE(nc.sbuf_tensor(nm(f"pT_sb{i}"), [128, S], BF16)) for i in range(2)]
                rinv = AE(nc.sbuf_tensor(nm("rinv"), [128, 16], F32))
                ao_sb = AE(nc.sbuf_tensor(nm("ao_sb"), [128, D], F32))
                t1_sb = AE(nc.sbuf_tensor(nm("t1_sb"), [128, D], F32))
                qsq = t1_sb
                t_bf = AE(nc.sbuf_tensor(nm("t_bf"), [128, D], BF16))
                tT_sb = AE(nc.sbuf_tensor(nm("tT_sb"), [128, D], BF16))
                xres = xtile
                xn1 = ao_sb
                xnew_s = AE(nc.sbuf_tensor(nm("axn"), [128, D], F32))
                xnew = [xnew_s, xnew_s]

                with nc.psum_tensor(nm("bc_ps"), [128, 512], F32) as bc_ps, \
                     nc.sbuf_tensor(nm("brow"), [1, D], F32) as brow:
                    b_rows2, b_bcps = Buf(), Buf()

                    def bcast2(dst, row_dram, n, scale=None):
                        dld = sch.dma("g", (lambda g, n=n, row_dram=row_dram: g.dma_start(
                            brow[:1, 0:n], row_dram)), deps=b_rows2.write_deps(), ch="drw")
                        b_rows2.wrote(dld)
                        if scale is not None:
                            dsc = sch.op("s", (lambda s, n=n, scale=scale: s.activation(
                                brow[:1, 0:n], brow[:1, 0:n], AF.Identity,
                                bias=f32(0.0), scale=f32(scale))), deps=[dld])
                            b_rows2.wrote(dsc)
                            src_tok = dsc
                        else:
                            src_tok = dld
                        last = None
                        for j in range((n + 511) // 512):
                            w = min(512, n - j * 512)
                            tmm = sch.op("t", (lambda t, j=j, w=w: t.matmul(
                                bc_ps[:, 0:w], ones_row[:1, :],
                                brow[:1, j * 512:j * 512 + w], start=True, stop=True)),
                                deps=[src_tok, tok_const[0]] + b_bcps.write_deps())
                            b_rows2.read(tmm)
                            b_bcps.wrote(tmm)
                            tev = sch.op("v", (lambda v, j=j, w=w, dst=dst: v.tensor_copy(
                                dst[:, j * 512:j * 512 + w], bc_ps[:, 0:w])), deps=[tmm])
                            b_bcps.read(tev)
                            last = tev
                        return last

                    bcast2(bqs_b, bq, D, scale=0.125)
                    bcast2(bk_b, bk, 256)
                    bcast2(bv_b, bv, 256)
                    bcast2(bo_b, bo, D)
                    bcast2(g_b, ln_g, D)
                    tok_bc = bcast2(b_b, ln_b, D)

                b_wq, b_wk, b_wv, b_wo = Buf(), Buf(), Buf(), Buf()
                sch.dma("g", lambda g: g.dma_start(
                    wq_sb[:].rearrange("p (c n) -> p c n", c=KC),
                    wq.rearrange("(c p) n -> p c n", p=128)), ch="wat")
                sch.dma("g", lambda g: g.dma_start(
                    wk_sb[:].rearrange("p (c n) -> p c n", c=KC),
                    wk.rearrange("(c p) n -> p c n", p=128)), ch="wat")
                sch.dma("g", lambda g: g.dma_start(
                    wv_sb[:].rearrange("p (c n) -> p c n", c=KC),
                    wv.rearrange("(c p) n -> p c n", p=128)), ch="wat")
                dw = sch.dma("g", lambda g: g.dma_start(
                    wo_sb[:].rearrange("p (c n) -> p c n", c=KC),
                    wo.rearrange("(c p) n -> p c n", p=128)), ch="wat")
                b_wq.wrote(dw); b_wk.wrote(dw); b_wv.wrote(dw); b_wo.wrote(dw)

                b_vsb = Buf()
                b_qfm = Buf()
                b_kfm = Buf()
                gctr = [0]

                for seq in range(SEQ_PER_CORE):
                    row0 = seq * S
                    b_xt = [Buf(), Buf()]
                    b_xhtm = [Buf(), Buf()]
                    b_xaT = [Buf(), Buf()]
                    b_qtm = [Buf(), Buf()]
                    b_ktm = [Buf(), Buf()]
                    b_qsq = Buf()
                    tok_ke = [None] * QT
                    tok_ve = [None] * QT
                    tok_qe = [None] * QT
                    with nc.psum_tensor(nm("xT_ps0"), [128, 1024], BF16) as xT_ps0, \
                         nc.psum_tensor(nm("xT_ps1"), [128, 1024], BF16) as xT_ps1, \
                         nc.psum_tensor(nm("q_ps"), [128, 1024], F32) as q_ps, \
                         nc.psum_tensor(nm("qT_ps"), [128, 1024], BF16) as qT_ps, \
                         nc.psum_tensor(nm("kv_ps"), [128, 512], F32) as kv_ps, \
                         nc.psum_tensor(nm("kT_ps"), [128, 256], BF16) as kT_ps, \
                         nc.psum_tensor(nm("sp_ps"), [128, 128], F32) as sp_ps:
                        xT_ps = [xT_ps0, xT_ps1]
                        b_xTps = [Buf(), Buf()]
                        b_qps, b_qTps, b_kvps, b_kTps, b_spps = \
                            Buf(), Buf(), Buf(), Buf(), Buf()
                        tko = sch.op("g", (lambda g: g.memset(
                            v_sb[:].rearrange("p (t h c) -> p t h c", h=4, c=65)[:, :, :, 64:65],
                            1.0)), deps=b_vsb.write_deps())
                        b_vsb.wrote(tko)
                        tkon = sch.op("g", (lambda g: g.memset(k_fm[64:65, :], 1.0)),
                                      deps=b_kfm.write_deps())
                        b_kfm.wrote(tkon)
                        tok_ln = [None] * QT

                        def p1_ln(tt):
                            par = tt % 2
                            r0 = row0 + tt * 128
                            gt = r0 // 128
                            dld = sch.dma("y", (lambda y, r0=r0, par=par: y.dma_start(
                                xtile[par][:], src_dram[r0:r0 + 128, :])),
                                deps=b_xt[par].write_deps() + ([src_toks[gt]] if src_toks else []),
                                ch=f"xa{par}")
                            b_xt[par].wrote(dld)
                            tln, _, lntoks = emit_ln(par, xtile[par][:], xhat_tm[par][:],
                                                     [dld] + b_xhtm[par].write_deps())
                            for tk in lntoks[:3]:
                                b_xt[par].read(tk)
                            b_xt[par].read(tln)
                            b_xhtm[par].wrote(tln)
                            tok_ln[tt] = tln

                        tok_ev = [None] * QT

                        def p1_stage1(tt):
                            par = tt % 2
                            tln = tok_ln[tt]
                            ttr = sch.op("t", (lambda t, par=par: [
                                t.transpose(xT_ps[par][:, c * 128:(c + 1) * 128],
                                            xhat_tm[par][:, c * 128:(c + 1) * 128],
                                            ident[:]) for c in range(KC)][-1]),
                                deps=[tln] + b_xTps[par].write_deps() + tok_const)
                            b_xhtm[par].read(ttr)
                            b_xTps[par].wrote(ttr)
                            tev = sch.op("v", (lambda v, par=par: v.tensor_copy(
                                xaT[par][:], xT_ps[par][:])),
                                deps=[ttr] + b_xaT[par].write_deps())
                            b_xTps[par].read(tev)
                            b_xaT[par].wrote(tev)
                            tok_ev[tt] = tev

                        p1_ln(0)
                        p1_ln(1)
                        p1_stage1(0)
                        for tt in range(QT):
                            par = tt % 2
                            r0 = row0 + tt * 128
                            gt = r0 // 128
                            if tt + 2 < QT:
                                p1_ln(tt + 2)
                            if tt + 1 < QT:
                                p1_stage1(tt + 1)
                            tev = tok_ev[tt]
                            tqm = sch.op("t", (lambda t, par=par: [
                                t.matmul(q_ps[:, n * 512:(n + 1) * 512],
                                         xaT[par][:, k * 128:(k + 1) * 128],
                                         wq_sb[:, k * 1024 + n * 512:k * 1024 + (n + 1) * 512],
                                         start=(k == 0), stop=(k == KC - 1))
                                for k in range(KC) for n in range(2)][-1]),
                                deps=[tev] + b_qps.write_deps() + b_wq.read_deps())
                            b_xaT[par].read(tqm)
                            b_wq.read(tqm)
                            b_qps.wrote(tqm)
                            tqs = sch.op("v", (lambda v, par=par: v.scalar_tensor_tensor(
                                out=q_tm[par][:], in0=q_ps[:], scalar=0.125,
                                in1=bqs_b[:], op0=ALU.mult, op1=ALU.add)),
                                deps=[tqm, tok_bc] + b_qtm[par].write_deps())
                            b_qps.read(tqs)
                            b_qtm[par].wrote(tqs)
                            tqn = sch.op("v", (lambda v, par=par, tt=tt: v.scalar_tensor_tensor(
                                out=qsq[:], in0=q_tm[par][:], scalar=1.0, in1=q_tm[par][:],
                                op0=ALU.bypass, op1=ALU.mult,
                                accum_out=qn2_all[:, tt:tt + 1])),
                                deps=[tqs] + b_qsq.write_deps())
                            b_qtm[par].read(tqn)
                            b_qsq.wrote(tqn)
                            tqt = sch.op("t", (lambda t, par=par: [
                                t.transpose(qT_ps[64 * (hq % 2):64 * (hq % 2) + 64,
                                                  (hq // 2) * 128:(hq // 2) * 128 + 128],
                                            q_tm[par][:, hq * 64:(hq + 1) * 64],
                                            ident[:]) for hq in range(16)][-1]),
                                deps=[tqs] + b_qTps.write_deps())
                            b_qtm[par].read(tqt)
                            b_qTps.wrote(tqt)
                            tqe = sch.op("v", (lambda v, tt=tt: [
                                v.tensor_copy(
                                    q_fm[0:64, :].rearrange("p (h n) -> p h n", n=S)[:, 0::2, tt * 128:(tt + 1) * 128],
                                    qT_ps[0:64, :].rearrange("p (h n) -> p h n", n=128)),
                                v.tensor_copy(
                                    q_fm[0:64, :].rearrange("p (h n) -> p h n", n=S)[:, 1::2, tt * 128:(tt + 1) * 128],
                                    qT_ps[64:128, :].rearrange("p (h n) -> p h n", n=128))][-1]),
                                deps=[tqt] + b_qfm.write_deps())
                            b_qTps.read(tqe)
                            tok_qe[tt] = tqe
                            tkm = sch.op("t", (lambda t, par=par: [
                                t.matmul(kv_ps[:, 0:256],
                                         xaT[par][:, k * 128:(k + 1) * 128],
                                         wk_sb[:, k * 256:(k + 1) * 256],
                                         start=(k == 0), stop=(k == KC - 1))
                                for k in range(KC)][-1]),
                                deps=[tev] + b_kvps.write_deps() + b_wk.read_deps())
                            b_xaT[par].read(tkm)
                            b_wk.read(tkm)
                            b_kvps.wrote(tkm)
                            tks = sch.op("v", (lambda v, par=par: v.scalar_tensor_tensor(
                                out=k_tm[par][:], in0=kv_ps[:, 0:256], scalar=1.0,
                                in1=bk_b[:], op0=ALU.mult, op1=ALU.add)),
                                deps=[tkm] + b_ktm[par].write_deps())
                            b_kvps.read(tks)
                            b_ktm[par].wrote(tks)
                            tkn = None
                            for h in range(4):
                                tkn = sch.op("v", (lambda v, par=par, tt=tt, h=h: v.scalar_tensor_tensor(
                                    out=qsq[:, h * 64:(h + 1) * 64],
                                    in0=k_tm[par][:, h * 64:(h + 1) * 64], scalar=1.0,
                                    in1=k_tm[par][:, h * 64:(h + 1) * 64],
                                    op0=ALU.bypass, op1=ALU.mult,
                                    accum_out=kn2_all[:, tt * 4 + h:tt * 4 + h + 1])),
                                    deps=[tks] + ([] if h else b_qsq.write_deps()))
                                b_ktm[par].read(tkn)
                            b_qsq.wrote(tkn)
                            tkt = sch.op("t", (lambda t, par=par: [
                                t.transpose(kT_ps[64 * (h % 2):64 * (h % 2) + 64,
                                                  (h // 2) * 128:(h // 2) * 128 + 128],
                                            k_tm[par][:, h * 64:(h + 1) * 64],
                                            ident[:]) for h in range(4)][-1]),
                                deps=[tks] + b_kTps.write_deps())
                            b_ktm[par].read(tkt)
                            b_kTps.wrote(tkt)
                            tke = sch.op("v", (lambda v, tt=tt: [
                                v.tensor_copy(
                                    k_fm[0:64, :].rearrange("p (h n) -> p h n", n=S)[:, 0::2, tt * 128:(tt + 1) * 128],
                                    kT_ps[0:64, 0:256].rearrange("p (h n) -> p h n", n=128)),
                                v.tensor_copy(
                                    k_fm[0:64, :].rearrange("p (h n) -> p h n", n=S)[:, 1::2, tt * 128:(tt + 1) * 128],
                                    kT_ps[64:128, 0:256].rearrange("p (h n) -> p h n", n=128))][-1]),
                                deps=[tkt] + b_kfm.write_deps())
                            b_kTps.read(tke)
                            tok_ke[tt] = tke
                            tvm = sch.op("t", (lambda t, par=par: [
                                t.matmul(kv_ps[:, 256:512],
                                         xaT[par][:, k * 128:(k + 1) * 128],
                                         wv_sb[:, k * 256:(k + 1) * 256],
                                         start=(k == 0), stop=(k == KC - 1))
                                for k in range(KC)][-1]),
                                deps=[tev, tks] + b_wv.read_deps())
                            b_xaT[par].read(tvm)
                            b_wv.read(tvm)
                            tvs = sch.op("v", (lambda v, tt=tt: v.scalar_tensor_tensor(
                                out=v_sb[:, tt * 260:(tt + 1) * 260].rearrange(
                                    "p (h c) -> p h c", c=65)[:, :, 0:64],
                                in0=kv_ps[:, 256:512].rearrange("p (h c) -> p h c", c=64),
                                scalar=1.0,
                                in1=bv_b[:].rearrange("p (h c) -> p h c", c=64),
                                op0=ALU.mult, op1=ALU.add)),
                                deps=[tvm, tko])
                            b_kvps.read(tvs)
                            b_kvps.wrote(tvm)
                            b_vsb.read(tvs)
                            tok_ve[tt] = tvs

                        tl1 = sch.op("s", (lambda s: s.activation(
                            qn8_all[:], qn2_all[:], AF.Ln)),
                            deps=[b_qsq.w, tok_qe[QT - 1]] + [t for t in [b_qsq.w] if t])
                        tl2 = sch.op("s", (lambda s: s.activation(
                            qn8_all[:], qn8_all[:], AF.Exp, bias=f32(-2.0794415416798357),
                            scale=f32(0.5))), deps=[tl1])
                        tok_qn8r = None
                        for tt in range(QT):
                            ttr2 = sch.op("t", (lambda t, tt=tt: t.transpose(
                                sp_ps[0:1, 0:128], qn8_all[:, tt:tt + 1], ident_f[:])),
                                deps=[tl2] + b_spps.write_deps() + tok_const)
                            b_spps.wrote(ttr2)
                            tev2 = sch.op("v", (lambda v, tt=tt: v.tensor_copy(
                                qn8row[0:1, tt * 128:(tt + 1) * 128], sp_ps[0:1, 0:128])),
                                deps=[ttr2])
                            b_spps.read(tev2)
                            tok_qn8r = tev2
                        tk1 = sch.op("v", (lambda v: v.reduce_max(
                            knsc[:, 0:1], kn2_all[:], axis=AX.X)), deps=[b_qsq.w])
                        tk2 = sch.op("t", (lambda t: t.transpose(
                            sp_ps[0:1, 0:128], knsc[:, 0:1], ident_f[:])),
                            deps=[tk1] + b_spps.write_deps())
                        b_spps.wrote(tk2)
                        tk4 = sch.op("v", (lambda v: v.reduce_max(
                            knsc[0:1, 1:2], sp_ps[0:1, 0:128], axis=AX.X)),
                            deps=[tk2])
                        b_spps.read(tk4)
                        tk5 = sch.op("s", (lambda s: s.activation(
                            knsc[0:1, 2:3], knsc[0:1, 1:2], AF.Ln)), deps=[tk4])
                        tk6 = sch.op("s", (lambda s: s.activation(
                            knsc[0:1, 2:3], knsc[0:1, 2:3], AF.Exp, bias=f32(0.0),
                            scale=f32(0.5))), deps=[tk5])
                        tk7 = sch.op("v", (lambda v: v.tensor_scalar_mul(
                            knsc[0:1, 3:4], knsc[0:1, 2:3], -1.0)), deps=[tk6])
                        tok_aug = None
                        for hq in range(16):
                            tok_aug = sch.op("v", (lambda v, hq=hq: v.tensor_scalar(
                                out=q_fm[64:65, hq * S:(hq + 1) * S], in0=qn8row[0:1, :],
                                scalar1=knsc[0:1, 3:4], scalar2=None, op0=ALU.mult)),
                                deps=[tk7, tok_qn8r] + (b_qfm.write_deps() if hq == 0 else []))
                        b_qfm.wrote(tok_aug)
                        for tt in range(QT):
                            b_qfm.read(tok_qe[tt])
                        sch.barrier()

                    with nc.psum_tensor(nm("sT_ps"), [128, 2048], F32) as sT_ps, \
                         nc.psum_tensor(nm("ao_ps"), [128, 3 * 512], F32) as ao_ps, \
                         nc.psum_tensor(nm("tT_ps"), [128, 1024], BF16) as tT_ps:
                        NSTB = 4
                        sT = [sT_ps[:, i * 512:(i + 1) * 512] for i in range(NSTB)]
                        wo_ps = ao_ps[:, 0:1024]
                        b_sT = [Buf() for _ in range(NSTB)]
                        b_aops, b_tTps = Buf(), Buf()
                        b_wops = b_aops
                        b_pT = [Buf(), Buf()]
                        b_rinv, b_ao = Buf(), Buf()
                        b_t1, b_t2, b_tbf, b_tT = Buf(), Buf(), Buf(), Buf()
                        b_xres = [Buf(), Buf()]
                        b_xn1 = Buf()
                        _bxn = Buf()
                        b_xn = [_bxn, _bxn]

                        def _ao_data_view():
                            return ao_ps[:].rearrange(
                                "p (b c) -> p b c", c=512).rearrange(
                                "p b (h c) -> p b h c", c=65)[:, :, :, 0:64].rearrange(
                                "p b h c -> p (b h) c")[:, 0:16, :] if False else None

                        p3b_pend = [None]

                        def emit_p3b():
                            if p3b_pend[0] is None:
                                return
                            tq, r0, gt, tb, drd = p3b_pend[0]
                            p3b_pend[0] = None
                            ttt = sch.op("t", (lambda t: [
                                t.transpose(tT_ps[:, c * 128:(c + 1) * 128],
                                            t_bf[:, c * 128:(c + 1) * 128], ident[:])
                                for c in range(KC)][-1]),
                                deps=[tb] + b_tTps.write_deps())
                            b_tbf.read(ttt)
                            b_tTps.wrote(ttt)
                            tte = sch.op("v", (lambda v: v.tensor_copy(
                                tT_sb[:], tT_ps[:])),
                                deps=[ttt] + b_tT.write_deps())
                            b_tTps.read(tte)
                            b_tT.wrote(tte)
                            twm = sch.op("t", (lambda t: [
                                t.matmul(wo_ps[:, n * 512:(n + 1) * 512],
                                         tT_sb[:, k * 128:(k + 1) * 128],
                                         wo_sb[:, k * 1024 + n * 512:k * 1024 + (n + 1) * 512],
                                         start=(k == 0), stop=(k == KC - 1))
                                for k in range(KC) for n in range(2)][-1]),
                                deps=[tte] + b_wops.write_deps() + b_wo.read_deps())
                            b_tT.read(twm)
                            b_wo.read(twm)
                            b_wops.wrote(twm)
                            tr1 = sch.op("v", (lambda v: v.scalar_tensor_tensor(
                                out=xn1[:], in0=wo_ps[:], scalar=1.0, in1=bo_b[:],
                                op0=ALU.mult, op1=ALU.add)),
                                deps=[twm] + b_xn1.write_deps() + b_ao.write_deps())
                            b_wops.read(tr1)
                            b_xn1.wrote(tr1)
                            b_ao.wrote(tr1)
                            tr2 = sch.op("v", (lambda v, tq=tq: v.tensor_add(
                                xnew[tq % 2][:], xn1[:], xres[tq % 2][:])),
                                deps=[tr1, drd] + b_xn[tq % 2].write_deps())
                            b_xn1.read(tr2)
                            b_xres[tq % 2].read(tr2)
                            b_xn[tq % 2].wrote(tr2)
                            dstk = sch.dma("y", (lambda y, r0=r0, tq=tq: y.dma_start(
                                dst_dram[r0:r0 + 128, :], xnew[tq % 2][:])), deps=[tr2],
                                ch=f"so{tq % 2}")
                            b_xn[tq % 2].read(dstk)
                            store_toks[dst_key][gt] = dstk

                        for tq in range(QT):
                            r0 = row0 + tq * 128
                            gt = r0 // 128
                            nk = tq + 1
                            drd = sch.dma("y", (lambda y, r0=r0, tq=tq: y.dma_start(
                                xres[tq % 2][:], src_dram[r0:r0 + 128, :])),
                                deps=b_xres[tq % 2].write_deps() + ([src_toks[gt]] if src_toks else []),
                                ch=f"xc{tq % 2}")
                            b_xres[tq % 2].wrote(drd)

                            ng = (nk + 3) // 4
                            items = [(hq, gi) for hq in range(16) for gi in range(ng)]
                            tok_exp = {}
                            tok_pv_last = None

                            def emit_pv(j):
                                nonlocal tok_pv_last
                                hq, gi = items[j]
                                h_ = hq % 4
                                hpar = hq % 2
                                c0 = 4 * gi
                                c1 = min(c0 + 4, nk)
                                off = (hq // 7) * 512 + (hq % 7) * 65
                                tk = sch.op("t", (lambda t, hq=hq, h_=h_, hpar=hpar,
                                                  c0=c0, c1=c1, nk=nk, off=off: [
                                    t.matmul(ao_ps[:, off:off + 65],
                                             pT_sb[hpar][:, c * 128:(c + 1) * 128],
                                             v_sb[:, c * 260 + h_ * 65:c * 260 + (h_ + 1) * 65],
                                             start=(c == 0), stop=(c == nk - 1))
                                    for c in range(c0, c1)][-1]),
                                    deps=[tok_exp[j], tok_ve[min(c1 - 1, nk - 1)]]
                                         + (b_aops.write_deps() if (hq == 0 and gi == 0) else []))
                                if c1 == nk:
                                    b_pT[hpar].read(tk)
                                tok_pv_last = tk

                            for j, (hq, gi) in enumerate(items):
                                if j == min(2, len(items) - 1):
                                    emit_p3b()
                                if j >= 2:
                                    emit_pv(j - 2)
                                h_ = hq % 4
                                hpar = hq % 2
                                c0 = 4 * gi
                                c1 = min(c0 + 4, nk)
                                gpar = gctr[0] % NSTB
                                gctr[0] += 1
                                tmm = sch.op("t", (lambda t, hq=hq, h_=h_, tq=tq, c0=c0, c1=c1, gpar=gpar: [
                                    t.matmul(sT[gpar][:, (c - c0) * 128:(c - c0 + 1) * 128],
                                             k_fm[0:65, h_ * S + c * 128:h_ * S + (c + 1) * 128],
                                             q_fm[0:65, hq * S + tq * 128:hq * S + (tq + 1) * 128],
                                             start=True, stop=True)
                                    for c in range(c0, c1)][-1]),
                                    deps=[tok_ke[c1 - 1], tok_qe[tq], tok_aug, tkon]
                                         + b_sT[gpar].write_deps())
                                b_sT[gpar].wrote(tmm)
                                tlast = tmm
                                if c0 <= tq < c1:
                                    tmk = sch.op("v", (lambda v, tq=tq, c0=c0, gpar=gpar: v.tensor_add(
                                        sT[gpar][:, (tq - c0) * 128:(tq - c0 + 1) * 128],
                                        sT[gpar][:, (tq - c0) * 128:(tq - c0 + 1) * 128],
                                        maskdT[:])),
                                        deps=[tmm] + tok_const)
                                    b_sT[gpar].wrote(tmk)
                                    tlast = tmk
                                texp = sch.op("s", (lambda s, hpar=hpar, c0=c0, c1=c1, gpar=gpar: s.activation(
                                    pT_sb[hpar][:, c0 * 128:c1 * 128],
                                    sT[gpar][:, 0:(c1 - c0) * 128], AF.Exp)),
                                    deps=[tlast] + (b_pT[hpar].write_deps() if gi == 0 else []))
                                b_sT[gpar].read(texp)
                                if gi == 0:
                                    b_pT[hpar].wrote(texp)
                                tok_exp[j] = texp
                            if len(items) >= 2:
                                emit_pv(len(items) - 2)
                            emit_pv(len(items) - 1)
                            b_aops.wrote(tok_pv_last)

                            trc = sch.op("v", (lambda v: [
                                v.reciprocal(
                                    rinv[:, g * 7:g * 7 + (7 if g < 2 else 2)],
                                    ao_ps[:, g * 512:g * 512 + (7 if g < 2 else 2) * 65]
                                    .rearrange("p (h c) -> p h c", c=65)[:, :, 64:65])
                                for g in range(3)][-1]),
                                deps=[tok_pv_last] + b_rinv.write_deps())
                            b_aops.read(trc)
                            b_rinv.wrote(trc)
                            def _aoevac(v):
                                last = None
                                for g, ng in ((0, 7), (1, 7), (2, 2)):
                                    last = v.tensor_tensor(
                                        ao_sb[:, g * 448:g * 448 + ng * 64].rearrange(
                                            "p (h c) -> p h c", c=64),
                                        ao_ps[:, g * 512:g * 512 + ng * 65].rearrange(
                                            "p (h c) -> p h c", c=65)[:, :, 0:64],
                                        rinv[:, g * 7:g * 7 + ng].rearrange(
                                            "p (h o) -> p h o", o=1).to_broadcast(
                                            [128, ng, 64]),
                                        op=ALU.mult)
                                return last
                            tae = sch.op("v", _aoevac,
                                deps=[tok_pv_last, trc] + b_ao.write_deps())
                            b_aops.read(tae)
                            b_rinv.read(tae)
                            b_ao.wrote(tae)
                            tln, _, lntoks = emit_ln(tq % 2, ao_sb[:], t1_sb[:],
                                                     [tae] + b_t1.write_deps())
                            for tk in lntoks[:3]:
                                b_ao.read(tk)
                            b_ao.read(tln)
                            b_t1.wrote(tln)
                            tg = sch.op("v", lambda v: v.tensor_mul(t1_sb[:], t1_sb[:], g_b[:]),
                                        deps=[tln])
                            b_t1.wrote(tg)
                            tb = sch.op("v", lambda v: v.tensor_add(t_bf[:], t1_sb[:], b_b[:]),
                                        deps=[tg] + b_tbf.write_deps())
                            b_t1.read(tb)
                            b_tbf.wrote(tb)
                            p3b_pend[0] = (tq, r0, gt, tb, drd)
                        emit_p3b()
                        sch.barrier()
                sch.barrier()

        if stage in ("f1", "f1dbg"):
            emit_ffn(x_in, out, "out", None, True, True)
        elif stage == "attn":
            emit_ffn(x_in, x1, "x1", None, True, True)
            emit_attn(x1, out, "out", store_toks["x1"])
        else:
            emit_ffn(x_in, x1, "x1", None, True, True)
            emit_attn(x1, x2, "x2", store_toks["x1"])
            emit_ffn(x2, out, "out", store_toks["x2"], False, False)

        with ExitStack() as semctx:
            semnames = sorted(set(list(sch.counts.keys()) + list(Sched.ENGS)))
            sems = {n: semctx.enter_context(nc.semaphore(f"sem_{n}")) for n in semnames}
            block = semctx.enter_context(nc.Block())

            def runner(name, eng):
                hw = {}
                for (e, waits, fn, amt, sem) in sch.recs:
                    if e != name:
                        continue
                    for (sname, val) in waits:
                        if hw.get(sname, 0) >= val:
                            continue
                        eng.wait_ge(sems[sname], val)
                        hw[sname] = val
                    if fn is not None:
                        inst = fn(eng)
                        if amt:
                            inst.then_inc(sems[sem], amt)
                if name == "y":
                    for sname, cnt in sorted(sch.counts.items()):
                        if sname not in Sched.ENGS and cnt:
                            eng.wait_ge(sems[sname], cnt)

            block.tensor(lambda t: runner("t", t))
            block.vector(lambda v: runner("v", v))
            block.scalar(lambda s: runner("s", s))
            block.gpsimd(lambda g: runner("g", g))
            block.sync(lambda y: runner("y", y))

    return nc


def kernel(L0_attn_x, L0_attn_Wq, L0_attn_bq, L0_attn_Wk, L0_attn_bk,
           L0_attn_Wv, L0_attn_bv, L0_attn_ln_g, L0_attn_ln_b, L0_attn_Wo,
           L0_attn_bo, L0_ffn_Wg, L0_ffn_W1, L0_ffn_W2, L0_ffn_Ws1, L0_ffn_Ws2,
           _stage="full"):
    x = np.ascontiguousarray(np.asarray(L0_attn_x, dtype=np.float32))
    shards = x.reshape(NCORES, TOK, D)
    common = {
        "Wq": np.ascontiguousarray(L0_attn_Wq, dtype=np.float32),
        "bq": np.ascontiguousarray(L0_attn_bq, dtype=np.float32).reshape(1, D),
        "Wk": np.ascontiguousarray(L0_attn_Wk, dtype=np.float32),
        "bk": np.ascontiguousarray(L0_attn_bk, dtype=np.float32).reshape(1, 256),
        "Wv": np.ascontiguousarray(L0_attn_Wv, dtype=np.float32),
        "bv": np.ascontiguousarray(L0_attn_bv, dtype=np.float32).reshape(1, 256),
        "ln_g": np.ascontiguousarray(L0_attn_ln_g, dtype=np.float32).reshape(1, D),
        "ln_b": np.ascontiguousarray(L0_attn_ln_b, dtype=np.float32).reshape(1, D),
        "Wo": np.ascontiguousarray(L0_attn_Wo, dtype=np.float32),
        "bo": np.ascontiguousarray(L0_attn_bo, dtype=np.float32).reshape(1, D),
        "Wg": np.ascontiguousarray(L0_ffn_Wg, dtype=np.float32),
        "W1": np.ascontiguousarray(L0_ffn_W1, dtype=np.float32),
        "W2": np.ascontiguousarray(L0_ffn_W2, dtype=np.float32),
        "Ws1": np.ascontiguousarray(L0_ffn_Ws1, dtype=np.float32),
        "Ws2": np.ascontiguousarray(L0_ffn_Ws2, dtype=np.float32),
    }
    nc = build(_stage)
    in_maps = [dict(common, x=np.ascontiguousarray(shards[c])) for c in range(NCORES)]
    res = run_bass_kernel_spmd(nc, in_maps, core_ids=list(range(NCORES)))
    outs = [res.results[c]["out"] for c in range(NCORES)]
    return np.concatenate(outs, axis=0).reshape(B, S, D)


# version 18
# speedup vs baseline: 1.1457x; 1.0096x over previous
from contextlib import ExitStack

import numpy as np
import concourse.bass as bass
import concourse.mybir as mybir
from concourse.bass_utils import run_bass_kernel_spmd
from concourse.bass_core_rust import IndirectOffsetOnAxis

F32 = mybir.dt.float32
BF16 = mybir.dt.bfloat16
I32 = mybir.dt.int32
RSQRT_MAGIC = 0x5F3759DF
AF = mybir.ActivationFunctionType
ALU = mybir.AluOpType
AX = mybir.AxisListType
f32 = np.float32

B, S, D = 16, 2048, 1024
NCORES = 8
SEQ_PER_CORE = B // NCORES
TOK = SEQ_PER_CORE * S
NT = TOK // 128
NEXP = 16
NPE = 18
ST = 2048
NT_ST = ST // 128
NST = TOK // ST
QT = S // 128
EPS = f32(1e-5)
KC = 8
CAP = 384
CAPT = CAP // 128
ZROW = 16 * CAP


class Sched:
    ENGS = ("t", "v", "s", "g", "y")

    def __init__(self):
        self.recs = []
        self.counts = {}
        self.barrier_level = {}

    def _chk(self, deps):
        waits = {}
        for d in deps:
            if d is None:
                continue
            sem, val = d
            assert val <= self.counts.get(sem, 0), (sem, val, self.counts.get(sem, 0))
            if waits.get(sem, 0) < val:
                waits[sem] = val
        for sem in [s for s, v in waits.items()
                    if s not in Sched.ENGS and v <= self.barrier_level.get(s, -1)]:
            del waits[sem]
        for sem, val in waits.items():
            if sem not in Sched.ENGS:
                assert val == self.counts.get(sem, 0), (
                    "unsafe intermediate DMA wait", sem, val, self.counts.get(sem, 0))
        return list(waits.items())

    def op(self, eng, fn, deps=()):
        waits = self._chk(deps)
        self.counts[eng] = self.counts.get(eng, 0) + 1
        tok = (eng, self.counts[eng])
        self.recs.append((eng, waits, fn, 1, eng))
        return tok

    def dma(self, q, fn, deps=(), ch="dy"):
        pass
        assert q in ("y", "g", "s")
        waits = self._chk(deps)
        sem = ch
        self.counts[sem] = self.counts.get(sem, 0) + 16
        tok = (sem, self.counts[sem])
        self.recs.append((q, waits, fn, 16, sem))
        return tok

    def wait_only(self, eng, deps):
        waits = self._chk(deps)
        if waits:
            self.recs.append((eng, waits, None, 0, None))

    def barrier(self):
        toks = [(s, v) for s, v in self.counts.items()]
        for e in self.ENGS:
            self.wait_only(e, toks)
        for s, v in self.counts.items():
            if s not in Sched.ENGS:
                self.barrier_level[s] = v


class Buf:

    def __init__(self):
        self.w = None
        self.r = {}

    def write_deps(self):
        return list(self.r.values()) or ([self.w] if self.w else [])

    def read_deps(self):
        return [self.w] if self.w else []

    def wrote(self, tok):
        self.w = tok
        self.r = {}

    def read(self, tok):
        self.r[tok[0]] = tok


def build(stage="full"):
    nc = bass.Bass("TRN2", target_bir_lowering=False)
    _uid = [0]

    def nm(base):
        _uid[0] += 1
        return f"{base}_{_uid[0]}"

    x_in = nc.dram_tensor("x", [TOK, D], F32, kind="ExternalInput").ap()
    wq = nc.dram_tensor("Wq", [D, D], F32, kind="ExternalInput").ap()
    bq = nc.dram_tensor("bq", [1, D], F32, kind="ExternalInput").ap()
    wk = nc.dram_tensor("Wk", [D, 256], F32, kind="ExternalInput").ap()
    bk = nc.dram_tensor("bk", [1, 256], F32, kind="ExternalInput").ap()
    wv = nc.dram_tensor("Wv", [D, 256], F32, kind="ExternalInput").ap()
    bv = nc.dram_tensor("bv", [1, 256], F32, kind="ExternalInput").ap()
    ln_g = nc.dram_tensor("ln_g", [1, D], F32, kind="ExternalInput").ap()
    ln_b = nc.dram_tensor("ln_b", [1, D], F32, kind="ExternalInput").ap()
    wo = nc.dram_tensor("Wo", [D, D], F32, kind="ExternalInput").ap()
    bo = nc.dram_tensor("bo", [1, D], F32, kind="ExternalInput").ap()
    wg = nc.dram_tensor("Wg", [NEXP, D], F32, kind="ExternalInput").ap()
    w1 = nc.dram_tensor("W1", [NEXP, D, 2 * 512], F32, kind="ExternalInput").ap()
    w2 = nc.dram_tensor("W2", [NEXP, 512, D], F32, kind="ExternalInput").ap()
    ws1 = nc.dram_tensor("Ws1", [D, 2048], F32, kind="ExternalInput").ap()
    ws2 = nc.dram_tensor("Ws2", [1024, D], F32, kind="ExternalInput").ap()
    out = nc.dram_tensor("out", [TOK, D], F32, kind="ExternalOutput").ap()
    x1 = nc.dram_tensor("x1", [TOK, D], F32, kind="Internal").ap()
    x2 = nc.dram_tensor("x2", [TOK, D], F32, kind="Internal").ap()
    xhat_dram = nc.dram_tensor("xhat_d", [TOK, D], BF16, kind="Internal").ap()
    xg_dram = [nc.dram_tensor(f"xg_d{i}", [ZROW + 1, D], BF16, kind="Internal").ap()
               for i in range(2)]
    zg_dram = [nc.dram_tensor(f"zg_d{i}", [ZROW + 1, D], F32, kind="Internal").ap()
               for i in range(2)]
    wbf1 = nc.dram_tensor("wbf1", [NPE, 128, KC * 1024], BF16, kind="Internal").ap()
    wbf2 = nc.dram_tensor("wbf2", [NPE, 128, 4 * 1024], BF16, kind="Internal").ap()

    sch = Sched()

    with ExitStack() as top:
        E = top.enter_context
        ones_row = E(nc.sbuf_tensor(nm("ones_row"), [1, 128], F32))
        ident = E(nc.sbuf_tensor(nm("ident"), [128, 128], BF16))
        maskdT = E(nc.sbuf_tensor(nm("maskdT"), [128, 128], F32))
        ident_f = E(nc.sbuf_tensor(nm("ident_f"), [128, 128], F32))
        wgT = E(nc.sbuf_tensor(nm("wgT"), [128, 128], BF16))
        ltri_bf = E(nc.sbuf_tensor(nm("ltri_bf"), [128, 128], BF16))
        ones_col_bf = E(nc.sbuf_tensor(nm("ones_col_bf"), [128, 1], BF16))
        ecb = E(nc.sbuf_tensor(nm("ecb"), [128, 16], F32))
        iotae = E(nc.sbuf_tensor(nm("iotae"), [128, 16], F32))


        with nc.psum_tensor(nm("set_ps"), [128, 512], F32) as set_ps, \
             nc.psum_tensor(nm("set_ps2"), [128, 128], BF16) as set_ps2, \
             nc.sbuf_tensor(nm("rows_sb"), [1, D], F32) as rows_sb, \
             nc.sbuf_tensor(nm("zero_row"), [1, D], F32) as zero_row:
            t0 = sch.op("g", lambda g: g.memset(ones_row[:], 1.0))
            t1 = sch.op("g", lambda g: g.memset(ident[:], 0.0))
            t2 = sch.op("g", lambda g: g.affine_select(
                out=ident[:], in_=ident[:], compare_op=ALU.not_equal, fill=1.0,
                base=0, pattern=[[-1, 128]], channel_multiplier=1), deps=[t1])
            t5 = sch.op("g", lambda g: g.memset(maskdT[:], 0.0))
            t6 = sch.op("g", lambda g: g.affine_select(
                out=maskdT[:], in_=maskdT[:], compare_op=ALU.is_ge, fill=-1e10,
                base=0, pattern=[[1, 128]], channel_multiplier=-1), deps=[t5])
            t7 = sch.op("g", lambda g: g.memset(ident_f[:], 0.0))
            t8 = sch.op("g", lambda g: g.affine_select(
                out=ident_f[:], in_=ident_f[:], compare_op=ALU.not_equal, fill=1.0,
                base=0, pattern=[[-1, 128]], channel_multiplier=1), deps=[t7])
            t9 = sch.op("g", lambda g: g.memset(ltri_bf[:], 0.0))
            t10 = sch.op("g", lambda g: g.affine_select(
                out=ltri_bf[:], in_=ltri_bf[:], compare_op=ALU.is_ge, fill=1.0,
                base=0, pattern=[[-1, 128]], channel_multiplier=1), deps=[t9])
            t11 = sch.op("g", lambda g: g.memset(ones_col_bf[:], 1.0))
            t12 = sch.op("g", lambda g: g.iota(
                ecb[:], pattern=[[CAP, 16]], base=0, channel_multiplier=0,
                allow_small_or_imprecise_dtypes=True))
            t13 = sch.op("g", lambda g: g.iota(
                iotae[:], pattern=[[1, 16]], base=0, channel_multiplier=0,
                allow_small_or_imprecise_dtypes=True))
            tok_const = [t0, t2, t6, t8, t10, t11, t12, t13]
            tz = sch.op("g", lambda g: g.memset(zero_row[:1, :], 0.0))
            for i in range(2):
                sch.dma("y", (lambda y, i=i: y.dma_start(
                    zg_dram[i][ZROW:ZROW + 1, :], zero_row[:1, :])),
                    deps=[tz], ch=f"zsink{i}")

            b_rows = Buf()
            b_setps = Buf()

            def _unused_bcast(dst, row_dram, n, scale=None):
                dld = sch.dma("y", (lambda y, n=n, row_dram=row_dram: y.dma_start(
                    rows_sb[:1, 0:n], row_dram)), deps=b_rows.write_deps(), ch="drw")
                b_rows.wrote(dld)
                if scale is not None:
                    dsc = sch.op("s", (lambda s, n=n, scale=scale: s.activation(
                        rows_sb[:1, 0:n], rows_sb[:1, 0:n], AF.Identity,
                        bias=f32(0.0), scale=f32(scale))), deps=[dld])
                    b_rows.wrote(dsc)
                    src_tok = dsc
                else:
                    src_tok = dld
                nj = (n + 511) // 512
                last = None
                for j in range(nj):
                    w = min(512, n - j * 512)
                    tmm = sch.op("t", (lambda t, j=j, w=w: t.matmul(
                        set_ps[:, 0:w], ones_row[:1, :],
                        rows_sb[:1, j * 512:j * 512 + w], start=True, stop=True)),
                        deps=[src_tok, t0] + b_setps.write_deps())
                    b_rows.read(tmm)
                    b_setps.wrote(tmm)
                    tev = sch.op("v", (lambda v, j=j, w=w, dst=dst: v.tensor_copy(
                        dst[:, j * 512:j * 512 + w], set_ps[:, 0:w])), deps=[tmm])
                    b_setps.read(tev)
                    last = tev
                return last

            tok_bcast = []

            with nc.sbuf_tensor(nm("wg_tmp"), [16, D], BF16) as wg_tmp:
                dwg = sch.dma("g", lambda g: g.dma_start(wg_tmp[:], wg), ch="dwg")
                tprev = None
                for c in range(KC):
                    ttr = sch.op("t", (lambda t, c=c: t.transpose(
                        set_ps2[:, 0:16], wg_tmp[:16, c * 128:(c + 1) * 128],
                        ident[0:16, 0:16])), deps=[dwg, t2] + ([tprev] if tprev else []))
                    tev = sch.op("v", (lambda v, c=c: v.tensor_copy(
                        wgT[:, c * 16:(c + 1) * 16], set_ps2[:, 0:16])), deps=[ttr])
                    tprev = tev
                tok_wgT = tprev
            sch.barrier()

        lnst = [E(nc.sbuf_tensor(nm(f"lnst{i}"), [128, 12], F32)) for i in range(2)]
        lnag = [E(nc.sbuf_tensor(nm(f"lnag{i}"), [128, 2], F32)) for i in range(2)]
        lnr = [E(nc.sbuf_tensor(nm(f"lnr{i}"), [128, 4], F32)) for i in range(2)]

        def emit_rsqrt(x_ap, xe_ap, y_ap, u_ap, eps, deps, iters=2):
            tk = sch.op("v", (lambda v: v.tensor_scalar_add(xe_ap, x_ap, eps)), deps=deps)
            tk = sch.op("v", (lambda v: v.tensor_scalar(
                out=y_ap.bitcast(I32), in0=xe_ap.bitcast(I32), scalar1=1,
                scalar2=None, op0=ALU.arith_shift_right)), deps=[tk])
            tk = sch.op("v", (lambda v: v.tensor_scalar(
                out=y_ap.bitcast(I32), in0=y_ap.bitcast(I32), scalar1=-1,
                scalar2=RSQRT_MAGIC, op0=ALU.mult, op1=ALU.add)), deps=[tk])
            for _ in range(iters):
                tk = sch.op("v", (lambda v: v.scalar_tensor_tensor(
                    out=u_ap, in0=y_ap, scalar=1.0, in1=y_ap,
                    op0=ALU.bypass, op1=ALU.mult)), deps=[tk])
                tk = sch.op("v", (lambda v: v.scalar_tensor_tensor(
                    out=u_ap, in0=u_ap, scalar=-0.5, in1=xe_ap,
                    op0=ALU.mult, op1=ALU.mult)), deps=[tk])
                tk = sch.op("v", (lambda v: v.scalar_tensor_tensor(
                    out=y_ap, in0=u_ap, scalar=1.5, in1=y_ap,
                    op0=ALU.add, op1=ALU.mult)), deps=[tk])
            return tk

        def emit_ln(par, src_ap, dst_ap, deps, scratch_r=None):
            stt = lnst[par]
            agg = lnag[par]
            rr = lnr[par] if scratch_r is None else scratch_r
            t_a = sch.op("v", lambda v: v.bn_stats(stt[:, 0:6], src_ap[:, 0:512]), deps=deps)
            t_b = sch.op("v", lambda v: v.bn_stats(stt[:, 6:12], src_ap[:, 512:1024]), deps=deps)
            t_c = sch.op("v", lambda v: v.bn_aggr(agg[:], stt[:]), deps=[t_a, t_b])
            t_e = emit_rsqrt(agg[:, 1:2], rr[:, 2:3], rr[:, 0:1], rr[:, 3:4],
                             float(EPS), [t_c], iters=2)
            t_f = sch.op("v", lambda v: v.scalar_tensor_tensor(
                out=rr[:, 1:2], in0=agg[:, 0:1], scalar=-1.0, in1=rr[:, 0:1],
                op0=ALU.mult, op1=ALU.mult), deps=[t_e, t_c])
            t_g = sch.op("s", lambda s: s.activation(
                dst_ap, src_ap, AF.Identity, bias=rr[:, 1:2], scale=rr[:, 0:1]),
                deps=[t_f, t_e] + list(deps))
            return t_g, (rr[:, 0:1], rr[:, 1:2]), [t_a, t_b, t_c, t_e, t_f, t_g]

        store_toks = {"x1": [None] * NT, "x2": [None] * NT, "out": [None] * NT}
        b_wbf1 = [Buf() for _ in range(NPE)]
        b_wbf2 = [Buf() for _ in range(NPE)]

        def emit_ffn(src_dram, dst_dram, dst_key, src_toks, first, sparse):
            with ExitStack() as ffn:
                FE = ffn.enter_context
                zacc = FE(nc.sbuf_tensor(nm("zacc"), [128, NT_ST * D], F32))
                cw_all = [FE(nc.sbuf_tensor(nm(f"cw_all{i}"), [128, NT_ST * 16], F32)) for i in range(2)]
                glog_all = cw_all
                w1s = [FE(nc.sbuf_tensor(nm(f"w1s{i}"), [128, KC * 1024], BF16)) for i in range(2)]
                w2s = [FE(nc.sbuf_tensor(nm(f"w2s{i}"), [128, 4 * 1024], BF16)) for i in range(2)]
                xtile = [FE(nc.sbuf_tensor(nm(f"xtile{i}"), [128, D], F32)) for i in range(2)]
                xhat_tm = [FE(nc.sbuf_tensor(nm(f"xhat_tm{i}"), [128, D], BF16)) for i in range(2)]
                if sparse:
                    xg_sb = [FE(nc.sbuf_tensor(nm(f"xg_sb{i}"), [128, D], BF16)) for i in range(3)]
                    xgT = [FE(nc.sbuf_tensor(nm(f"xgT{i}"), [128, KC * 128], BF16)) for i in range(2)]
                    _xfm = FE(nc.sbuf_tensor(nm("xhat_fm"), [128, KC * ST], BF16))
                    xhat_fm = [_xfm, _xfm]
                else:
                    xhat_fm = [FE(nc.sbuf_tensor(nm(f"xhat_fm{i}"), [128, KC * ST], BF16)) for i in range(2)]
                sg = [FE(nc.sbuf_tensor(nm(f"sg{i}"), [128, 512], BF16)) for i in range(2)]
                h_bf = [FE(nc.sbuf_tensor(nm(f"h_bf{i}"), [128, 512], BF16)) for i in range(2)]
                hT_sb = [FE(nc.sbuf_tensor(nm(f"hT_sb{i}"), [128, 512], BF16)) for i in range(3)]
                zg_sb = [FE(nc.sbuf_tensor(nm(f"zg_sb{i}"), [128, D], F32)) for i in range(2)] if sparse else None
                if sparse:
                    base_row = FE(nc.sbuf_tensor(nm("base_row"), [1, 16], F32))
                    maskf = [FE(nc.sbuf_tensor(nm(f"maskf{i}"), [128, 16], F32)) for i in range(2)]
                    mask_bf = [FE(nc.sbuf_tensor(nm(f"mask_bf{i}"), [128, 16], BF16)) for i in range(2)]
                    mask_all = [FE(nc.sbuf_tensor(nm(f"mask_all{i}"), [128, NT_ST * 16], BF16)) for i in range(2)]
                    rsc = FE(nc.sbuf_tensor(nm("rsc"), [128, 6 * 16], F32))
                    dk_all = [FE(nc.sbuf_tensor(nm(f"dk_all{i}"), [128, NT_ST * 2], I32)) for i in range(2)]
                    dkf = FE(nc.sbuf_tensor(nm("dkf"), [128, 2], F32))
                    cwk_all = [FE(nc.sbuf_tensor(nm(f"cwk_all{i}"), [128, NT_ST * 2], F32)) for i in range(2)]
                else:
                    base_row = maskf = mask_bf = mask_all = rsc = dk_all = dkf = cwk_all = None
                _gp = FE(nc.sbuf_tensor(nm("gp16"), [128, 16], F32))
                gp16 = [_gp, _gp]
                _s16 = FE(nc.sbuf_tensor(nm("s16"), [128, 16], F32))
                s16 = [_s16, _s16]
                _m8 = FE(nc.sbuf_tensor(nm("m8"), [128, 8], F32))
                m8 = [_m8, _m8]
                _gsc = FE(nc.sbuf_tensor(nm("gsc"), [128, 4], F32))
                gsc = [_gsc, _gsc]
                xctile_s = FE(nc.sbuf_tensor(nm("xctile"), [128, D], F32))
                xctile = [xctile_s, xctile_s]
                zk_sb = [FE(nc.sbuf_tensor(nm(f"zk_sb{i}"), [128, D], F32)) for i in range(2)] if sparse else None
                xnew_s = FE(nc.sbuf_tensor(nm("xnew"), [128, D], F32))
                xnew = [xnew_s, xnew_s]

                y_ps = [FE(nc.psum_tensor(nm(f"y_ps{i}"), [128, 1024], F32)) for i in range(2)]
                hT_ps = [FE(nc.psum_tensor(nm(f"hT_ps{i}"), [128, 1024], BF16)) for i in range(2)]
                w2o = FE(nc.psum_tensor(nm("w2o"), [128, 1024], F32))
                glog_ps = w2o[:, 0:16]
                pfx_ps = w2o[:, 16:32]
                tot_ps = w2o[0:1, 32:48]
                baseb_ps = w2o[:, 48:64]

                b_xt = [Buf(), Buf()]
                b_xhtm = [Buf(), Buf()]
                b_xg = [Buf(), Buf(), Buf()]
                if sparse:
                    _bxf = [Buf() for _ in range(NT_ST)]
                    b_xfm = [_bxf, _bxf]
                else:
                    b_xfm = [[Buf() for _ in range(NT_ST)] for _ in range(2)]
                b_xgT = [Buf(), Buf()]
                b_cw = [[Buf() for _ in range(NT_ST)] for _ in range(2)]
                b_mall = [[Buf() for _ in range(NT_ST)] for _ in range(2)]
                b_z = [Buf() for _ in range(NT_ST)]
                b_w1 = [Buf(), Buf()]
                b_w2 = [Buf(), Buf()]
                b_yps = [Buf(), Buf()]
                b_hTps = [Buf(), Buf()]
                b_w2o = Buf()
                b_sg = [Buf(), Buf()]
                b_h = [Buf(), Buf()]
                b_hTsb = [Buf(), Buf(), Buf()]
                _bg = Buf()
                b_g16 = [_bg, _bg]
                b_zg = [Buf(), Buf()]
                b_base = Buf()
                b_baser = Buf()
                b_rsc = Buf()
                b_dk = [[Buf() for _ in range(NT_ST)] for _ in range(2)]
                b_cwk = [[Buf() for _ in range(NT_ST)] for _ in range(2)]
                b_xgd = [Buf(), Buf()]
                b_zgd = [Buf(), Buf()]
                scat_last = [[None, None], [None, None]]
                zg_last = [[None, None], [None, None]]
                _bxc = Buf()
                b_xc = [_bxc, _bxc]
                b_zk = [Buf(), Buf()]
                _bxnf = Buf()
                b_xn = [_bxnf, _bxnf]

                def wdma(e, cast):
                    sl = e % 2
                    d1 = b_w1[sl].write_deps()
                    d2 = b_w2[sl].write_deps()
                    if cast:
                        if e < NEXP:
                            tk1 = sch.dma("g", (lambda g, e=e, sl=sl: g.dma_start(
                                w1s[sl][:].rearrange("p (c n) -> p c n", c=KC),
                                w1[e].rearrange("(c p) n -> p c n", p=128))), deps=d1, ch=f"w1{sl}")
                            tk2 = sch.dma("g", (lambda g, e=e, sl=sl: g.dma_start(
                                w2s[sl][:].rearrange("p (c n) -> p c n", c=4),
                                w2[e].rearrange("(c p) n -> p c n", p=128))), deps=d2, ch=f"w2{sl}")
                        else:
                            j = e - NEXP
                            tk1a = sch.dma("g", (lambda g, j=j, sl=sl: g.dma_start(
                                w1s[sl][:].rearrange("p (c n) -> p c n", c=KC)[:, :, 0:512],
                                ws1.rearrange("(c p) n -> p c n", p=128)[:, :, j * 512:(j + 1) * 512])),
                                deps=d1, ch=f"w1{sl}")
                            tk1 = sch.dma("g", (lambda g, j=j, sl=sl: g.dma_start(
                                w1s[sl][:].rearrange("p (c n) -> p c n", c=KC)[:, :, 512:1024],
                                ws1.rearrange("(c p) n -> p c n", p=128)[:, :, 1024 + j * 512:1024 + (j + 1) * 512])),
                                deps=[tk1a], ch=f"w1{sl}")
                            tk2 = sch.dma("g", (lambda g, j=j, sl=sl: g.dma_start(
                                w2s[sl][:].rearrange("p (c n) -> p c n", c=4),
                                ws2[j * 512:(j + 1) * 512, :].rearrange("(c p) n -> p c n", p=128))),
                                deps=d2, ch=f"w2{sl}")
                        ts1 = sch.dma("s", (lambda y, e=e, sl=sl: y.dma_start(
                            wbf1[e], w1s[sl][:])), deps=[tk1], ch=f"wb1{e}")
                        ts2 = sch.dma("s", (lambda y, e=e, sl=sl: y.dma_start(
                            wbf2[e], w2s[sl][:])), deps=[tk2], ch=f"wb2{e}")
                        b_w1[sl].wrote(tk1)
                        b_w1[sl].read(ts1)
                        b_w2[sl].wrote(tk2)
                        b_w2[sl].read(ts2)
                        b_wbf1[e].wrote(ts1)
                        b_wbf2[e].wrote(ts2)
                    else:
                        tk1 = sch.dma("y", (lambda y, e=e, sl=sl: y.dma_start(
                            w1s[sl][:], wbf1[e])), deps=d1 + b_wbf1[e].read_deps(),
                            ch=f"wB1{sl}")
                        tk2 = sch.dma("y", (lambda y, e=e, sl=sl: y.dma_start(
                            w2s[sl][:], wbf2[e])), deps=d2 + b_wbf2[e].read_deps(),
                            ch=f"wB2{sl}")
                        b_wbf1[e].read(tk1)
                        b_wbf2[e].read(tk2)
                        b_w1[sl].wrote(tk1)
                        b_w2[sl].wrote(tk2)

                tln_pend = {}

                def passA_front(st_i, tt):
                    par = tt % 2
                    r0 = st_i * ST + tt * 128
                    dld = sch.dma("y", (lambda y, r0=r0, par=par: y.dma_start(
                        xtile[par][:], src_dram[r0:r0 + 128, :])),
                        deps=b_xt[par].write_deps() + ([src_toks[(r0 // 128)]] if src_toks else []),
                        ch=f"xa{par}")
                    b_xt[par].wrote(dld)
                    tln, _, lntoks = emit_ln(par, xtile[par][:], xhat_tm[par][:],
                                             [dld] + b_xhtm[par].write_deps())
                    for tk in lntoks[:3]:
                        b_xt[par].read(tk)
                    b_xt[par].read(tln)
                    b_xhtm[par].wrote(tln)
                    tln_pend[(st_i, tt)] = tln

                def passA_back(st_i, tt):
                    sp = st_i % 2
                    par = tt % 2
                    tln = tln_pend.pop((st_i, tt))
                    r0 = st_i * ST + tt * 128
                    if sparse and tt == 0 and st_i > 0:
                        trst = sch.op("v", (lambda v: v.memset(base_row[:1, :], 0.0)),
                                      deps=b_baser.read_deps() + b_baser.write_deps())
                        b_baser.wrote(trst)
                    ttr = sch.op("t", (lambda t, par=par: [
                        t.transpose(hT_ps[par][:, c * 128:(c + 1) * 128],
                                    xhat_tm[par][:, c * 128:(c + 1) * 128],
                                    ident[:]) for c in range(KC)][-1]),
                        deps=[tln] + b_hTps[par].write_deps() + tok_const)
                    b_xhtm[par].read(ttr)
                    b_hTps[par].wrote(ttr)
                    tev = sch.op("v", (lambda v, sp=sp, par=par, tt=tt: v.tensor_copy(
                        xhat_fm[sp][:].rearrange("p (c n) -> p c n", c=KC)[:, :, tt * 128:(tt + 1) * 128],
                        hT_ps[par][:].rearrange("p (c n) -> p c n", c=KC))),
                        deps=[ttr] + b_xfm[sp][tt].write_deps())
                    b_hTps[par].read(tev)
                    b_xfm[sp][tt].wrote(tev)
                    gate_lhsT = (lambda c, sp=sp, tt=tt: xhat_fm[sp][:, c * ST + tt * 128:c * ST + (tt + 1) * 128])
                    tgm_deps = [tev]
                    tgm = sch.op("t", (lambda t, gate_lhsT=gate_lhsT: [
                        t.matmul(glog_ps,
                                 gate_lhsT(c),
                                 wgT[:, c * 16:(c + 1) * 16],
                                 start=(c == 0), stop=(c == KC - 1)) for c in range(KC)][-1]),
                        deps=tgm_deps + [tok_wgT] + b_w2o.write_deps())
                    b_xfm[sp][tt].read(tgm)
                    b_w2o.wrote(tgm)
                    tge = sch.op("v", (lambda v, sp=sp, tt=tt: v.tensor_copy(
                        glog_all[sp][:, tt * 16:(tt + 1) * 16], glog_ps)),
                        deps=[tgm] + b_cw[sp][tt].write_deps())
                    b_w2o.read(tge)
                    b_cw[sp][tt].wrote(tge)
                    if not sparse:
                        return
                    tm8 = sch.op("v", (lambda v, par=par, sp=sp, tt=tt: v.max(
                        out=m8[par][:], in_=glog_all[sp][:, tt * 16:(tt + 1) * 16])),
                        deps=[tge] + b_g16[par].write_deps())
                    b_cw[sp][tt].read(tm8)
                    tmk = sch.op("v", (lambda v, par=par, sp=sp, tt=tt: v.tensor_scalar(
                        out=maskf[par][:], in0=glog_all[sp][:, tt * 16:(tt + 1) * 16],
                        scalar1=m8[par][:, 1:2], scalar2=None, op0=ALU.is_ge)),
                        deps=[tm8])
                    b_cw[sp][tt].read(tmk)
                    b_g16[par].wrote(tmk)
                    tmb = sch.op("v", (lambda v, par=par: v.tensor_copy(
                        mask_bf[par][:], maskf[par][:])), deps=[tmk])
                    tma = sch.op("v", (lambda v, par=par, sp=sp, tt=tt: v.tensor_copy(
                        mask_all[sp][:, tt * 16:(tt + 1) * 16], maskf[par][:])),
                        deps=[tmk] + b_mall[sp][tt].write_deps())
                    b_mall[sp][tt].wrote(tma)
                    tpf = sch.op("t", (lambda t, par=par: t.matmul(
                        pfx_ps, ltri_bf[:], mask_bf[par][:], start=True, stop=True)),
                        deps=[tmb, tge] + b_w2o.write_deps() + tok_const)
                    b_w2o.wrote(tpf)
                    ttot = sch.op("t", (lambda t, par=par: t.matmul(
                        tot_ps, ones_col_bf[:, 0:1], mask_bf[par][:], start=True, stop=True)),
                        deps=[tmb, tpf])
                    b_w2o.wrote(ttot)
                    tbb = sch.op("t", (lambda t: t.matmul(
                        baseb_ps, ones_row[:1, :], base_row[:1, :], start=True, stop=True)),
                        deps=[tpf] + b_baser.read_deps() + b_baser.write_deps() + [tok_const[0]])
                    b_baser.read(tbb)
                    b_w2o.wrote(tbb)
                    R = rsc
                    t_s = sch.op("v", (lambda v: v.tensor_copy(R[:, 80:96], baseb_ps)),
                                 deps=[tbb] + b_rsc.write_deps())
                    b_w2o.read(t_s)
                    t_s2 = sch.op("v", (lambda v: v.tensor_add(R[:, 0:16], pfx_ps, R[:, 80:96])),
                                  deps=[t_s, tpf])
                    b_w2o.read(t_s2)
                    t_bu = sch.op("v", (lambda v: v.tensor_add(
                        base_row[:1, :], base_row[:1, :], tot_ps[:1, :])),
                        deps=[ttot, tbb])
                    b_baser.wrote(t_bu)
                    b_w2o.read(t_bu)
                    t_ok = sch.op("v", (lambda v: v.tensor_scalar(
                        out=R[:, 16:32], in0=R[:, 0:16], scalar1=float(CAP),
                        scalar2=None, op0=ALU.is_lt)), deps=[t_s2])
                    t_t1 = sch.op("v", (lambda v: v.tensor_add(R[:, 32:48], R[:, 0:16], ecb[:])),
                                  deps=[t_s2, tok_const[0]])
                    t_c1 = sch.op("v", (lambda v: v.tensor_scalar(
                        out=R[:, 32:48], in0=R[:, 32:48], scalar1=float(ZROW),
                        scalar2=None, op0=ALU.subtract)), deps=[t_t1])
                    t_c2 = sch.op("v", (lambda v: v.tensor_mul(
                        R[:, 32:48], R[:, 32:48], R[:, 16:32])), deps=[t_c1, t_ok])
                    t_c3 = sch.op("v", (lambda v: v.tensor_scalar(
                        out=R[:, 32:48], in0=R[:, 32:48], scalar1=float(ZROW),
                        scalar2=None, op0=ALU.add)), deps=[t_c2])
                    t_m1 = sch.op("v", (lambda v: v.tensor_scalar(
                        out=R[:, 48:64], in0=R[:, 32:48], scalar1=float(ZROW),
                        scalar2=None, op0=ALU.subtract)), deps=[t_c3])
                    t_m2 = sch.op("v", (lambda v, par=par: v.tensor_mul(
                        R[:, 48:64], R[:, 48:64], maskf[par][:])), deps=[t_m1, tmk])
                    t_m3 = sch.op("v", (lambda v: v.tensor_scalar(
                        out=R[:, 48:64], in0=R[:, 48:64], scalar1=float(ZROW),
                        scalar2=None, op0=ALU.add)), deps=[t_m2])
                    t_x1 = sch.op("v", (lambda v: v.tensor_scalar(
                        out=R[:, 64:80], in0=R[:, 32:48], scalar1=1.0,
                        scalar2=None, op0=ALU.add)), deps=[t_c3])
                    t_x2 = sch.op("v", (lambda v, par=par: v.tensor_mul(
                        R[:, 64:80], R[:, 64:80], maskf[par][:])), deps=[t_x1, tmk])
                    t_x3 = sch.op("v", (lambda v: v.tensor_scalar(
                        out=R[:, 64:80], in0=R[:, 64:80], scalar1=1.0,
                        scalar2=None, op0=ALU.subtract)), deps=[t_x2])
                    t_d0 = sch.op("v", (lambda v: v.tensor_reduce(
                        dkf[:, 0:1], R[:, 48:64], axis=AX.X, op=ALU.min)),
                        deps=[t_m3])
                    t_d1 = sch.op("v", (lambda v: v.reduce_max(
                        dkf[:, 1:2], R[:, 64:80], axis=AX.X)), deps=[t_x3])
                    b_rsc.wrote(t_d1)
                    tci = sch.op("v", (lambda v, sp=sp, tt=tt: v.tensor_copy(
                        dk_all[sp][:, 2 * tt:2 * tt + 2], dkf[:, 0:2])),
                        deps=[t_d0, t_d1] + b_dk[sp][tt].write_deps())
                    b_dk[sp][tt].wrote(tci)
                    for k in range(2):
                        tsc_ = sch.dma("g", (lambda g, sp=sp, par=par, tt=tt, k=k: g.indirect_dma_start(
                            out=xg_dram[sp], out_offset=IndirectOffsetOnAxis(
                                dk_all[sp][:, 2 * tt + k:2 * tt + k + 1], 0),
                            in_=xhat_tm[par][:], in_offset=None)),
                            deps=[tci, tln] + (b_xgd[sp].write_deps() if tt == 0 and k == 0 else [])
                                 + ([b_xgd[sp].w] if b_xgd[sp].w else []),
                            ch=f"sx{sp}{par}")
                        b_xhtm[par].read(tsc_)
                        b_dk[sp][tt].read(tsc_)
                        b_xgd[sp].wrote(tsc_)
                        scat_last[sp][par] = tsc_

                def gate_burst(st_i):
                    for tt in range(NT_ST):
                        gate_tile(st_i, tt)

                def gate_tile(st_i, tt):
                    sp = st_i % 2
                    if True:
                        par = tt % 2
                        tmx = sch.op("v", (lambda v, sp=sp, par=par, tt=tt: v.reduce_max(
                            gsc[par][:, 0:1], glog_all[sp][:, tt * 16:(tt + 1) * 16],
                            axis=AX.X, negate=True)),
                            deps=b_cw[sp][tt].read_deps() + b_g16[par].write_deps())
                        b_cw[sp][tt].read(tmx)
                        texp = sch.op("s", (lambda s, sp=sp, par=par, tt=tt: s.activation(
                            gp16[par][:], glog_all[sp][:, tt * 16:(tt + 1) * 16],
                            AF.Exp, bias=gsc[par][:, 0:1],
                            accum_out=gsc[par][:, 1:2])), deps=[tmx])
                        b_cw[sp][tt].read(texp)
                        trc = sch.op("v", (lambda v, par=par: v.reciprocal(
                            gsc[par][:, 0:1], gsc[par][:, 1:2])), deps=[texp])
                        tsc = sch.op("v", (lambda v, par=par: v.tensor_scalar(
                            out=s16[par][:], in0=gp16[par][:], scalar1=gsc[par][:, 0:1],
                            scalar2=None, op0=ALU.mult)), deps=[trc])
                        if not sparse:
                            tdm = sch.op("v", (lambda v, par=par: v.max(
                                out=m8[par][:], in_=s16[par][:])), deps=[tsc])
                            tcw = sch.op("v", (lambda v, sp=sp, par=par, tt=tt: v.scalar_tensor_tensor(
                                out=cw_all[sp][:, tt * 16:(tt + 1) * 16], in0=s16[par][:],
                                scalar=m8[par][:, 1:2], in1=s16[par][:],
                                op0=ALU.is_ge, op1=ALU.mult)),
                                deps=[tdm] + b_cw[sp][tt].write_deps())
                            b_cw[sp][tt].wrote(tcw)
                            b_g16[par].wrote(tcw)
                            return
                        ms = (lambda sp=sp, tt=tt: mask_all[sp][:, tt * 16:(tt + 1) * 16])
                        te0 = sch.op("v", (lambda v, par=par: v.tensor_scalar(
                            out=gp16[par][:], in0=iotae[:], scalar1=16.0,
                            scalar2=None, op0=ALU.subtract)),
                            deps=[tsc, tok_const[0]])
                        te0b = sch.op("v", (lambda v, sp=sp, par=par, tt=tt: v.tensor_mul(
                            gp16[par][:], gp16[par][:],
                            mask_all[sp][:, tt * 16:(tt + 1) * 16])),
                            deps=[te0] + b_mall[sp][tt].read_deps())
                        b_mall[sp][tt].read(te0b)
                        te0c = sch.op("v", (lambda v, par=par: v.tensor_scalar(
                            out=gp16[par][:], in0=gp16[par][:], scalar1=16.0,
                            scalar2=None, op0=ALU.add)), deps=[te0b])
                        te1 = sch.op("v", (lambda v, par=par: v.tensor_reduce(
                            gsc[par][:, 2:3], gp16[par][:], axis=AX.X, op=ALU.min)),
                            deps=[te0c])
                        te2 = sch.op("v", (lambda v, par=par: v.tensor_scalar(
                            out=gp16[par][:], in0=iotae[:], scalar1=1.0,
                            scalar2=None, op0=ALU.add)), deps=[te1])
                        te2b = sch.op("v", (lambda v, sp=sp, par=par, tt=tt: v.tensor_mul(
                            gp16[par][:], gp16[par][:],
                            mask_all[sp][:, tt * 16:(tt + 1) * 16])),
                            deps=[te2])
                        b_mall[sp][tt].read(te2b)
                        te2c = sch.op("v", (lambda v, par=par: v.tensor_scalar(
                            out=gp16[par][:], in0=gp16[par][:], scalar1=1.0,
                            scalar2=None, op0=ALU.subtract)), deps=[te2b])
                        te3 = sch.op("v", (lambda v, par=par: v.reduce_max(
                            gsc[par][:, 3:4], gp16[par][:], axis=AX.X)), deps=[te2c])
                        tq1 = None
                        for k in range(2):
                            tq0 = sch.op("v", (lambda v, par=par, k=k: v.tensor_scalar(
                                out=gp16[par][:], in0=iotae[:],
                                scalar1=gsc[par][:, 2 + k:3 + k], scalar2=None,
                                op0=ALU.is_equal)),
                                deps=[te3] + ([tq1] if tq1 else []))
                            tq1 = sch.op("v", (lambda v, sp=sp, par=par, tt=tt, k=k: v.scalar_tensor_tensor(
                                out=m8[par][:, 0:8].to_broadcast([128, 8]) if False else gp16[par][:],
                                in0=gp16[par][:], scalar=1.0, in1=s16[par][:],
                                op0=ALU.bypass, op1=ALU.mult,
                                accum_out=cwk_all[sp][:, 2 * tt + k:2 * tt + k + 1])),
                                deps=[tq0] + b_cwk[sp][tt].write_deps())
                            b_cwk[sp][tt].wrote(tq1)
                        b_g16[par].wrote(tq1)

                def passC_tile(st_c, tt):
                    sp = st_c % 2
                    row0 = st_c * ST
                    par = tt % 2
                    r0 = row0 + tt * 128
                    gt = r0 // 128
                    dld = sch.dma("y", (lambda y, r0=r0, par=par: y.dma_start(
                        xctile[par][:], src_dram[r0:r0 + 128, :])),
                        deps=b_xc[par].write_deps() + ([src_toks[gt]] if src_toks else []),
                        ch=f"xc{par}")
                    b_xc[par].wrote(dld)
                    if not sparse:
                        tfin = sch.op("v", (lambda v, par=par, tt=tt: v.scalar_tensor_tensor(
                            out=xnew[par][:], in0=zacc[:, tt * D:(tt + 1) * D], scalar=0.5,
                            in1=xctile[par][:], op0=ALU.mult, op1=ALU.add)),
                            deps=[dld] + b_z[tt].read_deps() + b_z[tt].write_deps()
                                 + b_xn[par].write_deps())
                        b_xc[par].read(tfin)
                        b_z[tt].read(tfin)
                        b_xn[par].wrote(tfin)
                        dst = sch.dma("y", (lambda y, r0=r0, par=par: y.dma_start(
                            dst_dram[r0:r0 + 128, :], xnew[par][:])), deps=[tfin],
                            ch=f"so{par}")
                        b_xn[par].read(dst)
                        store_toks[dst_key][gt] = dst
                        return
                    tg0 = sch.dma("g", (lambda g, sp=sp, tt=tt, par=par: g.indirect_dma_start(
                        out=zk_sb[par][:], out_offset=None,
                        in_=zg_dram[sp], in_offset=IndirectOffsetOnAxis(
                            dk_all[sp][:, 2 * tt:2 * tt + 1], 0))),
                        deps=b_zk[par].write_deps()
                             + [t for t in zg_last[sp] if t]
                             + b_dk[sp][tt].read_deps(),
                        ch=f"zk{par}")
                    b_zgd[sp].read(tg0)
                    b_dk[sp][tt].read(tg0)
                    b_zk[par].wrote(tg0)
                    ta0 = sch.op("v", (lambda v, sp=sp, tt=tt, par=par: v.scalar_tensor_tensor(
                        out=xnew[par][:], in0=zk_sb[par][:],
                        scalar=cwk_all[sp][:, 2 * tt:2 * tt + 1],
                        in1=zacc[:, tt * D:(tt + 1) * D], op0=ALU.mult, op1=ALU.add)),
                        deps=[tg0] + b_z[tt].read_deps() + b_z[tt].write_deps()
                             + b_cwk[sp][tt].read_deps() + [b_cwk[sp][tt].w]
                             + b_xn[par].write_deps())
                    b_z[tt].read(ta0)
                    b_cwk[sp][tt].read(ta0)
                    b_zk[par].read(ta0)
                    b_xn[par].wrote(ta0)
                    tg1 = sch.dma("g", (lambda g, sp=sp, tt=tt, par=par: g.indirect_dma_start(
                        out=zk_sb[par][:], out_offset=None,
                        in_=zg_dram[sp], in_offset=IndirectOffsetOnAxis(
                            dk_all[sp][:, 2 * tt + 1:2 * tt + 2], 0))),
                        deps=[ta0] + [t for t in zg_last[sp] if t], ch=f"zk{par}")
                    b_zgd[sp].read(tg1)
                    b_zk[par].wrote(tg1)
                    ta1 = sch.op("v", (lambda v, sp=sp, tt=tt, par=par: v.scalar_tensor_tensor(
                        out=xnew[par][:], in0=zk_sb[par][:],
                        scalar=cwk_all[sp][:, 2 * tt + 1:2 * tt + 2],
                        in1=xnew[par][:], op0=ALU.mult, op1=ALU.add)),
                        deps=[tg1, ta0])
                    b_zk[par].read(ta1)
                    b_xn[par].wrote(ta1)
                    tfin = sch.op("v", (lambda v, par=par: v.scalar_tensor_tensor(
                        out=xnew[par][:], in0=xnew[par][:], scalar=0.5,
                        in1=xctile[par][:], op0=ALU.mult, op1=ALU.add)),
                        deps=[ta1, dld])
                    b_xc[par].read(tfin)
                    b_xn[par].wrote(tfin)
                    dst = sch.dma("y", (lambda y, r0=r0, par=par: y.dma_start(
                        dst_dram[r0:r0 + 128, :], xnew[par][:])), deps=[tfin],
                        ch=f"so{par}")
                    b_xn[par].read(dst)
                    store_toks[dst_key][gt] = dst

                if sparse:
                    tbi = sch.op("v", lambda v: v.memset(base_row[:1, :], 0.0))
                    b_baser.wrote(tbi)
                    passA_front(0, 0)
                    passA_front(0, 1)
                    passA_back(0, 0)
                    passA_front(0, 2)
                    passA_back(0, 1)
                    passA_front(0, 3)
                else:
                    for tt in range(NT_ST):
                        passA_front(0, tt)
                        passA_back(0, tt)
                    gate_burst(0)

                for st_i in range(NST):
                    row0 = st_i * ST
                    sp = st_i % 2
                    if st_i == 0:
                        wdma(16 if sparse else 0, cast=first)
                        wdma(17 if sparse else 1, cast=first)

                    if sparse:
                        steps = ([(pe, tt) for pe in (16, 17) for tt in range(NT_ST)]
                                 + [(e, sl) for e in range(NEXP) for sl in range(CAPT)])
                    else:
                        steps = [(e, tt) for e in range(NPE) for tt in range(NT_ST)]
                    nsteps = len(steps)
                    tok_ld = {}
                    tok_trx = {}
                    tok_w1mm = {}
                    tok_hstt = {}
                    tok_hev = {}
                    tok_w2mm = {}

                    def src_rows(i):
                        e, sl = steps[i]
                        return xg_dram[sp][e * CAP + sl * 128: e * CAP + (sl + 1) * 128, :]

                    def emit_ldx(i):
                        e, sl = steps[i]
                        p = i % 3
                        deps = b_xg[p].write_deps() + [t for t in scat_last[sp] if t]
                        rows_ap = src_rows(i)
                        tk = sch.dma("y", (lambda y, p=p, rows_ap=rows_ap: y.dma_start(
                            xg_sb[p][:], rows_ap)), deps=deps, ch=f"xg{p}")
                        b_xg[p].wrote(tk)
                        b_xgd[sp].read(tk)
                        tok_ld[i] = tk

                    def emit_trx(i):
                        e, sl = steps[i]
                        p3 = i % 3
                        p = i % 2
                        tk = sch.op("t", (lambda t, p=p, p3=p3: [
                            t.transpose(hT_ps[p][:, c * 128:(c + 1) * 128],
                                        xg_sb[p3][:, c * 128:(c + 1) * 128],
                                        ident[:]) for c in range(KC)][-1]),
                            deps=[tok_ld[i]] + b_hTps[p].write_deps() + tok_const)
                        b_xg[p3].read(tk)
                        b_hTps[p].wrote(tk)
                        tev = sch.op("v", (lambda v, p=p: v.tensor_copy(
                            xgT[p][:], hT_ps[p][:])),
                            deps=[tk] + b_xgT[p].write_deps())
                        b_hTps[p].read(tev)
                        b_xgT[p].wrote(tev)
                        tok_trx[i] = tev

                    def emit_w1mm(i):
                        e, sl = steps[i]
                        p = i % 2
                        sl_w = e % 2
                        if sparse and e < NEXP:
                            deps = (b_yps[p].write_deps() + b_w1[sl_w].read_deps() + [tok_trx[i]])
                            tk = sch.op("t", (lambda t, p=p, sl_w=sl_w: [
                                t.matmul(y_ps[p][:, n * 512:(n + 1) * 512],
                                         xgT[p][:, k * 128:(k + 1) * 128],
                                         w1s[sl_w][:, k * 1024 + n * 512:k * 1024 + (n + 1) * 512],
                                         start=(k == 0), stop=(k == KC - 1))
                                for k in range(KC) for n in range(2)][-1]), deps=deps)
                            b_xgT[p].read(tk)
                        else:
                            tt = sl
                            deps = (b_yps[p].write_deps() + b_w1[sl_w].read_deps()
                                    + b_xfm[sp][tt].read_deps())
                            tk = sch.op("t", (lambda t, p=p, sl_w=sl_w, sp=sp, tt=tt: [
                                t.matmul(y_ps[p][:, n * 512:(n + 1) * 512],
                                         xhat_fm[sp][:, k * ST + tt * 128:k * ST + (tt + 1) * 128],
                                         w1s[sl_w][:, k * 1024 + n * 512:k * 1024 + (n + 1) * 512],
                                         start=(k == 0), stop=(k == KC - 1))
                                for k in range(KC) for n in range(2)][-1]), deps=deps)
                            b_xfm[sp][tt].read(tk)
                        tok_w1mm[i] = tk
                        b_yps[p].wrote(tk)
                        b_w1[sl_w].read(tk)

                    def emit_gating(i):
                        e, sl = steps[i]
                        p = i % 2
                        tsil = sch.op("s", (lambda s, p=p: s.activation(
                            sg[p][:], y_ps[p][:, 512:1024], AF.Silu)),
                            deps=[tok_w1mm[i]] + b_sg[p].write_deps())
                        b_yps[p].read(tsil)
                        b_sg[p].wrote(tsil)
                        use_cw = (not sparse) and e < NEXP
                        th = sch.op("v", (lambda v, p=p, sp=sp, e=e, sl=sl, use_cw=use_cw: v.scalar_tensor_tensor(
                            out=h_bf[p][:], in0=y_ps[p][:, 0:512],
                            scalar=(cw_all[sp][:, sl * 16 + e:sl * 16 + e + 1] if use_cw else 1.0),
                            in1=sg[p][:], op0=ALU.mult, op1=ALU.mult)),
                            deps=[tsil, tok_w1mm[i]] + b_h[p].write_deps()
                                 + (b_cw[sp][sl].read_deps() + [b_cw[sp][sl].w] if use_cw else []))
                        b_yps[p].read(th)
                        b_sg[p].read(th)
                        b_h[p].wrote(th)
                        if use_cw:
                            b_cw[sp][sl].read(th)
                        tok_hstt[i] = th

                    def emit_tr(i):
                        e, sl = steps[i]
                        p = i % 2
                        p3 = i % 3
                        tk = sch.op("t", (lambda t, p=p: [
                            t.transpose(hT_ps[p][:, j * 128:(j + 1) * 128],
                                        h_bf[p][:, j * 128:(j + 1) * 128], ident[:])
                            for j in range(4)][-1]),
                            deps=[tok_hstt[i]] + b_hTps[p].write_deps())
                        b_h[p].read(tk)
                        b_hTps[p].wrote(tk)
                        tev = sch.op("v", (lambda v, p=p, p3=p3: v.tensor_copy(
                            hT_sb[p3][:], hT_ps[p][:, 0:512])),
                            deps=[tk] + b_hTsb[p3].write_deps())
                        b_hTps[p].read(tev)
                        b_hTsb[p3].wrote(tev)
                        tok_hev[i] = tev

                    def emit_w2mm(i):
                        e, sl = steps[i]
                        p3 = i % 3
                        sl_w = e % 2
                        deps = ([tok_hev[i]] + b_w2o.write_deps() + b_w2[sl_w].read_deps())
                        tk = sch.op("t", (lambda t, p3=p3, sl_w=sl_w: [
                            t.matmul(w2o[:, n * 512:(n + 1) * 512],
                                     hT_sb[p3][:, k * 128:(k + 1) * 128],
                                     w2s[sl_w][:, k * 1024 + n * 512:k * 1024 + (n + 1) * 512],
                                     start=(k == 0), stop=(k == 3))
                            for k in range(4) for n in range(2)][-1]), deps=deps)
                        tok_w2mm[i] = tk
                        b_hTsb[p3].read(tk)
                        b_w2[sl_w].read(tk)
                        b_w2o.wrote(tk)

                    def emit_sink(i):
                        e, sl = steps[i]
                        p = i % 2
                        if sparse and e < NEXP:
                            tk = sch.op("v", (lambda v, p=p: v.tensor_copy(
                                zg_sb[p][:], w2o[:])),
                                deps=[tok_w2mm[i]] + b_zg[p].write_deps())
                            b_w2o.read(tk)
                            b_zg[p].wrote(tk)
                            td = sch.dma("s", (lambda y, p=p, e=e, sl=sl, sp=sp: y.dma_start(
                                zg_dram[sp][e * CAP + sl * 128:e * CAP + (sl + 1) * 128, :],
                                zg_sb[p][:])),
                                deps=[tk] + (b_zgd[sp].write_deps() if (e == 0 and sl == 0) else []),
                                ch=f"zg{sp}{p}")
                            b_zg[p].read(td)
                            b_zgd[sp].wrote(td)
                            zg_last[sp][p] = td
                        else:
                            tt = sl
                            zslv = zacc[:, tt * D:(tt + 1) * D]
                            if e == (16 if sparse else 0):
                                tk = sch.op("v", (lambda v, zslv=zslv: v.tensor_copy(zslv, w2o[:])),
                                            deps=[tok_w2mm[i]] + b_z[tt].write_deps())
                            else:
                                tk = sch.op("v", (lambda v, zslv=zslv: v.tensor_add(zslv, zslv, w2o[:])),
                                            deps=[tok_w2mm[i]] + b_z[tt].read_deps()
                                                 + b_z[tt].write_deps())
                            b_z[tt].wrote(tk)
                            b_w2o.read(tk)

                    fr_next = [0]
                    bk_next = [0]
                    cur_fr = [4]
                    cur_bk = [2]
                    for i in range(nsteps):
                        e, sl = steps[i]
                        if st_i > 0 and i < NT_ST:
                            passC_tile(st_i - 1, i)
                        if sparse and st_i == 0:
                            if cur_bk[0] < NT_ST:
                                passA_back(0, cur_bk[0])
                                cur_bk[0] += 1
                            if cur_fr[0] < NT_ST:
                                passA_front(0, cur_fr[0])
                                cur_fr[0] += 1
                        if sparse and 18 <= i < 18 + NT_ST:
                            gate_tile(st_i, i - 18)
                        if (not sparse) and st_i + 1 < NST and 250 <= i < 250 + NT_ST:
                            gate_tile(st_i + 1, i - 250)
                        if st_i + 1 < NST and i >= (16 if sparse else 2):
                            if i % 3 == 0 and fr_next[0] < NT_ST:
                                passA_front(st_i + 1, fr_next[0])
                                fr_next[0] += 1
                            if (i % 3 == 2 and bk_next[0] < NT_ST
                                    and (bk_next[0] <= fr_next[0] - 2
                                         or fr_next[0] == NT_ST)):
                                passA_back(st_i + 1, bk_next[0])
                                bk_next[0] += 1
                        if sparse and i + 2 < nsteps and steps[i + 2][0] < NEXP:
                            emit_ldx(i + 2)
                        emit_w1mm(i)
                        if sparse and i + 1 < nsteps and steps[i + 1][0] < NEXP:
                            emit_trx(i + 1)
                        if i >= 1:
                            emit_tr(i - 1)
                        if i >= 2:
                            emit_w2mm(i - 2)
                            emit_sink(i - 2)
                        emit_gating(i)
                        if sl == 1:
                            if sparse:
                                if e == 17:
                                    wdma(0, cast=(first and st_i == 0))
                                elif e < 15:
                                    wdma(e + 1, cast=(first and st_i == 0))
                            elif e >= 1 and e + 1 < NPE:
                                wdma(e + 1, cast=(first and st_i == 0))
                    emit_tr(nsteps - 1)
                    for i in (nsteps - 2, nsteps - 1):
                        emit_w2mm(i)
                        emit_sink(i)
                    if st_i + 1 < NST:
                        while fr_next[0] < NT_ST:
                            passA_front(st_i + 1, fr_next[0])
                            fr_next[0] += 1
                        while bk_next[0] < NT_ST:
                            passA_back(st_i + 1, bk_next[0])
                            bk_next[0] += 1
                        wdma(16 if sparse else 0, cast=False)
                        wdma(17 if sparse else 1, cast=False)

                for tt in range(NT_ST):
                    passC_tile(NST - 1, tt)
                sch.barrier()

        def emit_attn(src_dram, dst_dram, dst_key, src_toks):
            with ExitStack() as att:
                AE = att.enter_context
                wq_sb = AE(nc.sbuf_tensor(nm("wq_sb"), [128, KC * 1024], BF16))
                wk_sb = AE(nc.sbuf_tensor(nm("wk_sb"), [128, KC * 256], BF16))
                wv_sb = AE(nc.sbuf_tensor(nm("wv_sb"), [128, KC * 256], BF16))
                wo_sb = AE(nc.sbuf_tensor(nm("wo_sb"), [128, KC * 1024], BF16))
                bqs_b = AE(nc.sbuf_tensor(nm("bqs_b"), [128, D], F32))
                bk_b = AE(nc.sbuf_tensor(nm("bk_b"), [128, 256], F32))
                bv_b = AE(nc.sbuf_tensor(nm("bv_b"), [128, 256], F32))
                bo_b = AE(nc.sbuf_tensor(nm("bo_b"), [128, D], F32))
                g_b = AE(nc.sbuf_tensor(nm("g_b"), [128, D], F32))
                b_b = AE(nc.sbuf_tensor(nm("b_b"), [128, D], F32))
                q_fm = AE(nc.sbuf_tensor(nm("q_fm"), [128, 16 * S], BF16))
                k_fm = AE(nc.sbuf_tensor(nm("k_fm"), [128, 4 * S], BF16))
                v_sb = AE(nc.sbuf_tensor(nm("v_sb"), [128, QT * 260], BF16))
                xtile = [AE(nc.sbuf_tensor(nm(f"axt{i}"), [128, D], F32)) for i in range(2)]
                xhat_tm = [AE(nc.sbuf_tensor(nm(f"axh{i}"), [128, D], BF16)) for i in range(2)]
                xaT = [AE(nc.sbuf_tensor(nm(f"xaT{i}"), [128, KC * 128], BF16)) for i in range(2)]
                q_tm = [AE(nc.sbuf_tensor(nm(f"q_tm{i}"), [128, D], BF16)) for i in range(2)]
                k_tm = [AE(nc.sbuf_tensor(nm(f"k_tm{i}"), [128, 256], BF16)) for i in range(2)]
                qn2_all = AE(nc.sbuf_tensor(nm("qn2_all"), [128, QT], F32))
                kn2_all = AE(nc.sbuf_tensor(nm("kn2_all"), [128, QT * 4], F32))
                qn8_all = AE(nc.sbuf_tensor(nm("qn8_all"), [128, QT], F32))
                qn8row = AE(nc.sbuf_tensor(nm("qn8row"), [1, S], BF16))
                knsc = AE(nc.sbuf_tensor(nm("knsc"), [128, 4], F32))
                pT_sb = [AE(nc.sbuf_tensor(nm(f"pT_sb{i}"), [128, S], BF16)) for i in range(2)]
                rinv = AE(nc.sbuf_tensor(nm("rinv"), [128, 16], F32))
                ao_sb = AE(nc.sbuf_tensor(nm("ao_sb"), [128, D], F32))
                t1_sb = AE(nc.sbuf_tensor(nm("t1_sb"), [128, D], F32))
                qsq = t1_sb
                t_bf = AE(nc.sbuf_tensor(nm("t_bf"), [128, D], BF16))
                tT_sb = AE(nc.sbuf_tensor(nm("tT_sb"), [128, D], BF16))
                xres = xtile
                xn1 = ao_sb
                xnew_s = AE(nc.sbuf_tensor(nm("axn"), [128, D], F32))
                xnew = [xnew_s, xnew_s]

                with nc.psum_tensor(nm("bc_ps"), [128, 512], F32) as bc_ps, \
                     nc.sbuf_tensor(nm("brow"), [1, D], F32) as brow:
                    b_rows2, b_bcps = Buf(), Buf()

                    def bcast2(dst, row_dram, n, scale=None):
                        dld = sch.dma("g", (lambda g, n=n, row_dram=row_dram: g.dma_start(
                            brow[:1, 0:n], row_dram)), deps=b_rows2.write_deps(), ch="drw")
                        b_rows2.wrote(dld)
                        if scale is not None:
                            dsc = sch.op("s", (lambda s, n=n, scale=scale: s.activation(
                                brow[:1, 0:n], brow[:1, 0:n], AF.Identity,
                                bias=f32(0.0), scale=f32(scale))), deps=[dld])
                            b_rows2.wrote(dsc)
                            src_tok = dsc
                        else:
                            src_tok = dld
                        last = None
                        for j in range((n + 511) // 512):
                            w = min(512, n - j * 512)
                            tmm = sch.op("t", (lambda t, j=j, w=w: t.matmul(
                                bc_ps[:, 0:w], ones_row[:1, :],
                                brow[:1, j * 512:j * 512 + w], start=True, stop=True)),
                                deps=[src_tok, tok_const[0]] + b_bcps.write_deps())
                            b_rows2.read(tmm)
                            b_bcps.wrote(tmm)
                            tev = sch.op("v", (lambda v, j=j, w=w, dst=dst: v.tensor_copy(
                                dst[:, j * 512:j * 512 + w], bc_ps[:, 0:w])), deps=[tmm])
                            b_bcps.read(tev)
                            last = tev
                        return last

                    bcast2(bqs_b, bq, D, scale=0.125)
                    bcast2(bk_b, bk, 256)
                    bcast2(bv_b, bv, 256)
                    bcast2(bo_b, bo, D)
                    bcast2(g_b, ln_g, D)
                    tok_bc = bcast2(b_b, ln_b, D)

                b_wq, b_wk, b_wv, b_wo = Buf(), Buf(), Buf(), Buf()
                sch.dma("g", lambda g: g.dma_start(
                    wq_sb[:].rearrange("p (c n) -> p c n", c=KC),
                    wq.rearrange("(c p) n -> p c n", p=128)), ch="wat")
                sch.dma("g", lambda g: g.dma_start(
                    wk_sb[:].rearrange("p (c n) -> p c n", c=KC),
                    wk.rearrange("(c p) n -> p c n", p=128)), ch="wat")
                sch.dma("g", lambda g: g.dma_start(
                    wv_sb[:].rearrange("p (c n) -> p c n", c=KC),
                    wv.rearrange("(c p) n -> p c n", p=128)), ch="wat")
                dw = sch.dma("g", lambda g: g.dma_start(
                    wo_sb[:].rearrange("p (c n) -> p c n", c=KC),
                    wo.rearrange("(c p) n -> p c n", p=128)), ch="wat")
                b_wq.wrote(dw); b_wk.wrote(dw); b_wv.wrote(dw); b_wo.wrote(dw)

                b_vsb = Buf()
                b_qfm = Buf()
                b_kfm = Buf()
                gctr = [0]

                for seq in range(SEQ_PER_CORE):
                    row0 = seq * S
                    b_xt = [Buf(), Buf()]
                    b_xhtm = [Buf(), Buf()]
                    b_xaT = [Buf(), Buf()]
                    b_qtm = [Buf(), Buf()]
                    b_ktm = [Buf(), Buf()]
                    b_qsq = Buf()
                    tok_ke = [None] * QT
                    tok_ve = [None] * QT
                    tok_qe = [None] * QT
                    with nc.psum_tensor(nm("xT_ps0"), [128, 1024], BF16) as xT_ps0, \
                         nc.psum_tensor(nm("xT_ps1"), [128, 1024], BF16) as xT_ps1, \
                         nc.psum_tensor(nm("q_ps"), [128, 1024], F32) as q_ps, \
                         nc.psum_tensor(nm("qT_ps"), [128, 1024], BF16) as qT_ps, \
                         nc.psum_tensor(nm("kv_ps"), [128, 512], F32) as kv_ps, \
                         nc.psum_tensor(nm("kT_ps"), [128, 256], BF16) as kT_ps, \
                         nc.psum_tensor(nm("sp_ps"), [128, 128], F32) as sp_ps:
                        xT_ps = [xT_ps0, xT_ps1]
                        b_xTps = [Buf(), Buf()]
                        b_qps, b_qTps, b_kvps, b_kTps, b_spps = \
                            Buf(), Buf(), Buf(), Buf(), Buf()
                        tko = sch.op("g", (lambda g: g.memset(
                            v_sb[:].rearrange("p (t h c) -> p t h c", h=4, c=65)[:, :, :, 64:65],
                            1.0)), deps=b_vsb.write_deps())
                        b_vsb.wrote(tko)
                        tkon = sch.op("g", (lambda g: g.memset(k_fm[64:65, :], 1.0)),
                                      deps=b_kfm.write_deps())
                        b_kfm.wrote(tkon)
                        tok_ln = [None] * QT

                        def p1_ln(tt):
                            par = tt % 2
                            r0 = row0 + tt * 128
                            gt = r0 // 128
                            dld = sch.dma("y", (lambda y, r0=r0, par=par: y.dma_start(
                                xtile[par][:], src_dram[r0:r0 + 128, :])),
                                deps=b_xt[par].write_deps() + ([src_toks[gt]] if src_toks else []),
                                ch=f"xa{par}")
                            b_xt[par].wrote(dld)
                            tln, _, lntoks = emit_ln(par, xtile[par][:], xhat_tm[par][:],
                                                     [dld] + b_xhtm[par].write_deps())
                            for tk in lntoks[:3]:
                                b_xt[par].read(tk)
                            b_xt[par].read(tln)
                            b_xhtm[par].wrote(tln)
                            tok_ln[tt] = tln

                        tok_ev = [None] * QT

                        def p1_stage1(tt):
                            par = tt % 2
                            tln = tok_ln[tt]
                            ttr = sch.op("t", (lambda t, par=par: [
                                t.transpose(xT_ps[par][:, c * 128:(c + 1) * 128],
                                            xhat_tm[par][:, c * 128:(c + 1) * 128],
                                            ident[:]) for c in range(KC)][-1]),
                                deps=[tln] + b_xTps[par].write_deps() + tok_const)
                            b_xhtm[par].read(ttr)
                            b_xTps[par].wrote(ttr)
                            tev = sch.op("v", (lambda v, par=par: v.tensor_copy(
                                xaT[par][:], xT_ps[par][:])),
                                deps=[ttr] + b_xaT[par].write_deps())
                            b_xTps[par].read(tev)
                            b_xaT[par].wrote(tev)
                            tok_ev[tt] = tev

                        p1_ln(0)
                        p1_ln(1)
                        p1_stage1(0)
                        for tt in range(QT):
                            par = tt % 2
                            r0 = row0 + tt * 128
                            gt = r0 // 128
                            if tt + 2 < QT:
                                p1_ln(tt + 2)
                            if tt + 1 < QT:
                                p1_stage1(tt + 1)
                            tev = tok_ev[tt]
                            tqm = sch.op("t", (lambda t, par=par: [
                                t.matmul(q_ps[:, n * 512:(n + 1) * 512],
                                         xaT[par][:, k * 128:(k + 1) * 128],
                                         wq_sb[:, k * 1024 + n * 512:k * 1024 + (n + 1) * 512],
                                         start=(k == 0), stop=(k == KC - 1))
                                for k in range(KC) for n in range(2)][-1]),
                                deps=[tev] + b_qps.write_deps() + b_wq.read_deps())
                            b_xaT[par].read(tqm)
                            b_wq.read(tqm)
                            b_qps.wrote(tqm)
                            tqs = sch.op("v", (lambda v, par=par: v.scalar_tensor_tensor(
                                out=q_tm[par][:], in0=q_ps[:], scalar=0.125,
                                in1=bqs_b[:], op0=ALU.mult, op1=ALU.add)),
                                deps=[tqm, tok_bc] + b_qtm[par].write_deps())
                            b_qps.read(tqs)
                            b_qtm[par].wrote(tqs)
                            tqn = sch.op("v", (lambda v, par=par, tt=tt: v.scalar_tensor_tensor(
                                out=qsq[:], in0=q_tm[par][:], scalar=1.0, in1=q_tm[par][:],
                                op0=ALU.bypass, op1=ALU.mult,
                                accum_out=qn2_all[:, tt:tt + 1])),
                                deps=[tqs] + b_qsq.write_deps())
                            b_qtm[par].read(tqn)
                            b_qsq.wrote(tqn)
                            tqt = sch.op("t", (lambda t, par=par: [
                                t.transpose(qT_ps[64 * (hq % 2):64 * (hq % 2) + 64,
                                                  (hq // 2) * 128:(hq // 2) * 128 + 128],
                                            q_tm[par][:, hq * 64:(hq + 1) * 64],
                                            ident[:]) for hq in range(16)][-1]),
                                deps=[tqs] + b_qTps.write_deps())
                            b_qtm[par].read(tqt)
                            b_qTps.wrote(tqt)
                            tqe = sch.op("v", (lambda v, tt=tt: [
                                v.tensor_copy(
                                    q_fm[0:64, :].rearrange("p (h n) -> p h n", n=S)[:, 0::2, tt * 128:(tt + 1) * 128],
                                    qT_ps[0:64, :].rearrange("p (h n) -> p h n", n=128)),
                                v.tensor_copy(
                                    q_fm[0:64, :].rearrange("p (h n) -> p h n", n=S)[:, 1::2, tt * 128:(tt + 1) * 128],
                                    qT_ps[64:128, :].rearrange("p (h n) -> p h n", n=128))][-1]),
                                deps=[tqt] + b_qfm.write_deps())
                            b_qTps.read(tqe)
                            tok_qe[tt] = tqe
                            tkm = sch.op("t", (lambda t, par=par: [
                                t.matmul(kv_ps[:, 0:256],
                                         xaT[par][:, k * 128:(k + 1) * 128],
                                         wk_sb[:, k * 256:(k + 1) * 256],
                                         start=(k == 0), stop=(k == KC - 1))
                                for k in range(KC)][-1]),
                                deps=[tev] + b_kvps.write_deps() + b_wk.read_deps())
                            b_xaT[par].read(tkm)
                            b_wk.read(tkm)
                            b_kvps.wrote(tkm)
                            tks = sch.op("v", (lambda v, par=par: v.scalar_tensor_tensor(
                                out=k_tm[par][:], in0=kv_ps[:, 0:256], scalar=1.0,
                                in1=bk_b[:], op0=ALU.mult, op1=ALU.add)),
                                deps=[tkm] + b_ktm[par].write_deps())
                            b_kvps.read(tks)
                            b_ktm[par].wrote(tks)
                            tkn = None
                            for h in range(4):
                                tkn = sch.op("v", (lambda v, par=par, tt=tt, h=h: v.scalar_tensor_tensor(
                                    out=qsq[:, h * 64:(h + 1) * 64],
                                    in0=k_tm[par][:, h * 64:(h + 1) * 64], scalar=1.0,
                                    in1=k_tm[par][:, h * 64:(h + 1) * 64],
                                    op0=ALU.bypass, op1=ALU.mult,
                                    accum_out=kn2_all[:, tt * 4 + h:tt * 4 + h + 1])),
                                    deps=[tks] + ([] if h else b_qsq.write_deps()))
                                b_ktm[par].read(tkn)
                            b_qsq.wrote(tkn)
                            tkt = sch.op("t", (lambda t, par=par: [
                                t.transpose(kT_ps[64 * (h % 2):64 * (h % 2) + 64,
                                                  (h // 2) * 128:(h // 2) * 128 + 128],
                                            k_tm[par][:, h * 64:(h + 1) * 64],
                                            ident[:]) for h in range(4)][-1]),
                                deps=[tks] + b_kTps.write_deps())
                            b_ktm[par].read(tkt)
                            b_kTps.wrote(tkt)
                            tke = sch.op("v", (lambda v, tt=tt: [
                                v.tensor_copy(
                                    k_fm[0:64, :].rearrange("p (h n) -> p h n", n=S)[:, 0::2, tt * 128:(tt + 1) * 128],
                                    kT_ps[0:64, 0:256].rearrange("p (h n) -> p h n", n=128)),
                                v.tensor_copy(
                                    k_fm[0:64, :].rearrange("p (h n) -> p h n", n=S)[:, 1::2, tt * 128:(tt + 1) * 128],
                                    kT_ps[64:128, 0:256].rearrange("p (h n) -> p h n", n=128))][-1]),
                                deps=[tkt] + b_kfm.write_deps())
                            b_kTps.read(tke)
                            tok_ke[tt] = tke
                            tvm = sch.op("t", (lambda t, par=par: [
                                t.matmul(kv_ps[:, 256:512],
                                         xaT[par][:, k * 128:(k + 1) * 128],
                                         wv_sb[:, k * 256:(k + 1) * 256],
                                         start=(k == 0), stop=(k == KC - 1))
                                for k in range(KC)][-1]),
                                deps=[tev, tks] + b_wv.read_deps())
                            b_xaT[par].read(tvm)
                            b_wv.read(tvm)
                            tvs = sch.op("v", (lambda v, tt=tt: v.scalar_tensor_tensor(
                                out=v_sb[:, tt * 260:(tt + 1) * 260].rearrange(
                                    "p (h c) -> p h c", c=65)[:, :, 0:64],
                                in0=kv_ps[:, 256:512].rearrange("p (h c) -> p h c", c=64),
                                scalar=1.0,
                                in1=bv_b[:].rearrange("p (h c) -> p h c", c=64),
                                op0=ALU.mult, op1=ALU.add)),
                                deps=[tvm, tko])
                            b_kvps.read(tvs)
                            b_kvps.wrote(tvm)
                            b_vsb.read(tvs)
                            tok_ve[tt] = tvs

                        tl1 = sch.op("s", (lambda s: s.activation(
                            qn8_all[:], qn2_all[:], AF.Ln)),
                            deps=[b_qsq.w, tok_qe[QT - 1]] + [t for t in [b_qsq.w] if t])
                        tl2 = sch.op("s", (lambda s: s.activation(
                            qn8_all[:], qn8_all[:], AF.Exp, bias=f32(-2.0794415416798357),
                            scale=f32(0.5))), deps=[tl1])
                        tok_qn8r = None
                        for tt in range(QT):
                            ttr2 = sch.op("t", (lambda t, tt=tt: t.transpose(
                                sp_ps[0:1, 0:128], qn8_all[:, tt:tt + 1], ident_f[:])),
                                deps=[tl2] + b_spps.write_deps() + tok_const)
                            b_spps.wrote(ttr2)
                            tev2 = sch.op("v", (lambda v, tt=tt: v.tensor_copy(
                                qn8row[0:1, tt * 128:(tt + 1) * 128], sp_ps[0:1, 0:128])),
                                deps=[ttr2])
                            b_spps.read(tev2)
                            tok_qn8r = tev2
                        tk1 = sch.op("v", (lambda v: v.reduce_max(
                            knsc[:, 0:1], kn2_all[:], axis=AX.X)), deps=[b_qsq.w])
                        tk2 = sch.op("t", (lambda t: t.transpose(
                            sp_ps[0:1, 0:128], knsc[:, 0:1], ident_f[:])),
                            deps=[tk1] + b_spps.write_deps())
                        b_spps.wrote(tk2)
                        tk4 = sch.op("v", (lambda v: v.reduce_max(
                            knsc[0:1, 1:2], sp_ps[0:1, 0:128], axis=AX.X)),
                            deps=[tk2])
                        b_spps.read(tk4)
                        tk5 = sch.op("s", (lambda s: s.activation(
                            knsc[0:1, 2:3], knsc[0:1, 1:2], AF.Ln)), deps=[tk4])
                        tk6 = sch.op("s", (lambda s: s.activation(
                            knsc[0:1, 2:3], knsc[0:1, 2:3], AF.Exp, bias=f32(0.0),
                            scale=f32(0.5))), deps=[tk5])
                        tk7 = sch.op("v", (lambda v: v.tensor_scalar_mul(
                            knsc[0:1, 3:4], knsc[0:1, 2:3], -1.0)), deps=[tk6])
                        tok_aug = None
                        for hq in range(16):
                            tok_aug = sch.op("v", (lambda v, hq=hq: v.tensor_scalar(
                                out=q_fm[64:65, hq * S:(hq + 1) * S], in0=qn8row[0:1, :],
                                scalar1=knsc[0:1, 3:4], scalar2=None, op0=ALU.mult)),
                                deps=[tk7, tok_qn8r] + (b_qfm.write_deps() if hq == 0 else []))
                        b_qfm.wrote(tok_aug)
                        for tt in range(QT):
                            b_qfm.read(tok_qe[tt])
                        sch.barrier()

                    with nc.psum_tensor(nm("sT_ps"), [128, 2048], F32) as sT_ps, \
                         nc.psum_tensor(nm("ao_ps"), [128, 3 * 512], F32) as ao_ps, \
                         nc.psum_tensor(nm("tT_ps"), [128, 1024], BF16) as tT_ps:
                        NSTB = 4
                        sT = [sT_ps[:, i * 512:(i + 1) * 512] for i in range(NSTB)]
                        wo_ps = ao_ps[:, 0:1024]
                        b_sT = [Buf() for _ in range(NSTB)]
                        b_aops, b_tTps = Buf(), Buf()
                        b_wops = b_aops
                        b_pT = [Buf(), Buf()]
                        b_rinv, b_ao = Buf(), Buf()
                        b_t1, b_t2, b_tbf, b_tT = Buf(), Buf(), Buf(), Buf()
                        b_xres = [Buf(), Buf()]
                        b_xn1 = Buf()
                        _bxn = Buf()
                        b_xn = [_bxn, _bxn]

                        def _ao_data_view():
                            return ao_ps[:].rearrange(
                                "p (b c) -> p b c", c=512).rearrange(
                                "p b (h c) -> p b h c", c=65)[:, :, :, 0:64].rearrange(
                                "p b h c -> p (b h) c")[:, 0:16, :] if False else None

                        p3b_pend = [None]

                        def emit_p3b():
                            if p3b_pend[0] is None:
                                return
                            tq, r0, gt, tb, drd = p3b_pend[0]
                            p3b_pend[0] = None
                            ttt = sch.op("t", (lambda t: [
                                t.transpose(tT_ps[:, c * 128:(c + 1) * 128],
                                            t_bf[:, c * 128:(c + 1) * 128], ident[:])
                                for c in range(KC)][-1]),
                                deps=[tb] + b_tTps.write_deps())
                            b_tbf.read(ttt)
                            b_tTps.wrote(ttt)
                            tte = sch.op("v", (lambda v: v.tensor_copy(
                                tT_sb[:], tT_ps[:])),
                                deps=[ttt] + b_tT.write_deps())
                            b_tTps.read(tte)
                            b_tT.wrote(tte)
                            twm = sch.op("t", (lambda t: [
                                t.matmul(wo_ps[:, n * 512:(n + 1) * 512],
                                         tT_sb[:, k * 128:(k + 1) * 128],
                                         wo_sb[:, k * 1024 + n * 512:k * 1024 + (n + 1) * 512],
                                         start=(k == 0), stop=(k == KC - 1))
                                for k in range(KC) for n in range(2)][-1]),
                                deps=[tte] + b_wops.write_deps() + b_wo.read_deps())
                            b_tT.read(twm)
                            b_wo.read(twm)
                            b_wops.wrote(twm)
                            tr1 = sch.op("v", (lambda v: v.scalar_tensor_tensor(
                                out=xn1[:], in0=wo_ps[:], scalar=1.0, in1=bo_b[:],
                                op0=ALU.mult, op1=ALU.add)),
                                deps=[twm] + b_xn1.write_deps() + b_ao.write_deps())
                            b_wops.read(tr1)
                            b_xn1.wrote(tr1)
                            b_ao.wrote(tr1)
                            tr2 = sch.op("v", (lambda v, tq=tq: v.tensor_add(
                                xnew[tq % 2][:], xn1[:], xres[tq % 2][:])),
                                deps=[tr1, drd] + b_xn[tq % 2].write_deps())
                            b_xn1.read(tr2)
                            b_xres[tq % 2].read(tr2)
                            b_xn[tq % 2].wrote(tr2)
                            dstk = sch.dma("y", (lambda y, r0=r0, tq=tq: y.dma_start(
                                dst_dram[r0:r0 + 128, :], xnew[tq % 2][:])), deps=[tr2],
                                ch=f"so{tq % 2}")
                            b_xn[tq % 2].read(dstk)
                            store_toks[dst_key][gt] = dstk

                        for tq in range(QT):
                            r0 = row0 + tq * 128
                            gt = r0 // 128
                            nk = tq + 1
                            drd = sch.dma("y", (lambda y, r0=r0, tq=tq: y.dma_start(
                                xres[tq % 2][:], src_dram[r0:r0 + 128, :])),
                                deps=b_xres[tq % 2].write_deps() + ([src_toks[gt]] if src_toks else []),
                                ch=f"xc{tq % 2}")
                            b_xres[tq % 2].wrote(drd)

                            ng = (nk + 3) // 4
                            items = [(hq, gi) for hq in range(16) for gi in range(ng)]
                            tok_exp = {}
                            tok_pv_last = None

                            def emit_pv(j):
                                nonlocal tok_pv_last
                                hq, gi = items[j]
                                h_ = hq % 4
                                hpar = hq % 2
                                c0 = 4 * gi
                                c1 = min(c0 + 4, nk)
                                off = (hq // 7) * 512 + (hq % 7) * 65
                                tk = sch.op("t", (lambda t, hq=hq, h_=h_, hpar=hpar,
                                                  c0=c0, c1=c1, nk=nk, off=off: [
                                    t.matmul(ao_ps[:, off:off + 65],
                                             pT_sb[hpar][:, c * 128:(c + 1) * 128],
                                             v_sb[:, c * 260 + h_ * 65:c * 260 + (h_ + 1) * 65],
                                             start=(c == 0), stop=(c == nk - 1))
                                    for c in range(c0, c1)][-1]),
                                    deps=[tok_exp[j], tok_ve[min(c1 - 1, nk - 1)]]
                                         + (b_aops.write_deps() if (hq == 0 and gi == 0) else []))
                                if c1 == nk:
                                    b_pT[hpar].read(tk)
                                tok_pv_last = tk

                            for j, (hq, gi) in enumerate(items):
                                if j == min(2, len(items) - 1):
                                    emit_p3b()
                                if j >= 2:
                                    emit_pv(j - 2)
                                h_ = hq % 4
                                hpar = hq % 2
                                c0 = 4 * gi
                                c1 = min(c0 + 4, nk)
                                gpar = gctr[0] % NSTB
                                gctr[0] += 1
                                tmm = sch.op("t", (lambda t, hq=hq, h_=h_, tq=tq, c0=c0, c1=c1, gpar=gpar: [
                                    t.matmul(sT[gpar][:, (c - c0) * 128:(c - c0 + 1) * 128],
                                             k_fm[0:65, h_ * S + c * 128:h_ * S + (c + 1) * 128],
                                             q_fm[0:65, hq * S + tq * 128:hq * S + (tq + 1) * 128],
                                             start=True, stop=True)
                                    for c in range(c0, c1)][-1]),
                                    deps=[tok_ke[c1 - 1], tok_qe[tq], tok_aug, tkon]
                                         + b_sT[gpar].write_deps())
                                b_sT[gpar].wrote(tmm)
                                tlast = tmm
                                if c0 <= tq < c1:
                                    tmk = sch.op("v", (lambda v, tq=tq, c0=c0, gpar=gpar: v.tensor_add(
                                        sT[gpar][:, (tq - c0) * 128:(tq - c0 + 1) * 128],
                                        sT[gpar][:, (tq - c0) * 128:(tq - c0 + 1) * 128],
                                        maskdT[:])),
                                        deps=[tmm] + tok_const)
                                    b_sT[gpar].wrote(tmk)
                                    tlast = tmk
                                texp = sch.op("s", (lambda s, hpar=hpar, c0=c0, c1=c1, gpar=gpar: s.activation(
                                    pT_sb[hpar][:, c0 * 128:c1 * 128],
                                    sT[gpar][:, 0:(c1 - c0) * 128], AF.Exp)),
                                    deps=[tlast] + (b_pT[hpar].write_deps() if gi == 0 else []))
                                b_sT[gpar].read(texp)
                                if gi == 0:
                                    b_pT[hpar].wrote(texp)
                                tok_exp[j] = texp
                            if len(items) >= 2:
                                emit_pv(len(items) - 2)
                            emit_pv(len(items) - 1)
                            b_aops.wrote(tok_pv_last)

                            trc = sch.op("v", (lambda v: [
                                v.reciprocal(
                                    rinv[:, g * 7:g * 7 + (7 if g < 2 else 2)],
                                    ao_ps[:, g * 512:g * 512 + (7 if g < 2 else 2) * 65]
                                    .rearrange("p (h c) -> p h c", c=65)[:, :, 64:65])
                                for g in range(3)][-1]),
                                deps=[tok_pv_last] + b_rinv.write_deps())
                            b_aops.read(trc)
                            b_rinv.wrote(trc)
                            def _aoevac(v):
                                last = None
                                for g, ng in ((0, 7), (1, 7), (2, 2)):
                                    last = v.tensor_tensor(
                                        ao_sb[:, g * 448:g * 448 + ng * 64].rearrange(
                                            "p (h c) -> p h c", c=64),
                                        ao_ps[:, g * 512:g * 512 + ng * 65].rearrange(
                                            "p (h c) -> p h c", c=65)[:, :, 0:64],
                                        rinv[:, g * 7:g * 7 + ng].rearrange(
                                            "p (h o) -> p h o", o=1).to_broadcast(
                                            [128, ng, 64]),
                                        op=ALU.mult)
                                return last
                            tae = sch.op("v", _aoevac,
                                deps=[tok_pv_last, trc] + b_ao.write_deps())
                            b_aops.read(tae)
                            b_rinv.read(tae)
                            b_ao.wrote(tae)
                            tln, _, lntoks = emit_ln(tq % 2, ao_sb[:], t1_sb[:],
                                                     [tae] + b_t1.write_deps())
                            for tk in lntoks[:3]:
                                b_ao.read(tk)
                            b_ao.read(tln)
                            b_t1.wrote(tln)
                            tg = sch.op("v", lambda v: v.tensor_mul(t1_sb[:], t1_sb[:], g_b[:]),
                                        deps=[tln])
                            b_t1.wrote(tg)
                            tb = sch.op("v", lambda v: v.tensor_add(t_bf[:], t1_sb[:], b_b[:]),
                                        deps=[tg] + b_tbf.write_deps())
                            b_t1.read(tb)
                            b_tbf.wrote(tb)
                            p3b_pend[0] = (tq, r0, gt, tb, drd)
                        emit_p3b()
                        sch.barrier()
                sch.barrier()

        if stage in ("f1", "f1dbg"):
            emit_ffn(x_in, out, "out", None, True, True)
        elif stage == "attn":
            emit_ffn(x_in, x1, "x1", None, True, True)
            emit_attn(x1, out, "out", store_toks["x1"])
        else:
            emit_ffn(x_in, x1, "x1", None, True, True)
            emit_attn(x1, x2, "x2", store_toks["x1"])
            emit_ffn(x2, out, "out", store_toks["x2"], False, False)

        with ExitStack() as semctx:
            semnames = sorted(set(list(sch.counts.keys()) + list(Sched.ENGS)))
            sems = {n: semctx.enter_context(nc.semaphore(f"sem_{n}")) for n in semnames}
            block = semctx.enter_context(nc.Block())

            def runner(name, eng):
                hw = {}
                for (e, waits, fn, amt, sem) in sch.recs:
                    if e != name:
                        continue
                    for (sname, val) in waits:
                        if hw.get(sname, 0) >= val:
                            continue
                        eng.wait_ge(sems[sname], val)
                        hw[sname] = val
                    if fn is not None:
                        inst = fn(eng)
                        if amt:
                            inst.then_inc(sems[sem], amt)
                if name == "y":
                    for sname, cnt in sorted(sch.counts.items()):
                        if sname not in Sched.ENGS and cnt:
                            eng.wait_ge(sems[sname], cnt)

            block.tensor(lambda t: runner("t", t))
            block.vector(lambda v: runner("v", v))
            block.scalar(lambda s: runner("s", s))
            block.gpsimd(lambda g: runner("g", g))
            block.sync(lambda y: runner("y", y))

    return nc


def kernel(L0_attn_x, L0_attn_Wq, L0_attn_bq, L0_attn_Wk, L0_attn_bk,
           L0_attn_Wv, L0_attn_bv, L0_attn_ln_g, L0_attn_ln_b, L0_attn_Wo,
           L0_attn_bo, L0_ffn_Wg, L0_ffn_W1, L0_ffn_W2, L0_ffn_Ws1, L0_ffn_Ws2,
           _stage="full"):
    x = np.ascontiguousarray(np.asarray(L0_attn_x, dtype=np.float32))
    shards = x.reshape(NCORES, TOK, D)
    common = {
        "Wq": np.ascontiguousarray(L0_attn_Wq, dtype=np.float32),
        "bq": np.ascontiguousarray(L0_attn_bq, dtype=np.float32).reshape(1, D),
        "Wk": np.ascontiguousarray(L0_attn_Wk, dtype=np.float32),
        "bk": np.ascontiguousarray(L0_attn_bk, dtype=np.float32).reshape(1, 256),
        "Wv": np.ascontiguousarray(L0_attn_Wv, dtype=np.float32),
        "bv": np.ascontiguousarray(L0_attn_bv, dtype=np.float32).reshape(1, 256),
        "ln_g": np.ascontiguousarray(L0_attn_ln_g, dtype=np.float32).reshape(1, D),
        "ln_b": np.ascontiguousarray(L0_attn_ln_b, dtype=np.float32).reshape(1, D),
        "Wo": np.ascontiguousarray(L0_attn_Wo, dtype=np.float32),
        "bo": np.ascontiguousarray(L0_attn_bo, dtype=np.float32).reshape(1, D),
        "Wg": np.ascontiguousarray(L0_ffn_Wg, dtype=np.float32),
        "W1": np.ascontiguousarray(L0_ffn_W1, dtype=np.float32),
        "W2": np.ascontiguousarray(L0_ffn_W2, dtype=np.float32),
        "Ws1": np.ascontiguousarray(L0_ffn_Ws1, dtype=np.float32),
        "Ws2": np.ascontiguousarray(L0_ffn_Ws2, dtype=np.float32),
    }
    nc = build(_stage)
    in_maps = [dict(common, x=np.ascontiguousarray(shards[c])) for c in range(NCORES)]
    res = run_bass_kernel_spmd(nc, in_maps, core_ids=list(range(NCORES)))
    outs = [res.results[c]["out"] for c in range(NCORES)]
    return np.concatenate(outs, axis=0).reshape(B, S, D)


# version 19
# speedup vs baseline: 1.1835x; 1.0329x over previous
from contextlib import ExitStack

import numpy as np
import concourse.bass as bass
import concourse.mybir as mybir
from concourse.bass_utils import run_bass_kernel_spmd
from concourse.bass_core_rust import IndirectOffsetOnAxis

F32 = mybir.dt.float32
BF16 = mybir.dt.bfloat16
I32 = mybir.dt.int32
RSQRT_MAGIC = 0x5F3759DF
AF = mybir.ActivationFunctionType
ALU = mybir.AluOpType
AX = mybir.AxisListType
f32 = np.float32

B, S, D = 16, 2048, 1024
NCORES = 8
SEQ_PER_CORE = B // NCORES
TOK = SEQ_PER_CORE * S
NT = TOK // 128
NEXP = 16
NPE = 18
ST = 2048
NT_ST = ST // 128
NST = TOK // ST
QT = S // 128
EPS = f32(1e-5)
KC = 8
CAP = 384
CAPT = CAP // 128
ZROW = 16 * CAP


class Sched:
    ENGS = ("t", "v", "s", "g", "y")

    def __init__(self):
        self.recs = []
        self.counts = {}
        self.barrier_level = {}

    def _chk(self, deps):
        waits = {}
        for d in deps:
            if d is None:
                continue
            sem, val = d
            assert val <= self.counts.get(sem, 0), (sem, val, self.counts.get(sem, 0))
            if waits.get(sem, 0) < val:
                waits[sem] = val
        for sem in [s for s, v in waits.items()
                    if s not in Sched.ENGS and v <= self.barrier_level.get(s, -1)]:
            del waits[sem]
        for sem, val in waits.items():
            if sem not in Sched.ENGS:
                assert val == self.counts.get(sem, 0), (
                    "unsafe intermediate DMA wait", sem, val, self.counts.get(sem, 0))
        return list(waits.items())

    def op(self, eng, fn, deps=()):
        waits = self._chk(deps)
        self.counts[eng] = self.counts.get(eng, 0) + 1
        tok = (eng, self.counts[eng])
        self.recs.append((eng, waits, fn, 1, eng))
        return tok

    def dma(self, q, fn, deps=(), ch="dy"):
        pass
        assert q in ("y", "g", "s")
        waits = self._chk(deps)
        sem = ch
        self.counts[sem] = self.counts.get(sem, 0) + 16
        tok = (sem, self.counts[sem])
        self.recs.append((q, waits, fn, 16, sem))
        return tok

    def wait_only(self, eng, deps):
        waits = self._chk(deps)
        if waits:
            self.recs.append((eng, waits, None, 0, None))

    def barrier(self):
        toks = [(s, v) for s, v in self.counts.items()]
        for e in self.ENGS:
            self.wait_only(e, toks)
        for s, v in self.counts.items():
            if s not in Sched.ENGS:
                self.barrier_level[s] = v


class Buf:

    def __init__(self):
        self.w = None
        self.r = {}

    def write_deps(self):
        return list(self.r.values()) or ([self.w] if self.w else [])

    def read_deps(self):
        return [self.w] if self.w else []

    def wrote(self, tok):
        self.w = tok
        self.r = {}

    def read(self, tok):
        self.r[tok[0]] = tok


def build(stage="full"):
    nc = bass.Bass("TRN2", target_bir_lowering=False)
    _uid = [0]

    def nm(base):
        _uid[0] += 1
        return f"{base}_{_uid[0]}"

    x_in = nc.dram_tensor("x", [TOK, D], F32, kind="ExternalInput").ap()
    wq = nc.dram_tensor("Wq", [D, D], F32, kind="ExternalInput").ap()
    bq = nc.dram_tensor("bq", [1, D], F32, kind="ExternalInput").ap()
    wk = nc.dram_tensor("Wk", [D, 256], F32, kind="ExternalInput").ap()
    bk = nc.dram_tensor("bk", [1, 256], F32, kind="ExternalInput").ap()
    wv = nc.dram_tensor("Wv", [D, 256], F32, kind="ExternalInput").ap()
    bv = nc.dram_tensor("bv", [1, 256], F32, kind="ExternalInput").ap()
    ln_g = nc.dram_tensor("ln_g", [1, D], F32, kind="ExternalInput").ap()
    ln_b = nc.dram_tensor("ln_b", [1, D], F32, kind="ExternalInput").ap()
    wo = nc.dram_tensor("Wo", [D, D], F32, kind="ExternalInput").ap()
    bo = nc.dram_tensor("bo", [1, D], F32, kind="ExternalInput").ap()
    wg = nc.dram_tensor("Wg", [NEXP, D], F32, kind="ExternalInput").ap()
    w1 = nc.dram_tensor("W1", [NEXP, D, 2 * 512], F32, kind="ExternalInput").ap()
    w2 = nc.dram_tensor("W2", [NEXP, 512, D], F32, kind="ExternalInput").ap()
    ws1 = nc.dram_tensor("Ws1", [D, 2048], F32, kind="ExternalInput").ap()
    ws2 = nc.dram_tensor("Ws2", [1024, D], F32, kind="ExternalInput").ap()
    out = nc.dram_tensor("out", [TOK, D], F32, kind="ExternalOutput").ap()
    x1 = nc.dram_tensor("x1", [TOK, D], F32, kind="Internal").ap()
    x2 = nc.dram_tensor("x2", [TOK, D], F32, kind="Internal").ap()
    xhat_dram = nc.dram_tensor("xhat_d", [TOK, D], BF16, kind="Internal").ap()
    xg_dram = [nc.dram_tensor(f"xg_d{i}", [ZROW + 1, D], BF16, kind="Internal").ap()
               for i in range(2)]
    zg_dram = [nc.dram_tensor(f"zg_d{i}", [ZROW + 1, D], F32, kind="Internal").ap()
               for i in range(2)]
    wbf1 = nc.dram_tensor("wbf1", [NPE, 128, KC * 1024], BF16, kind="Internal").ap()
    wbf2 = nc.dram_tensor("wbf2", [NPE, 128, 4 * 1024], BF16, kind="Internal").ap()

    sch = Sched()

    with ExitStack() as top:
        E = top.enter_context
        ones_row = E(nc.sbuf_tensor(nm("ones_row"), [1, 128], F32))
        ident = E(nc.sbuf_tensor(nm("ident"), [128, 128], BF16))
        maskdT = E(nc.sbuf_tensor(nm("maskdT"), [128, 128], F32))
        ident_f = E(nc.sbuf_tensor(nm("ident_f"), [128, 128], F32))
        wgT = E(nc.sbuf_tensor(nm("wgT"), [128, 128], BF16))
        ltri_bf = E(nc.sbuf_tensor(nm("ltri_bf"), [128, 128], BF16))
        ones_col_bf = E(nc.sbuf_tensor(nm("ones_col_bf"), [128, 1], BF16))
        ecb = E(nc.sbuf_tensor(nm("ecb"), [128, 16], F32))
        iotae = E(nc.sbuf_tensor(nm("iotae"), [128, 16], F32))


        with nc.psum_tensor(nm("set_ps"), [128, 512], F32) as set_ps, \
             nc.psum_tensor(nm("set_ps2"), [128, 128], BF16) as set_ps2, \
             nc.sbuf_tensor(nm("rows_sb"), [1, D], F32) as rows_sb, \
             nc.sbuf_tensor(nm("zero_row"), [1, D], F32) as zero_row:
            t0 = sch.op("g", lambda g: g.memset(ones_row[:], 1.0))
            t1 = sch.op("g", lambda g: g.memset(ident[:], 0.0))
            t2 = sch.op("g", lambda g: g.affine_select(
                out=ident[:], in_=ident[:], compare_op=ALU.not_equal, fill=1.0,
                base=0, pattern=[[-1, 128]], channel_multiplier=1), deps=[t1])
            t5 = sch.op("g", lambda g: g.memset(maskdT[:], 0.0))
            t6 = sch.op("g", lambda g: g.affine_select(
                out=maskdT[:], in_=maskdT[:], compare_op=ALU.is_ge, fill=-1e10,
                base=0, pattern=[[1, 128]], channel_multiplier=-1), deps=[t5])
            t7 = sch.op("g", lambda g: g.memset(ident_f[:], 0.0))
            t8 = sch.op("g", lambda g: g.affine_select(
                out=ident_f[:], in_=ident_f[:], compare_op=ALU.not_equal, fill=1.0,
                base=0, pattern=[[-1, 128]], channel_multiplier=1), deps=[t7])
            t9 = sch.op("g", lambda g: g.memset(ltri_bf[:], 0.0))
            t10 = sch.op("g", lambda g: g.affine_select(
                out=ltri_bf[:], in_=ltri_bf[:], compare_op=ALU.is_ge, fill=1.0,
                base=0, pattern=[[-1, 128]], channel_multiplier=1), deps=[t9])
            t11 = sch.op("g", lambda g: g.memset(ones_col_bf[:], 1.0))
            t12 = sch.op("g", lambda g: g.iota(
                ecb[:], pattern=[[CAP, 16]], base=0, channel_multiplier=0,
                allow_small_or_imprecise_dtypes=True))
            t13 = sch.op("g", lambda g: g.iota(
                iotae[:], pattern=[[1, 16]], base=0, channel_multiplier=0,
                allow_small_or_imprecise_dtypes=True))
            tok_const = [t0, t2, t6, t8, t10, t11, t12, t13]
            tz = sch.op("g", lambda g: g.memset(zero_row[:1, :], 0.0))
            for i in range(2):
                sch.dma("y", (lambda y, i=i: y.dma_start(
                    zg_dram[i][ZROW:ZROW + 1, :], zero_row[:1, :])),
                    deps=[tz], ch=f"zsink{i}")

            b_rows = Buf()
            b_setps = Buf()

            def _unused_bcast(dst, row_dram, n, scale=None):
                dld = sch.dma("y", (lambda y, n=n, row_dram=row_dram: y.dma_start(
                    rows_sb[:1, 0:n], row_dram)), deps=b_rows.write_deps(), ch="drw")
                b_rows.wrote(dld)
                if scale is not None:
                    dsc = sch.op("s", (lambda s, n=n, scale=scale: s.activation(
                        rows_sb[:1, 0:n], rows_sb[:1, 0:n], AF.Identity,
                        bias=f32(0.0), scale=f32(scale))), deps=[dld])
                    b_rows.wrote(dsc)
                    src_tok = dsc
                else:
                    src_tok = dld
                nj = (n + 511) // 512
                last = None
                for j in range(nj):
                    w = min(512, n - j * 512)
                    tmm = sch.op("t", (lambda t, j=j, w=w: t.matmul(
                        set_ps[:, 0:w], ones_row[:1, :],
                        rows_sb[:1, j * 512:j * 512 + w], start=True, stop=True)),
                        deps=[src_tok, t0] + b_setps.write_deps())
                    b_rows.read(tmm)
                    b_setps.wrote(tmm)
                    tev = sch.op("v", (lambda v, j=j, w=w, dst=dst: v.tensor_copy(
                        dst[:, j * 512:j * 512 + w], set_ps[:, 0:w])), deps=[tmm])
                    b_setps.read(tev)
                    last = tev
                return last

            tok_bcast = []

            with nc.sbuf_tensor(nm("wg_tmp"), [16, D], BF16) as wg_tmp:
                dwg = sch.dma("g", lambda g: g.dma_start(wg_tmp[:], wg), ch="dwg")
                tprev = None
                for c in range(KC):
                    ttr = sch.op("t", (lambda t, c=c: t.transpose(
                        set_ps2[:, 0:16], wg_tmp[:16, c * 128:(c + 1) * 128],
                        ident[0:16, 0:16])), deps=[dwg, t2] + ([tprev] if tprev else []))
                    tev = sch.op("v", (lambda v, c=c: v.tensor_copy(
                        wgT[:, c * 16:(c + 1) * 16], set_ps2[:, 0:16])), deps=[ttr])
                    tprev = tev
                tok_wgT = tprev
            sch.barrier()

        lnst = [E(nc.sbuf_tensor(nm(f"lnst{i}"), [128, 12], F32)) for i in range(2)]
        lnag = [E(nc.sbuf_tensor(nm(f"lnag{i}"), [128, 2], F32)) for i in range(2)]
        lnr = [E(nc.sbuf_tensor(nm(f"lnr{i}"), [128, 4], F32)) for i in range(2)]

        def emit_rsqrt(x_ap, xe_ap, y_ap, u_ap, eps, deps, iters=2):
            tk = sch.op("v", (lambda v: v.tensor_scalar_add(xe_ap, x_ap, eps)), deps=deps)
            tk = sch.op("v", (lambda v: v.tensor_scalar(
                out=y_ap.bitcast(I32), in0=xe_ap.bitcast(I32), scalar1=1,
                scalar2=None, op0=ALU.arith_shift_right)), deps=[tk])
            tk = sch.op("v", (lambda v: v.tensor_scalar(
                out=y_ap.bitcast(I32), in0=y_ap.bitcast(I32), scalar1=-1,
                scalar2=RSQRT_MAGIC, op0=ALU.mult, op1=ALU.add)), deps=[tk])
            for _ in range(iters):
                tk = sch.op("v", (lambda v: v.scalar_tensor_tensor(
                    out=u_ap, in0=y_ap, scalar=1.0, in1=y_ap,
                    op0=ALU.bypass, op1=ALU.mult)), deps=[tk])
                tk = sch.op("v", (lambda v: v.scalar_tensor_tensor(
                    out=u_ap, in0=u_ap, scalar=-0.5, in1=xe_ap,
                    op0=ALU.mult, op1=ALU.mult)), deps=[tk])
                tk = sch.op("v", (lambda v: v.scalar_tensor_tensor(
                    out=y_ap, in0=u_ap, scalar=1.5, in1=y_ap,
                    op0=ALU.add, op1=ALU.mult)), deps=[tk])
            return tk

        def emit_ln(par, src_ap, dst_ap, deps, scratch_r=None):
            stt = lnst[par]
            agg = lnag[par]
            rr = lnr[par] if scratch_r is None else scratch_r
            t_a = sch.op("v", lambda v: v.bn_stats(stt[:, 0:6], src_ap[:, 0:512]), deps=deps)
            t_b = sch.op("v", lambda v: v.bn_stats(stt[:, 6:12], src_ap[:, 512:1024]), deps=deps)
            t_c = sch.op("v", lambda v: v.bn_aggr(agg[:], stt[:]), deps=[t_a, t_b])
            t_e = emit_rsqrt(agg[:, 1:2], rr[:, 2:3], rr[:, 0:1], rr[:, 3:4],
                             float(EPS), [t_c], iters=2)
            t_f = sch.op("v", lambda v: v.scalar_tensor_tensor(
                out=rr[:, 1:2], in0=agg[:, 0:1], scalar=-1.0, in1=rr[:, 0:1],
                op0=ALU.mult, op1=ALU.mult), deps=[t_e, t_c])
            t_g = sch.op("s", lambda s: s.activation(
                dst_ap, src_ap, AF.Identity, bias=rr[:, 1:2], scale=rr[:, 0:1]),
                deps=[t_f, t_e] + list(deps))
            return t_g, (rr[:, 0:1], rr[:, 1:2]), [t_a, t_b, t_c, t_e, t_f, t_g]

        store_toks = {"x1": [None] * NT, "x2": [None] * NT, "out": [None] * NT}
        b_wbf1 = [Buf() for _ in range(NPE)]
        b_wbf2 = [Buf() for _ in range(NPE)]

        def emit_ffn(src_dram, dst_dram, dst_key, src_toks, first, sparse):
            with ExitStack() as ffn:
                FE = ffn.enter_context
                zacc = FE(nc.sbuf_tensor(nm("zacc"), [128, NT_ST * D], F32))
                cw_all = [FE(nc.sbuf_tensor(nm(f"cw_all{i}"), [128, NT_ST * 16], F32)) for i in range(2)]
                glog_all = cw_all
                w1s = [FE(nc.sbuf_tensor(nm(f"w1s{i}"), [128, KC * 1024], BF16)) for i in range(2)]
                w2s = [FE(nc.sbuf_tensor(nm(f"w2s{i}"), [128, 4 * 1024], BF16)) for i in range(2)]
                xtile = [FE(nc.sbuf_tensor(nm(f"xtile{i}"), [128, D], F32)) for i in range(2)]
                xhat_tm = [FE(nc.sbuf_tensor(nm(f"xhat_tm{i}"), [128, D], BF16)) for i in range(2)]
                if sparse:
                    xg_sb = [FE(nc.sbuf_tensor(nm(f"xg_sb{i}"), [128, D], BF16)) for i in range(3)]
                    xgT = [FE(nc.sbuf_tensor(nm(f"xgT{i}"), [128, KC * 128], BF16)) for i in range(2)]
                    _xfm = FE(nc.sbuf_tensor(nm("xhat_fm"), [128, KC * ST], BF16))
                    xhat_fm = [_xfm, _xfm]
                else:
                    xhat_fm = [FE(nc.sbuf_tensor(nm(f"xhat_fm{i}"), [128, KC * ST], BF16)) for i in range(2)]
                sg = [FE(nc.sbuf_tensor(nm(f"sg{i}"), [128, 512], BF16)) for i in range(2)]
                h_bf = [FE(nc.sbuf_tensor(nm(f"h_bf{i}"), [128, 512], BF16)) for i in range(2)]
                hT_sb = [FE(nc.sbuf_tensor(nm(f"hT_sb{i}"), [128, 512], BF16)) for i in range(3)]
                zg_sb = [FE(nc.sbuf_tensor(nm(f"zg_sb{i}"), [128, D], F32)) for i in range(2)] if sparse else None
                if sparse:
                    base_row = FE(nc.sbuf_tensor(nm("base_row"), [1, 16], F32))
                    maskf = [FE(nc.sbuf_tensor(nm(f"maskf{i}"), [128, 16], F32)) for i in range(2)]
                    mask_bf = [FE(nc.sbuf_tensor(nm(f"mask_bf{i}"), [128, 16], BF16)) for i in range(2)]
                    mask_all = [FE(nc.sbuf_tensor(nm(f"mask_all{i}"), [128, NT_ST * 16], BF16)) for i in range(2)]
                    rsc = FE(nc.sbuf_tensor(nm("rsc"), [128, 6 * 16], F32))
                    dk_all = [FE(nc.sbuf_tensor(nm(f"dk_all{i}"), [128, NT_ST * 2], I32)) for i in range(2)]
                    dkf = FE(nc.sbuf_tensor(nm("dkf"), [128, 2], F32))
                    cwk_all = [FE(nc.sbuf_tensor(nm(f"cwk_all{i}"), [128, NT_ST * 2], F32)) for i in range(2)]
                else:
                    base_row = maskf = mask_bf = mask_all = rsc = dk_all = dkf = cwk_all = None
                _gp = FE(nc.sbuf_tensor(nm("gp16"), [128, 16], F32))
                gp16 = [_gp, _gp]
                _s16 = FE(nc.sbuf_tensor(nm("s16"), [128, 16], F32))
                s16 = [_s16, _s16]
                _m8 = FE(nc.sbuf_tensor(nm("m8"), [128, 8], F32))
                m8 = [_m8, _m8]
                _gsc = FE(nc.sbuf_tensor(nm("gsc"), [128, 4], F32))
                gsc = [_gsc, _gsc]
                xctile_s = FE(nc.sbuf_tensor(nm("xctile"), [128, D], F32))
                xctile = [xctile_s, xctile_s]
                zk_sb = [FE(nc.sbuf_tensor(nm(f"zk_sb{i}"), [128, D], F32)) for i in range(2)] if sparse else None
                xnew_s = FE(nc.sbuf_tensor(nm("xnew"), [128, D], F32))
                xnew = [xnew_s, xnew_s]

                y_ps = [FE(nc.psum_tensor(nm(f"y_ps{i}"), [128, 1024], F32)) for i in range(2)]
                hT_ps = [FE(nc.psum_tensor(nm(f"hT_ps{i}"), [128, 1024], BF16)) for i in range(2)]
                w2o = FE(nc.psum_tensor(nm("w2o"), [128, 1024], F32))
                glog_ps = w2o[:, 0:16]
                pfx_ps = w2o[:, 16:32]
                tot_ps = w2o[0:1, 32:48]
                baseb_ps = w2o[:, 48:64]

                b_xt = [Buf(), Buf()]
                b_xhtm = [Buf(), Buf()]
                b_xg = [Buf(), Buf(), Buf()]
                if sparse:
                    _bxf = [Buf() for _ in range(NT_ST)]
                    b_xfm = [_bxf, _bxf]
                else:
                    b_xfm = [[Buf() for _ in range(NT_ST)] for _ in range(2)]
                b_xgT = [Buf(), Buf()]
                b_cw = [[Buf() for _ in range(NT_ST)] for _ in range(2)]
                b_mall = [[Buf() for _ in range(NT_ST)] for _ in range(2)]
                b_z = [Buf() for _ in range(NT_ST)]
                b_w1 = [Buf(), Buf()]
                b_w2 = [Buf(), Buf()]
                b_yps = [Buf(), Buf()]
                b_hTps = [Buf(), Buf()]
                b_w2o = Buf()
                b_sg = [Buf(), Buf()]
                b_h = [Buf(), Buf()]
                b_hTsb = [Buf(), Buf(), Buf()]
                _bg = Buf()
                b_g16 = [_bg, _bg]
                b_zg = [Buf(), Buf()]
                b_base = Buf()
                b_baser = Buf()
                b_rsc = Buf()
                b_dk = [[Buf() for _ in range(NT_ST)] for _ in range(2)]
                b_cwk = [[Buf() for _ in range(NT_ST)] for _ in range(2)]
                b_xgd = [Buf(), Buf()]
                b_zgd = [Buf(), Buf()]
                scat_last = [[None, None], [None, None]]
                zg_last = [[None, None], [None, None]]
                _bxc = Buf()
                b_xc = [_bxc, _bxc]
                b_zk = [Buf(), Buf()]
                _bxnf = Buf()
                b_xn = [_bxnf, _bxnf]

                def wdma(e, cast):
                    sl = e % 2
                    d1 = b_w1[sl].write_deps()
                    d2 = b_w2[sl].write_deps()
                    if cast:
                        if e < NEXP:
                            tk1 = sch.dma("g", (lambda g, e=e, sl=sl: g.dma_start(
                                w1s[sl][:].rearrange("p (c n) -> p c n", c=KC),
                                w1[e].rearrange("(c p) n -> p c n", p=128))), deps=d1, ch=f"w1{sl}")
                            tk2 = sch.dma("g", (lambda g, e=e, sl=sl: g.dma_start(
                                w2s[sl][:].rearrange("p (c n) -> p c n", c=4),
                                w2[e].rearrange("(c p) n -> p c n", p=128))), deps=d2, ch=f"w2{sl}")
                        else:
                            j = e - NEXP
                            tk1a = sch.dma("g", (lambda g, j=j, sl=sl: g.dma_start(
                                w1s[sl][:].rearrange("p (c n) -> p c n", c=KC)[:, :, 0:512],
                                ws1.rearrange("(c p) n -> p c n", p=128)[:, :, j * 512:(j + 1) * 512])),
                                deps=d1, ch=f"w1{sl}")
                            tk1 = sch.dma("g", (lambda g, j=j, sl=sl: g.dma_start(
                                w1s[sl][:].rearrange("p (c n) -> p c n", c=KC)[:, :, 512:1024],
                                ws1.rearrange("(c p) n -> p c n", p=128)[:, :, 1024 + j * 512:1024 + (j + 1) * 512])),
                                deps=[tk1a], ch=f"w1{sl}")
                            tk2 = sch.dma("g", (lambda g, j=j, sl=sl: g.dma_start(
                                w2s[sl][:].rearrange("p (c n) -> p c n", c=4),
                                ws2[j * 512:(j + 1) * 512, :].rearrange("(c p) n -> p c n", p=128))),
                                deps=d2, ch=f"w2{sl}")
                        ts1 = sch.dma("s", (lambda y, e=e, sl=sl: y.dma_start(
                            wbf1[e], w1s[sl][:])), deps=[tk1], ch=f"wb1{e}")
                        ts2 = sch.dma("s", (lambda y, e=e, sl=sl: y.dma_start(
                            wbf2[e], w2s[sl][:])), deps=[tk2], ch=f"wb2{e}")
                        b_w1[sl].wrote(tk1)
                        b_w1[sl].read(ts1)
                        b_w2[sl].wrote(tk2)
                        b_w2[sl].read(ts2)
                        b_wbf1[e].wrote(ts1)
                        b_wbf2[e].wrote(ts2)
                    else:
                        tk1 = sch.dma("y", (lambda y, e=e, sl=sl: y.dma_start(
                            w1s[sl][:], wbf1[e])), deps=d1 + b_wbf1[e].read_deps(),
                            ch=f"wB1{sl}")
                        tk2 = sch.dma("y", (lambda y, e=e, sl=sl: y.dma_start(
                            w2s[sl][:], wbf2[e])), deps=d2 + b_wbf2[e].read_deps(),
                            ch=f"wB2{sl}")
                        b_wbf1[e].read(tk1)
                        b_wbf2[e].read(tk2)
                        b_w1[sl].wrote(tk1)
                        b_w2[sl].wrote(tk2)

                tln_pend = {}

                def passA_front(st_i, tt):
                    par = tt % 2
                    r0 = st_i * ST + tt * 128
                    dld = sch.dma("y", (lambda y, r0=r0, par=par: y.dma_start(
                        xtile[par][:], src_dram[r0:r0 + 128, :])),
                        deps=b_xt[par].write_deps() + ([src_toks[(r0 // 128)]] if src_toks else []),
                        ch=f"xa{par}")
                    b_xt[par].wrote(dld)
                    tln, _, lntoks = emit_ln(par, xtile[par][:], xhat_tm[par][:],
                                             [dld] + b_xhtm[par].write_deps())
                    for tk in lntoks[:3]:
                        b_xt[par].read(tk)
                    b_xt[par].read(tln)
                    b_xhtm[par].wrote(tln)
                    tln_pend[(st_i, tt)] = tln

                def passA_back(st_i, tt):
                    sp = st_i % 2
                    par = tt % 2
                    tln = tln_pend.pop((st_i, tt))
                    r0 = st_i * ST + tt * 128
                    if sparse and tt == 0 and st_i > 0:
                        trst = sch.op("v", (lambda v: v.memset(base_row[:1, :], 0.0)),
                                      deps=b_baser.read_deps() + b_baser.write_deps())
                        b_baser.wrote(trst)
                    ttr = sch.op("t", (lambda t, par=par: [
                        t.transpose(hT_ps[par][:, c * 128:(c + 1) * 128],
                                    xhat_tm[par][:, c * 128:(c + 1) * 128],
                                    ident[:]) for c in range(KC)][-1]),
                        deps=[tln] + b_hTps[par].write_deps() + tok_const)
                    b_xhtm[par].read(ttr)
                    b_hTps[par].wrote(ttr)
                    tev = sch.op("v", (lambda v, sp=sp, par=par, tt=tt: v.tensor_copy(
                        xhat_fm[sp][:].rearrange("p (c n) -> p c n", c=KC)[:, :, tt * 128:(tt + 1) * 128],
                        hT_ps[par][:].rearrange("p (c n) -> p c n", c=KC))),
                        deps=[ttr] + b_xfm[sp][tt].write_deps())
                    b_hTps[par].read(tev)
                    b_xfm[sp][tt].wrote(tev)
                    gate_lhsT = (lambda c, sp=sp, tt=tt: xhat_fm[sp][:, c * ST + tt * 128:c * ST + (tt + 1) * 128])
                    tgm_deps = [tev]
                    tgm = sch.op("t", (lambda t, gate_lhsT=gate_lhsT: [
                        t.matmul(glog_ps,
                                 gate_lhsT(c),
                                 wgT[:, c * 16:(c + 1) * 16],
                                 start=(c == 0), stop=(c == KC - 1)) for c in range(KC)][-1]),
                        deps=tgm_deps + [tok_wgT] + b_w2o.write_deps())
                    b_xfm[sp][tt].read(tgm)
                    b_w2o.wrote(tgm)
                    tge = sch.op("v", (lambda v, sp=sp, tt=tt: v.tensor_copy(
                        glog_all[sp][:, tt * 16:(tt + 1) * 16], glog_ps)),
                        deps=[tgm] + b_cw[sp][tt].write_deps())
                    b_w2o.read(tge)
                    b_cw[sp][tt].wrote(tge)
                    if not sparse:
                        return
                    tm8 = sch.op("v", (lambda v, par=par, sp=sp, tt=tt: v.max(
                        out=m8[par][:], in_=glog_all[sp][:, tt * 16:(tt + 1) * 16])),
                        deps=[tge] + b_g16[par].write_deps())
                    b_cw[sp][tt].read(tm8)
                    tmk = sch.op("v", (lambda v, par=par, sp=sp, tt=tt: v.tensor_scalar(
                        out=maskf[par][:], in0=glog_all[sp][:, tt * 16:(tt + 1) * 16],
                        scalar1=m8[par][:, 1:2], scalar2=None, op0=ALU.is_ge)),
                        deps=[tm8])
                    b_cw[sp][tt].read(tmk)
                    b_g16[par].wrote(tmk)
                    tmb = sch.op("v", (lambda v, par=par: v.tensor_copy(
                        mask_bf[par][:], maskf[par][:])), deps=[tmk])
                    tma = sch.op("v", (lambda v, par=par, sp=sp, tt=tt: v.tensor_copy(
                        mask_all[sp][:, tt * 16:(tt + 1) * 16], maskf[par][:])),
                        deps=[tmk] + b_mall[sp][tt].write_deps())
                    b_mall[sp][tt].wrote(tma)
                    tpf = sch.op("t", (lambda t, par=par: t.matmul(
                        pfx_ps, ltri_bf[:], mask_bf[par][:], start=True, stop=True)),
                        deps=[tmb, tge] + b_w2o.write_deps() + tok_const)
                    b_w2o.wrote(tpf)
                    ttot = sch.op("t", (lambda t, par=par: t.matmul(
                        tot_ps, ones_col_bf[:, 0:1], mask_bf[par][:], start=True, stop=True)),
                        deps=[tmb, tpf])
                    b_w2o.wrote(ttot)
                    tbb = sch.op("t", (lambda t: t.matmul(
                        baseb_ps, ones_row[:1, :], base_row[:1, :], start=True, stop=True)),
                        deps=[tpf] + b_baser.read_deps() + b_baser.write_deps() + [tok_const[0]])
                    b_baser.read(tbb)
                    b_w2o.wrote(tbb)
                    R = rsc
                    t_s = sch.op("v", (lambda v: v.tensor_copy(R[:, 80:96], baseb_ps)),
                                 deps=[tbb] + b_rsc.write_deps())
                    b_w2o.read(t_s)
                    t_s2 = sch.op("v", (lambda v: v.tensor_add(R[:, 0:16], pfx_ps, R[:, 80:96])),
                                  deps=[t_s, tpf])
                    b_w2o.read(t_s2)
                    t_bu = sch.op("v", (lambda v: v.tensor_add(
                        base_row[:1, :], base_row[:1, :], tot_ps[:1, :])),
                        deps=[ttot, tbb])
                    b_baser.wrote(t_bu)
                    b_w2o.read(t_bu)
                    t_ok = sch.op("v", (lambda v: v.tensor_scalar(
                        out=R[:, 16:32], in0=R[:, 0:16], scalar1=float(CAP),
                        scalar2=None, op0=ALU.is_lt)), deps=[t_s2])
                    t_t1 = sch.op("v", (lambda v: v.tensor_add(R[:, 32:48], R[:, 0:16], ecb[:])),
                                  deps=[t_s2, tok_const[0]])
                    t_c1 = sch.op("v", (lambda v: v.tensor_scalar(
                        out=R[:, 32:48], in0=R[:, 32:48], scalar1=float(ZROW),
                        scalar2=None, op0=ALU.subtract)), deps=[t_t1])
                    t_c2 = sch.op("v", (lambda v: v.tensor_mul(
                        R[:, 32:48], R[:, 32:48], R[:, 16:32])), deps=[t_c1, t_ok])
                    t_c3 = sch.op("v", (lambda v: v.tensor_scalar(
                        out=R[:, 32:48], in0=R[:, 32:48], scalar1=float(ZROW),
                        scalar2=None, op0=ALU.add)), deps=[t_c2])
                    t_m1 = sch.op("v", (lambda v: v.tensor_scalar(
                        out=R[:, 48:64], in0=R[:, 32:48], scalar1=float(ZROW),
                        scalar2=None, op0=ALU.subtract)), deps=[t_c3])
                    t_m2 = sch.op("v", (lambda v, par=par: v.tensor_mul(
                        R[:, 48:64], R[:, 48:64], maskf[par][:])), deps=[t_m1, tmk])
                    t_m3 = sch.op("v", (lambda v: v.tensor_scalar(
                        out=R[:, 48:64], in0=R[:, 48:64], scalar1=float(ZROW),
                        scalar2=None, op0=ALU.add)), deps=[t_m2])
                    t_x1 = sch.op("v", (lambda v: v.tensor_scalar(
                        out=R[:, 64:80], in0=R[:, 32:48], scalar1=1.0,
                        scalar2=None, op0=ALU.add)), deps=[t_c3])
                    t_x2 = sch.op("v", (lambda v, par=par: v.tensor_mul(
                        R[:, 64:80], R[:, 64:80], maskf[par][:])), deps=[t_x1, tmk])
                    t_x3 = sch.op("v", (lambda v: v.tensor_scalar(
                        out=R[:, 64:80], in0=R[:, 64:80], scalar1=1.0,
                        scalar2=None, op0=ALU.subtract)), deps=[t_x2])
                    t_d0 = sch.op("v", (lambda v: v.tensor_reduce(
                        dkf[:, 0:1], R[:, 48:64], axis=AX.X, op=ALU.min)),
                        deps=[t_m3])
                    t_d1 = sch.op("v", (lambda v: v.reduce_max(
                        dkf[:, 1:2], R[:, 64:80], axis=AX.X)), deps=[t_x3])
                    b_rsc.wrote(t_d1)
                    tci = sch.op("v", (lambda v, sp=sp, tt=tt: v.tensor_copy(
                        dk_all[sp][:, 2 * tt:2 * tt + 2], dkf[:, 0:2])),
                        deps=[t_d0, t_d1] + b_dk[sp][tt].write_deps())
                    b_dk[sp][tt].wrote(tci)
                    for k in range(2):
                        tsc_ = sch.dma("g", (lambda g, sp=sp, par=par, tt=tt, k=k: g.indirect_dma_start(
                            out=xg_dram[sp], out_offset=IndirectOffsetOnAxis(
                                dk_all[sp][:, 2 * tt + k:2 * tt + k + 1], 0),
                            in_=xhat_tm[par][:], in_offset=None)),
                            deps=[tci, tln] + (b_xgd[sp].write_deps() if tt == 0 and k == 0 else [])
                                 + ([b_xgd[sp].w] if b_xgd[sp].w else []),
                            ch=f"sx{sp}{par}")
                        b_xhtm[par].read(tsc_)
                        b_dk[sp][tt].read(tsc_)
                        b_xgd[sp].wrote(tsc_)
                        scat_last[sp][par] = tsc_

                def gate_burst(st_i):
                    for tt in range(NT_ST):
                        gate_tile(st_i, tt)

                def gate_tile(st_i, tt):
                    sp = st_i % 2
                    if True:
                        par = tt % 2
                        tmx = sch.op("v", (lambda v, sp=sp, par=par, tt=tt: v.reduce_max(
                            gsc[par][:, 0:1], glog_all[sp][:, tt * 16:(tt + 1) * 16],
                            axis=AX.X, negate=True)),
                            deps=b_cw[sp][tt].read_deps() + b_g16[par].write_deps())
                        b_cw[sp][tt].read(tmx)
                        texp = sch.op("s", (lambda s, sp=sp, par=par, tt=tt: s.activation(
                            gp16[par][:], glog_all[sp][:, tt * 16:(tt + 1) * 16],
                            AF.Exp, bias=gsc[par][:, 0:1],
                            accum_out=gsc[par][:, 1:2])), deps=[tmx])
                        b_cw[sp][tt].read(texp)
                        trc = sch.op("v", (lambda v, par=par: v.reciprocal(
                            gsc[par][:, 0:1], gsc[par][:, 1:2])), deps=[texp])
                        tsc = sch.op("v", (lambda v, par=par: v.tensor_scalar(
                            out=s16[par][:], in0=gp16[par][:], scalar1=gsc[par][:, 0:1],
                            scalar2=None, op0=ALU.mult)), deps=[trc])
                        if not sparse:
                            tdm = sch.op("v", (lambda v, par=par: v.max(
                                out=m8[par][:], in_=s16[par][:])), deps=[tsc])
                            tcw = sch.op("v", (lambda v, sp=sp, par=par, tt=tt: v.scalar_tensor_tensor(
                                out=cw_all[sp][:, tt * 16:(tt + 1) * 16], in0=s16[par][:],
                                scalar=m8[par][:, 1:2], in1=s16[par][:],
                                op0=ALU.is_ge, op1=ALU.mult)),
                                deps=[tdm] + b_cw[sp][tt].write_deps())
                            b_cw[sp][tt].wrote(tcw)
                            b_g16[par].wrote(tcw)
                            return
                        ms = (lambda sp=sp, tt=tt: mask_all[sp][:, tt * 16:(tt + 1) * 16])
                        te0 = sch.op("v", (lambda v, par=par: v.tensor_scalar(
                            out=gp16[par][:], in0=iotae[:], scalar1=16.0,
                            scalar2=None, op0=ALU.subtract)),
                            deps=[tsc, tok_const[0]])
                        te0b = sch.op("v", (lambda v, sp=sp, par=par, tt=tt: v.tensor_mul(
                            gp16[par][:], gp16[par][:],
                            mask_all[sp][:, tt * 16:(tt + 1) * 16])),
                            deps=[te0] + b_mall[sp][tt].read_deps())
                        b_mall[sp][tt].read(te0b)
                        te0c = sch.op("v", (lambda v, par=par: v.tensor_scalar(
                            out=gp16[par][:], in0=gp16[par][:], scalar1=16.0,
                            scalar2=None, op0=ALU.add)), deps=[te0b])
                        te1 = sch.op("v", (lambda v, par=par: v.tensor_reduce(
                            gsc[par][:, 2:3], gp16[par][:], axis=AX.X, op=ALU.min)),
                            deps=[te0c])
                        te2 = sch.op("v", (lambda v, par=par: v.tensor_scalar(
                            out=gp16[par][:], in0=iotae[:], scalar1=1.0,
                            scalar2=None, op0=ALU.add)), deps=[te1])
                        te2b = sch.op("v", (lambda v, sp=sp, par=par, tt=tt: v.tensor_mul(
                            gp16[par][:], gp16[par][:],
                            mask_all[sp][:, tt * 16:(tt + 1) * 16])),
                            deps=[te2])
                        b_mall[sp][tt].read(te2b)
                        te2c = sch.op("v", (lambda v, par=par: v.tensor_scalar(
                            out=gp16[par][:], in0=gp16[par][:], scalar1=1.0,
                            scalar2=None, op0=ALU.subtract)), deps=[te2b])
                        te3 = sch.op("v", (lambda v, par=par: v.reduce_max(
                            gsc[par][:, 3:4], gp16[par][:], axis=AX.X)), deps=[te2c])
                        tq1 = None
                        for k in range(2):
                            tq0 = sch.op("v", (lambda v, par=par, k=k: v.tensor_scalar(
                                out=gp16[par][:], in0=iotae[:],
                                scalar1=gsc[par][:, 2 + k:3 + k], scalar2=None,
                                op0=ALU.is_equal)),
                                deps=[te3] + ([tq1] if tq1 else []))
                            tq1 = sch.op("v", (lambda v, sp=sp, par=par, tt=tt, k=k: v.scalar_tensor_tensor(
                                out=m8[par][:, 0:8].to_broadcast([128, 8]) if False else gp16[par][:],
                                in0=gp16[par][:], scalar=1.0, in1=s16[par][:],
                                op0=ALU.bypass, op1=ALU.mult,
                                accum_out=cwk_all[sp][:, 2 * tt + k:2 * tt + k + 1])),
                                deps=[tq0] + b_cwk[sp][tt].write_deps())
                            b_cwk[sp][tt].wrote(tq1)
                        b_g16[par].wrote(tq1)

                def passC_tile(st_c, tt):
                    sp = st_c % 2
                    row0 = st_c * ST
                    par = tt % 2
                    r0 = row0 + tt * 128
                    gt = r0 // 128
                    dld = sch.dma("y", (lambda y, r0=r0, par=par: y.dma_start(
                        xctile[par][:], src_dram[r0:r0 + 128, :])),
                        deps=b_xc[par].write_deps() + ([src_toks[gt]] if src_toks else []),
                        ch=f"xc{par}")
                    b_xc[par].wrote(dld)
                    if not sparse:
                        tfin = sch.op("v", (lambda v, par=par, tt=tt: v.scalar_tensor_tensor(
                            out=xnew[par][:], in0=zacc[:, tt * D:(tt + 1) * D], scalar=0.5,
                            in1=xctile[par][:], op0=ALU.mult, op1=ALU.add)),
                            deps=[dld] + b_z[tt].read_deps() + b_z[tt].write_deps()
                                 + b_xn[par].write_deps())
                        b_xc[par].read(tfin)
                        b_z[tt].read(tfin)
                        b_xn[par].wrote(tfin)
                        dst = sch.dma("y", (lambda y, r0=r0, par=par: y.dma_start(
                            dst_dram[r0:r0 + 128, :], xnew[par][:])), deps=[tfin],
                            ch=f"so{par}")
                        b_xn[par].read(dst)
                        store_toks[dst_key][gt] = dst
                        return
                    tg0 = sch.dma("g", (lambda g, sp=sp, tt=tt, par=par: g.indirect_dma_start(
                        out=zk_sb[par][:], out_offset=None,
                        in_=zg_dram[sp], in_offset=IndirectOffsetOnAxis(
                            dk_all[sp][:, 2 * tt:2 * tt + 1], 0))),
                        deps=b_zk[par].write_deps()
                             + [t for t in zg_last[sp] if t]
                             + b_dk[sp][tt].read_deps(),
                        ch=f"zk{par}")
                    b_zgd[sp].read(tg0)
                    b_dk[sp][tt].read(tg0)
                    b_zk[par].wrote(tg0)
                    ta0 = sch.op("v", (lambda v, sp=sp, tt=tt, par=par: v.scalar_tensor_tensor(
                        out=xnew[par][:], in0=zk_sb[par][:],
                        scalar=cwk_all[sp][:, 2 * tt:2 * tt + 1],
                        in1=zacc[:, tt * D:(tt + 1) * D], op0=ALU.mult, op1=ALU.add)),
                        deps=[tg0] + b_z[tt].read_deps() + b_z[tt].write_deps()
                             + b_cwk[sp][tt].read_deps() + [b_cwk[sp][tt].w]
                             + b_xn[par].write_deps())
                    b_z[tt].read(ta0)
                    b_cwk[sp][tt].read(ta0)
                    b_zk[par].read(ta0)
                    b_xn[par].wrote(ta0)
                    tg1 = sch.dma("g", (lambda g, sp=sp, tt=tt, par=par: g.indirect_dma_start(
                        out=zk_sb[par][:], out_offset=None,
                        in_=zg_dram[sp], in_offset=IndirectOffsetOnAxis(
                            dk_all[sp][:, 2 * tt + 1:2 * tt + 2], 0))),
                        deps=[ta0] + [t for t in zg_last[sp] if t], ch=f"zk{par}")
                    b_zgd[sp].read(tg1)
                    b_zk[par].wrote(tg1)
                    ta1 = sch.op("v", (lambda v, sp=sp, tt=tt, par=par: v.scalar_tensor_tensor(
                        out=xnew[par][:], in0=zk_sb[par][:],
                        scalar=cwk_all[sp][:, 2 * tt + 1:2 * tt + 2],
                        in1=xnew[par][:], op0=ALU.mult, op1=ALU.add)),
                        deps=[tg1, ta0])
                    b_zk[par].read(ta1)
                    b_xn[par].wrote(ta1)
                    tfin = sch.op("v", (lambda v, par=par: v.scalar_tensor_tensor(
                        out=xnew[par][:], in0=xnew[par][:], scalar=0.5,
                        in1=xctile[par][:], op0=ALU.mult, op1=ALU.add)),
                        deps=[ta1, dld])
                    b_xc[par].read(tfin)
                    b_xn[par].wrote(tfin)
                    dst = sch.dma("y", (lambda y, r0=r0, par=par: y.dma_start(
                        dst_dram[r0:r0 + 128, :], xnew[par][:])), deps=[tfin],
                        ch=f"so{par}")
                    b_xn[par].read(dst)
                    store_toks[dst_key][gt] = dst

                if sparse:
                    tbi = sch.op("v", lambda v: v.memset(base_row[:1, :], 0.0))
                    b_baser.wrote(tbi)
                    passA_front(0, 0)
                    passA_front(0, 1)
                    passA_back(0, 0)
                    passA_front(0, 2)
                    passA_back(0, 1)
                    passA_front(0, 3)
                else:
                    for tt in range(NT_ST):
                        passA_front(0, tt)
                        passA_back(0, tt)
                    gate_burst(0)

                for st_i in range(NST):
                    row0 = st_i * ST
                    sp = st_i % 2
                    if st_i == 0:
                        wdma(16 if sparse else 0, cast=first)
                        wdma(17 if sparse else 1, cast=first)

                    if sparse:
                        steps = ([(pe, tt) for pe in (16, 17) for tt in range(NT_ST)]
                                 + [(e, sl) for e in range(NEXP) for sl in range(CAPT)])
                    else:
                        steps = [(e, tt) for e in range(NPE) for tt in range(NT_ST)]
                    nsteps = len(steps)
                    tok_ld = {}
                    tok_trx = {}
                    tok_w1mm = {}
                    tok_hstt = {}
                    tok_hev = {}
                    tok_w2mm = {}

                    def src_rows(i):
                        e, sl = steps[i]
                        return xg_dram[sp][e * CAP + sl * 128: e * CAP + (sl + 1) * 128, :]

                    def emit_ldx(i):
                        e, sl = steps[i]
                        p = i % 3
                        deps = b_xg[p].write_deps() + [t for t in scat_last[sp] if t]
                        rows_ap = src_rows(i)
                        tk = sch.dma("y", (lambda y, p=p, rows_ap=rows_ap: y.dma_start(
                            xg_sb[p][:], rows_ap)), deps=deps, ch=f"xg{p}")
                        b_xg[p].wrote(tk)
                        b_xgd[sp].read(tk)
                        tok_ld[i] = tk

                    def emit_trx(i):
                        e, sl = steps[i]
                        p3 = i % 3
                        p = i % 2
                        tk = sch.op("t", (lambda t, p=p, p3=p3: [
                            t.transpose(hT_ps[p][:, c * 128:(c + 1) * 128],
                                        xg_sb[p3][:, c * 128:(c + 1) * 128],
                                        ident[:]) for c in range(KC)][-1]),
                            deps=[tok_ld[i]] + b_hTps[p].write_deps() + tok_const)
                        b_xg[p3].read(tk)
                        b_hTps[p].wrote(tk)
                        tev = sch.op("v", (lambda v, p=p: v.tensor_copy(
                            xgT[p][:], hT_ps[p][:])),
                            deps=[tk] + b_xgT[p].write_deps())
                        b_hTps[p].read(tev)
                        b_xgT[p].wrote(tev)
                        tok_trx[i] = tev

                    def emit_w1mm(i):
                        e, sl = steps[i]
                        p = i % 2
                        sl_w = e % 2
                        if sparse and e < NEXP:
                            deps = (b_yps[p].write_deps() + b_w1[sl_w].read_deps() + [tok_trx[i]])
                            tk = sch.op("t", (lambda t, p=p, sl_w=sl_w: [
                                t.matmul(y_ps[p][:, n * 512:(n + 1) * 512],
                                         xgT[p][:, k * 128:(k + 1) * 128],
                                         w1s[sl_w][:, k * 1024 + n * 512:k * 1024 + (n + 1) * 512],
                                         start=(k == 0), stop=(k == KC - 1))
                                for k in range(KC) for n in range(2)][-1]), deps=deps)
                            b_xgT[p].read(tk)
                        else:
                            tt = sl
                            deps = (b_yps[p].write_deps() + b_w1[sl_w].read_deps()
                                    + b_xfm[sp][tt].read_deps())
                            tk = sch.op("t", (lambda t, p=p, sl_w=sl_w, sp=sp, tt=tt: [
                                t.matmul(y_ps[p][:, n * 512:(n + 1) * 512],
                                         xhat_fm[sp][:, k * ST + tt * 128:k * ST + (tt + 1) * 128],
                                         w1s[sl_w][:, k * 1024 + n * 512:k * 1024 + (n + 1) * 512],
                                         start=(k == 0), stop=(k == KC - 1))
                                for k in range(KC) for n in range(2)][-1]), deps=deps)
                            b_xfm[sp][tt].read(tk)
                        tok_w1mm[i] = tk
                        b_yps[p].wrote(tk)
                        b_w1[sl_w].read(tk)

                    def emit_gating(i):
                        e, sl = steps[i]
                        p = i % 2
                        tsil = sch.op("s", (lambda s, p=p: s.activation(
                            sg[p][:], y_ps[p][:, 512:1024], AF.Silu)),
                            deps=[tok_w1mm[i]] + b_sg[p].write_deps())
                        b_yps[p].read(tsil)
                        b_sg[p].wrote(tsil)
                        use_cw = (not sparse) and e < NEXP
                        th = sch.op("v", (lambda v, p=p, sp=sp, e=e, sl=sl, use_cw=use_cw: v.scalar_tensor_tensor(
                            out=h_bf[p][:], in0=y_ps[p][:, 0:512],
                            scalar=(cw_all[sp][:, sl * 16 + e:sl * 16 + e + 1] if use_cw else 1.0),
                            in1=sg[p][:], op0=ALU.mult, op1=ALU.mult)),
                            deps=[tsil, tok_w1mm[i]] + b_h[p].write_deps()
                                 + (b_cw[sp][sl].read_deps() + [b_cw[sp][sl].w] if use_cw else []))
                        b_yps[p].read(th)
                        b_sg[p].read(th)
                        b_h[p].wrote(th)
                        if use_cw:
                            b_cw[sp][sl].read(th)
                        tok_hstt[i] = th

                    def emit_tr(i):
                        e, sl = steps[i]
                        p = i % 2
                        p3 = i % 3
                        tk = sch.op("t", (lambda t, p=p: [
                            t.transpose(hT_ps[p][:, j * 128:(j + 1) * 128],
                                        h_bf[p][:, j * 128:(j + 1) * 128], ident[:])
                            for j in range(4)][-1]),
                            deps=[tok_hstt[i]] + b_hTps[p].write_deps())
                        b_h[p].read(tk)
                        b_hTps[p].wrote(tk)
                        tev = sch.op("v", (lambda v, p=p, p3=p3: v.tensor_copy(
                            hT_sb[p3][:], hT_ps[p][:, 0:512])),
                            deps=[tk] + b_hTsb[p3].write_deps())
                        b_hTps[p].read(tev)
                        b_hTsb[p3].wrote(tev)
                        tok_hev[i] = tev

                    def emit_w2mm(i):
                        e, sl = steps[i]
                        p3 = i % 3
                        sl_w = e % 2
                        deps = ([tok_hev[i]] + b_w2o.write_deps() + b_w2[sl_w].read_deps())
                        tk = sch.op("t", (lambda t, p3=p3, sl_w=sl_w: [
                            t.matmul(w2o[:, n * 512:(n + 1) * 512],
                                     hT_sb[p3][:, k * 128:(k + 1) * 128],
                                     w2s[sl_w][:, k * 1024 + n * 512:k * 1024 + (n + 1) * 512],
                                     start=(k == 0), stop=(k == 3))
                            for k in range(4) for n in range(2)][-1]), deps=deps)
                        tok_w2mm[i] = tk
                        b_hTsb[p3].read(tk)
                        b_w2[sl_w].read(tk)
                        b_w2o.wrote(tk)

                    def emit_sink(i):
                        e, sl = steps[i]
                        p = i % 2
                        if sparse and e < NEXP:
                            tk = sch.op("v", (lambda v, p=p: v.tensor_copy(
                                zg_sb[p][:], w2o[:])),
                                deps=[tok_w2mm[i]] + b_zg[p].write_deps())
                            b_w2o.read(tk)
                            b_zg[p].wrote(tk)
                            td = sch.dma("s", (lambda y, p=p, e=e, sl=sl, sp=sp: y.dma_start(
                                zg_dram[sp][e * CAP + sl * 128:e * CAP + (sl + 1) * 128, :],
                                zg_sb[p][:])),
                                deps=[tk] + (b_zgd[sp].write_deps() if (e == 0 and sl == 0) else []),
                                ch=f"zg{sp}{p}")
                            b_zg[p].read(td)
                            b_zgd[sp].wrote(td)
                            zg_last[sp][p] = td
                        else:
                            tt = sl
                            zslv = zacc[:, tt * D:(tt + 1) * D]
                            if e == (16 if sparse else 0):
                                tk = sch.op("v", (lambda v, zslv=zslv: v.tensor_copy(zslv, w2o[:])),
                                            deps=[tok_w2mm[i]] + b_z[tt].write_deps())
                            else:
                                tk = sch.op("v", (lambda v, zslv=zslv: v.tensor_add(zslv, zslv, w2o[:])),
                                            deps=[tok_w2mm[i]] + b_z[tt].read_deps()
                                                 + b_z[tt].write_deps())
                            b_z[tt].wrote(tk)
                            b_w2o.read(tk)

                    fr_next = [0]
                    bk_next = [0]
                    cur_fr = [4]
                    cur_bk = [2]
                    for i in range(nsteps):
                        e, sl = steps[i]
                        if st_i > 0 and i < NT_ST:
                            passC_tile(st_i - 1, i)
                        if sparse and st_i == 0:
                            if cur_bk[0] < NT_ST:
                                passA_back(0, cur_bk[0])
                                cur_bk[0] += 1
                            if cur_fr[0] < NT_ST:
                                passA_front(0, cur_fr[0])
                                cur_fr[0] += 1
                        if sparse and 18 <= i < 18 + NT_ST:
                            gate_tile(st_i, i - 18)
                        if (not sparse) and st_i + 1 < NST and 250 <= i < 250 + NT_ST:
                            gate_tile(st_i + 1, i - 250)
                        if st_i + 1 < NST and i >= (16 if sparse else 2):
                            if i % 3 == 0 and fr_next[0] < NT_ST:
                                passA_front(st_i + 1, fr_next[0])
                                fr_next[0] += 1
                            if (i % 3 == 2 and bk_next[0] < NT_ST
                                    and (bk_next[0] <= fr_next[0] - 2
                                         or fr_next[0] == NT_ST)):
                                passA_back(st_i + 1, bk_next[0])
                                bk_next[0] += 1
                        if sparse and i + 2 < nsteps and steps[i + 2][0] < NEXP:
                            emit_ldx(i + 2)
                        emit_w1mm(i)
                        if sparse and i + 1 < nsteps and steps[i + 1][0] < NEXP:
                            emit_trx(i + 1)
                        if i >= 1:
                            emit_tr(i - 1)
                        if i >= 2:
                            emit_w2mm(i - 2)
                            emit_sink(i - 2)
                        emit_gating(i)
                        if sl == 1:
                            if sparse:
                                if e == 17:
                                    wdma(0, cast=(first and st_i == 0))
                                elif e < 15:
                                    wdma(e + 1, cast=(first and st_i == 0))
                            elif e >= 1 and e + 1 < NPE:
                                wdma(e + 1, cast=(first and st_i == 0))
                    emit_tr(nsteps - 1)
                    for i in (nsteps - 2, nsteps - 1):
                        emit_w2mm(i)
                        emit_sink(i)
                    if st_i + 1 < NST:
                        while fr_next[0] < NT_ST:
                            passA_front(st_i + 1, fr_next[0])
                            fr_next[0] += 1
                        while bk_next[0] < NT_ST:
                            passA_back(st_i + 1, bk_next[0])
                            bk_next[0] += 1
                        wdma(16 if sparse else 0, cast=False)
                        wdma(17 if sparse else 1, cast=False)

                for tt in range(NT_ST):
                    passC_tile(NST - 1, tt)
                sch.barrier()

        def emit_attn(src_dram, dst_dram, dst_key, src_toks):
            with ExitStack() as att:
                AE = att.enter_context
                wq_sb = AE(nc.sbuf_tensor(nm("wq_sb"), [128, KC * 1024], BF16))
                wk_sb = AE(nc.sbuf_tensor(nm("wk_sb"), [128, KC * 256], BF16))
                wv_sb = AE(nc.sbuf_tensor(nm("wv_sb"), [128, KC * 256], BF16))
                wo_sb = AE(nc.sbuf_tensor(nm("wo_sb"), [128, KC * 1024], BF16))
                bqs_b = AE(nc.sbuf_tensor(nm("bqs_b"), [128, D], F32))
                bk_b = AE(nc.sbuf_tensor(nm("bk_b"), [128, 256], F32))
                bv_b = AE(nc.sbuf_tensor(nm("bv_b"), [128, 256], F32))
                bo_b = AE(nc.sbuf_tensor(nm("bo_b"), [128, D], F32))
                g_b = AE(nc.sbuf_tensor(nm("g_b"), [128, D], F32))
                b_b = AE(nc.sbuf_tensor(nm("b_b"), [128, D], F32))
                q_fm = AE(nc.sbuf_tensor(nm("q_fm"), [128, 16 * S], BF16))
                k_fm = AE(nc.sbuf_tensor(nm("k_fm"), [128, 4 * S], BF16))
                v_sb = AE(nc.sbuf_tensor(nm("v_sb"), [128, QT * 260], BF16))
                xtile = [AE(nc.sbuf_tensor(nm(f"axt{i}"), [128, D], F32)) for i in range(2)]
                xhat_tm = [AE(nc.sbuf_tensor(nm(f"axh{i}"), [128, D], BF16)) for i in range(2)]
                xaT = [AE(nc.sbuf_tensor(nm(f"xaT{i}"), [128, KC * 128], BF16)) for i in range(2)]
                q_tm = [AE(nc.sbuf_tensor(nm(f"q_tm{i}"), [128, D], BF16)) for i in range(2)]
                k_tm = [AE(nc.sbuf_tensor(nm(f"k_tm{i}"), [128, 256], BF16)) for i in range(2)]
                qn2_all = AE(nc.sbuf_tensor(nm("qn2_all"), [128, QT], F32))
                kn2_all = AE(nc.sbuf_tensor(nm("kn2_all"), [128, QT * 4], F32))
                qn8_all = AE(nc.sbuf_tensor(nm("qn8_all"), [128, QT], F32))
                qn8row = AE(nc.sbuf_tensor(nm("qn8row"), [1, S], BF16))
                knsc = AE(nc.sbuf_tensor(nm("knsc"), [128, 4], F32))
                pT_sb = [AE(nc.sbuf_tensor(nm(f"pT_sb{i}"), [128, S], BF16)) for i in range(3)]
                rinv = AE(nc.sbuf_tensor(nm("rinv"), [128, 16], F32))
                ao_sb = AE(nc.sbuf_tensor(nm("ao_sb"), [128, D], F32))
                t1_sb = AE(nc.sbuf_tensor(nm("t1_sb"), [128, D], F32))
                qsq = t1_sb
                t_bf = AE(nc.sbuf_tensor(nm("t_bf"), [128, D], BF16))
                tT_sb = AE(nc.sbuf_tensor(nm("tT_sb"), [128, D], BF16))
                xres = xtile
                xn1 = ao_sb
                xnew_s = AE(nc.sbuf_tensor(nm("axn"), [128, D], F32))
                xnew = [xnew_s, xnew_s]

                with nc.psum_tensor(nm("bc_ps"), [128, 512], F32) as bc_ps, \
                     nc.sbuf_tensor(nm("brow"), [1, D], F32) as brow:
                    b_rows2, b_bcps = Buf(), Buf()

                    def bcast2(dst, row_dram, n, scale=None):
                        dld = sch.dma("g", (lambda g, n=n, row_dram=row_dram: g.dma_start(
                            brow[:1, 0:n], row_dram)), deps=b_rows2.write_deps(), ch="drw")
                        b_rows2.wrote(dld)
                        if scale is not None:
                            dsc = sch.op("s", (lambda s, n=n, scale=scale: s.activation(
                                brow[:1, 0:n], brow[:1, 0:n], AF.Identity,
                                bias=f32(0.0), scale=f32(scale))), deps=[dld])
                            b_rows2.wrote(dsc)
                            src_tok = dsc
                        else:
                            src_tok = dld
                        last = None
                        for j in range((n + 511) // 512):
                            w = min(512, n - j * 512)
                            tmm = sch.op("t", (lambda t, j=j, w=w: t.matmul(
                                bc_ps[:, 0:w], ones_row[:1, :],
                                brow[:1, j * 512:j * 512 + w], start=True, stop=True)),
                                deps=[src_tok, tok_const[0]] + b_bcps.write_deps())
                            b_rows2.read(tmm)
                            b_bcps.wrote(tmm)
                            tev = sch.op("v", (lambda v, j=j, w=w, dst=dst: v.tensor_copy(
                                dst[:, j * 512:j * 512 + w], bc_ps[:, 0:w])), deps=[tmm])
                            b_bcps.read(tev)
                            last = tev
                        return last

                    bcast2(bqs_b, bq, D, scale=0.125)
                    bcast2(bk_b, bk, 256)
                    bcast2(bv_b, bv, 256)
                    bcast2(bo_b, bo, D)
                    bcast2(g_b, ln_g, D)
                    tok_bc = bcast2(b_b, ln_b, D)

                b_wq, b_wk, b_wv, b_wo = Buf(), Buf(), Buf(), Buf()
                sch.dma("g", lambda g: g.dma_start(
                    wq_sb[:].rearrange("p (c n) -> p c n", c=KC),
                    wq.rearrange("(c p) n -> p c n", p=128)), ch="wat")
                sch.dma("g", lambda g: g.dma_start(
                    wk_sb[:].rearrange("p (c n) -> p c n", c=KC),
                    wk.rearrange("(c p) n -> p c n", p=128)), ch="wat")
                sch.dma("g", lambda g: g.dma_start(
                    wv_sb[:].rearrange("p (c n) -> p c n", c=KC),
                    wv.rearrange("(c p) n -> p c n", p=128)), ch="wat")
                dw = sch.dma("g", lambda g: g.dma_start(
                    wo_sb[:].rearrange("p (c n) -> p c n", c=KC),
                    wo.rearrange("(c p) n -> p c n", p=128)), ch="wat")
                b_wq.wrote(dw); b_wk.wrote(dw); b_wv.wrote(dw); b_wo.wrote(dw)

                b_vsb = Buf()
                b_qfm = Buf()
                b_kfm = Buf()
                gctr = [0]

                for seq in range(SEQ_PER_CORE):
                    row0 = seq * S
                    b_xt = [Buf(), Buf()]
                    b_xhtm = [Buf(), Buf()]
                    b_xaT = [Buf(), Buf()]
                    b_qtm = [Buf(), Buf()]
                    b_ktm = [Buf(), Buf()]
                    b_qsq = Buf()
                    tok_ke = [None] * QT
                    tok_ve = [None] * QT
                    tok_qe = [None] * QT
                    with nc.psum_tensor(nm("xT_ps0"), [128, 1024], BF16) as xT_ps0, \
                         nc.psum_tensor(nm("xT_ps1"), [128, 1024], BF16) as xT_ps1, \
                         nc.psum_tensor(nm("q_ps"), [128, 1024], F32) as q_ps, \
                         nc.psum_tensor(nm("qT_ps"), [128, 1024], BF16) as qT_ps, \
                         nc.psum_tensor(nm("kv_ps"), [128, 512], F32) as kv_ps, \
                         nc.psum_tensor(nm("kT_ps"), [128, 256], BF16) as kT_ps, \
                         nc.psum_tensor(nm("sp_ps"), [128, 128], F32) as sp_ps:
                        xT_ps = [xT_ps0, xT_ps1]
                        b_xTps = [Buf(), Buf()]
                        b_qps, b_qTps, b_kvps, b_kTps, b_spps = \
                            Buf(), Buf(), Buf(), Buf(), Buf()
                        tko = sch.op("g", (lambda g: g.memset(
                            v_sb[:].rearrange("p (t h c) -> p t h c", h=4, c=65)[:, :, :, 64:65],
                            1.0)), deps=b_vsb.write_deps())
                        b_vsb.wrote(tko)
                        tkon = sch.op("g", (lambda g: g.memset(k_fm[64:65, :], 1.0)),
                                      deps=b_kfm.write_deps())
                        b_kfm.wrote(tkon)
                        tok_ln = [None] * QT

                        def p1_ln(tt):
                            par = tt % 2
                            r0 = row0 + tt * 128
                            gt = r0 // 128
                            dld = sch.dma("y", (lambda y, r0=r0, par=par: y.dma_start(
                                xtile[par][:], src_dram[r0:r0 + 128, :])),
                                deps=b_xt[par].write_deps() + ([src_toks[gt]] if src_toks else []),
                                ch=f"xa{par}")
                            b_xt[par].wrote(dld)
                            tln, _, lntoks = emit_ln(par, xtile[par][:], xhat_tm[par][:],
                                                     [dld] + b_xhtm[par].write_deps())
                            for tk in lntoks[:3]:
                                b_xt[par].read(tk)
                            b_xt[par].read(tln)
                            b_xhtm[par].wrote(tln)
                            tok_ln[tt] = tln

                        tok_ev = [None] * QT

                        def p1_stage1(tt):
                            par = tt % 2
                            tln = tok_ln[tt]
                            ttr = sch.op("t", (lambda t, par=par: [
                                t.transpose(xT_ps[par][:, c * 128:(c + 1) * 128],
                                            xhat_tm[par][:, c * 128:(c + 1) * 128],
                                            ident[:]) for c in range(KC)][-1]),
                                deps=[tln] + b_xTps[par].write_deps() + tok_const)
                            b_xhtm[par].read(ttr)
                            b_xTps[par].wrote(ttr)
                            tev = sch.op("v", (lambda v, par=par: v.tensor_copy(
                                xaT[par][:], xT_ps[par][:])),
                                deps=[ttr] + b_xaT[par].write_deps())
                            b_xTps[par].read(tev)
                            b_xaT[par].wrote(tev)
                            tok_ev[tt] = tev

                        p1_ln(0)
                        p1_ln(1)
                        p1_stage1(0)
                        for tt in range(QT):
                            par = tt % 2
                            r0 = row0 + tt * 128
                            gt = r0 // 128
                            if tt + 2 < QT:
                                p1_ln(tt + 2)
                            if tt + 1 < QT:
                                p1_stage1(tt + 1)
                            tev = tok_ev[tt]
                            tqm = sch.op("t", (lambda t, par=par: [
                                t.matmul(q_ps[:, n * 512:(n + 1) * 512],
                                         xaT[par][:, k * 128:(k + 1) * 128],
                                         wq_sb[:, k * 1024 + n * 512:k * 1024 + (n + 1) * 512],
                                         start=(k == 0), stop=(k == KC - 1))
                                for k in range(KC) for n in range(2)][-1]),
                                deps=[tev] + b_qps.write_deps() + b_wq.read_deps())
                            b_xaT[par].read(tqm)
                            b_wq.read(tqm)
                            b_qps.wrote(tqm)
                            tqs = sch.op("v", (lambda v, par=par: v.scalar_tensor_tensor(
                                out=q_tm[par][:], in0=q_ps[:], scalar=0.125,
                                in1=bqs_b[:], op0=ALU.mult, op1=ALU.add)),
                                deps=[tqm, tok_bc] + b_qtm[par].write_deps())
                            b_qps.read(tqs)
                            b_qtm[par].wrote(tqs)
                            tqn = sch.op("v", (lambda v, par=par, tt=tt: v.scalar_tensor_tensor(
                                out=qsq[:], in0=q_tm[par][:], scalar=1.0, in1=q_tm[par][:],
                                op0=ALU.bypass, op1=ALU.mult,
                                accum_out=qn2_all[:, tt:tt + 1])),
                                deps=[tqs] + b_qsq.write_deps())
                            b_qtm[par].read(tqn)
                            b_qsq.wrote(tqn)
                            tqt = sch.op("t", (lambda t, par=par: [
                                t.transpose(qT_ps[64 * (hq % 2):64 * (hq % 2) + 64,
                                                  (hq // 2) * 128:(hq // 2) * 128 + 128],
                                            q_tm[par][:, hq * 64:(hq + 1) * 64],
                                            ident[:]) for hq in range(16)][-1]),
                                deps=[tqs] + b_qTps.write_deps())
                            b_qtm[par].read(tqt)
                            b_qTps.wrote(tqt)
                            tqe = sch.op("v", (lambda v, tt=tt: [
                                v.tensor_copy(
                                    q_fm[0:64, :].rearrange("p (h n) -> p h n", n=S)[:, 0::2, tt * 128:(tt + 1) * 128],
                                    qT_ps[0:64, :].rearrange("p (h n) -> p h n", n=128)),
                                v.tensor_copy(
                                    q_fm[0:64, :].rearrange("p (h n) -> p h n", n=S)[:, 1::2, tt * 128:(tt + 1) * 128],
                                    qT_ps[64:128, :].rearrange("p (h n) -> p h n", n=128))][-1]),
                                deps=[tqt] + b_qfm.write_deps())
                            b_qTps.read(tqe)
                            tok_qe[tt] = tqe
                            tkm = sch.op("t", (lambda t, par=par: [
                                t.matmul(kv_ps[:, 0:256],
                                         xaT[par][:, k * 128:(k + 1) * 128],
                                         wk_sb[:, k * 256:(k + 1) * 256],
                                         start=(k == 0), stop=(k == KC - 1))
                                for k in range(KC)][-1]),
                                deps=[tev] + b_kvps.write_deps() + b_wk.read_deps())
                            b_xaT[par].read(tkm)
                            b_wk.read(tkm)
                            b_kvps.wrote(tkm)
                            tks = sch.op("v", (lambda v, par=par: v.scalar_tensor_tensor(
                                out=k_tm[par][:], in0=kv_ps[:, 0:256], scalar=1.0,
                                in1=bk_b[:], op0=ALU.mult, op1=ALU.add)),
                                deps=[tkm] + b_ktm[par].write_deps())
                            b_kvps.read(tks)
                            b_ktm[par].wrote(tks)
                            tkn = None
                            for h in range(4):
                                tkn = sch.op("v", (lambda v, par=par, tt=tt, h=h: v.scalar_tensor_tensor(
                                    out=qsq[:, h * 64:(h + 1) * 64],
                                    in0=k_tm[par][:, h * 64:(h + 1) * 64], scalar=1.0,
                                    in1=k_tm[par][:, h * 64:(h + 1) * 64],
                                    op0=ALU.bypass, op1=ALU.mult,
                                    accum_out=kn2_all[:, tt * 4 + h:tt * 4 + h + 1])),
                                    deps=[tks] + ([] if h else b_qsq.write_deps()))
                                b_ktm[par].read(tkn)
                            b_qsq.wrote(tkn)
                            tkt = sch.op("t", (lambda t, par=par: [
                                t.transpose(kT_ps[64 * (h % 2):64 * (h % 2) + 64,
                                                  (h // 2) * 128:(h // 2) * 128 + 128],
                                            k_tm[par][:, h * 64:(h + 1) * 64],
                                            ident[:]) for h in range(4)][-1]),
                                deps=[tks] + b_kTps.write_deps())
                            b_ktm[par].read(tkt)
                            b_kTps.wrote(tkt)
                            tke = sch.op("v", (lambda v, tt=tt: [
                                v.tensor_copy(
                                    k_fm[0:64, :].rearrange("p (h n) -> p h n", n=S)[:, 0::2, tt * 128:(tt + 1) * 128],
                                    kT_ps[0:64, 0:256].rearrange("p (h n) -> p h n", n=128)),
                                v.tensor_copy(
                                    k_fm[0:64, :].rearrange("p (h n) -> p h n", n=S)[:, 1::2, tt * 128:(tt + 1) * 128],
                                    kT_ps[64:128, 0:256].rearrange("p (h n) -> p h n", n=128))][-1]),
                                deps=[tkt] + b_kfm.write_deps())
                            b_kTps.read(tke)
                            tok_ke[tt] = tke
                            tvm = sch.op("t", (lambda t, par=par: [
                                t.matmul(kv_ps[:, 256:512],
                                         xaT[par][:, k * 128:(k + 1) * 128],
                                         wv_sb[:, k * 256:(k + 1) * 256],
                                         start=(k == 0), stop=(k == KC - 1))
                                for k in range(KC)][-1]),
                                deps=[tev, tks] + b_wv.read_deps())
                            b_xaT[par].read(tvm)
                            b_wv.read(tvm)
                            tvs = sch.op("v", (lambda v, tt=tt: v.scalar_tensor_tensor(
                                out=v_sb[:, tt * 260:(tt + 1) * 260].rearrange(
                                    "p (h c) -> p h c", c=65)[:, :, 0:64],
                                in0=kv_ps[:, 256:512].rearrange("p (h c) -> p h c", c=64),
                                scalar=1.0,
                                in1=bv_b[:].rearrange("p (h c) -> p h c", c=64),
                                op0=ALU.mult, op1=ALU.add)),
                                deps=[tvm, tko])
                            b_kvps.read(tvs)
                            b_kvps.wrote(tvm)
                            b_vsb.read(tvs)
                            tok_ve[tt] = tvs

                        tl1 = sch.op("s", (lambda s: s.activation(
                            qn8_all[:], qn2_all[:], AF.Ln)),
                            deps=[b_qsq.w, tok_qe[QT - 1]] + [t for t in [b_qsq.w] if t])
                        tl2 = sch.op("s", (lambda s: s.activation(
                            qn8_all[:], qn8_all[:], AF.Exp, bias=f32(-2.0794415416798357),
                            scale=f32(0.5))), deps=[tl1])
                        tok_qn8r = None
                        for tt in range(QT):
                            ttr2 = sch.op("t", (lambda t, tt=tt: t.transpose(
                                sp_ps[0:1, 0:128], qn8_all[:, tt:tt + 1], ident_f[:])),
                                deps=[tl2] + b_spps.write_deps() + tok_const)
                            b_spps.wrote(ttr2)
                            tev2 = sch.op("v", (lambda v, tt=tt: v.tensor_copy(
                                qn8row[0:1, tt * 128:(tt + 1) * 128], sp_ps[0:1, 0:128])),
                                deps=[ttr2])
                            b_spps.read(tev2)
                            tok_qn8r = tev2
                        tk1 = sch.op("v", (lambda v: v.reduce_max(
                            knsc[:, 0:1], kn2_all[:], axis=AX.X)), deps=[b_qsq.w])
                        tk2 = sch.op("t", (lambda t: t.transpose(
                            sp_ps[0:1, 0:128], knsc[:, 0:1], ident_f[:])),
                            deps=[tk1] + b_spps.write_deps())
                        b_spps.wrote(tk2)
                        tk4 = sch.op("v", (lambda v: v.reduce_max(
                            knsc[0:1, 1:2], sp_ps[0:1, 0:128], axis=AX.X)),
                            deps=[tk2])
                        b_spps.read(tk4)
                        tk5 = sch.op("s", (lambda s: s.activation(
                            knsc[0:1, 2:3], knsc[0:1, 1:2], AF.Ln)), deps=[tk4])
                        tk6 = sch.op("s", (lambda s: s.activation(
                            knsc[0:1, 2:3], knsc[0:1, 2:3], AF.Exp, bias=f32(0.0),
                            scale=f32(0.5))), deps=[tk5])
                        tk7 = sch.op("v", (lambda v: v.tensor_scalar_mul(
                            knsc[0:1, 3:4], knsc[0:1, 2:3], -1.0)), deps=[tk6])
                        tok_aug = None
                        for hq in range(16):
                            tok_aug = sch.op("v", (lambda v, hq=hq: v.tensor_scalar(
                                out=q_fm[64:65, hq * S:(hq + 1) * S], in0=qn8row[0:1, :],
                                scalar1=knsc[0:1, 3:4], scalar2=None, op0=ALU.mult)),
                                deps=[tk7, tok_qn8r] + (b_qfm.write_deps() if hq == 0 else []))
                        b_qfm.wrote(tok_aug)
                        for tt in range(QT):
                            b_qfm.read(tok_qe[tt])
                        sch.barrier()

                    with nc.psum_tensor(nm("sT_ps"), [128, 2048], F32) as sT_ps, \
                         nc.psum_tensor(nm("ao_ps"), [128, 3 * 512], F32) as ao_ps, \
                         nc.psum_tensor(nm("tT_ps"), [128, 1024], BF16) as tT_ps:
                        NSTB = 4
                        sT = [sT_ps[:, i * 512:(i + 1) * 512] for i in range(NSTB)]
                        wo_ps = ao_ps[:, 0:1024]
                        b_sT = [Buf() for _ in range(NSTB)]
                        b_aops, b_tTps = Buf(), Buf()
                        b_wops = b_aops
                        b_pT = [Buf(), Buf(), Buf()]
                        b_rinv, b_ao = Buf(), Buf()
                        b_t1, b_t2, b_tbf, b_tT = Buf(), Buf(), Buf(), Buf()
                        b_xres = [Buf(), Buf()]
                        b_xn1 = Buf()
                        _bxn = Buf()
                        b_xn = [_bxn, _bxn]

                        def _ao_data_view():
                            return ao_ps[:].rearrange(
                                "p (b c) -> p b c", c=512).rearrange(
                                "p b (h c) -> p b h c", c=65)[:, :, :, 0:64].rearrange(
                                "p b h c -> p (b h) c")[:, 0:16, :] if False else None

                        p3b_pend = [None]

                        def emit_p3b():
                            if p3b_pend[0] is None:
                                return
                            tq, r0, gt, tb, drd = p3b_pend[0]
                            p3b_pend[0] = None
                            ttt = sch.op("t", (lambda t: [
                                t.transpose(tT_ps[:, c * 128:(c + 1) * 128],
                                            t_bf[:, c * 128:(c + 1) * 128], ident[:])
                                for c in range(KC)][-1]),
                                deps=[tb] + b_tTps.write_deps())
                            b_tbf.read(ttt)
                            b_tTps.wrote(ttt)
                            tte = sch.op("v", (lambda v: v.tensor_copy(
                                tT_sb[:], tT_ps[:])),
                                deps=[ttt] + b_tT.write_deps())
                            b_tTps.read(tte)
                            b_tT.wrote(tte)
                            twm = sch.op("t", (lambda t: [
                                t.matmul(wo_ps[:, n * 512:(n + 1) * 512],
                                         tT_sb[:, k * 128:(k + 1) * 128],
                                         wo_sb[:, k * 1024 + n * 512:k * 1024 + (n + 1) * 512],
                                         start=(k == 0), stop=(k == KC - 1))
                                for k in range(KC) for n in range(2)][-1]),
                                deps=[tte] + b_wops.write_deps() + b_wo.read_deps())
                            b_tT.read(twm)
                            b_wo.read(twm)
                            b_wops.wrote(twm)
                            tr1 = sch.op("v", (lambda v: v.scalar_tensor_tensor(
                                out=xn1[:], in0=wo_ps[:], scalar=1.0, in1=bo_b[:],
                                op0=ALU.mult, op1=ALU.add)),
                                deps=[twm] + b_xn1.write_deps() + b_ao.write_deps())
                            b_wops.read(tr1)
                            b_xn1.wrote(tr1)
                            b_ao.wrote(tr1)
                            tr2 = sch.op("v", (lambda v, tq=tq: v.tensor_add(
                                xnew[tq % 2][:], xn1[:], xres[tq % 2][:])),
                                deps=[tr1, drd] + b_xn[tq % 2].write_deps())
                            b_xn1.read(tr2)
                            b_xres[tq % 2].read(tr2)
                            b_xn[tq % 2].wrote(tr2)
                            dstk = sch.dma("y", (lambda y, r0=r0, tq=tq: y.dma_start(
                                dst_dram[r0:r0 + 128, :], xnew[tq % 2][:])), deps=[tr2],
                                ch=f"so{tq % 2}")
                            b_xn[tq % 2].read(dstk)
                            store_toks[dst_key][gt] = dstk

                        for tq in range(QT):
                            r0 = row0 + tq * 128
                            gt = r0 // 128
                            nk = tq + 1
                            drd = sch.dma("y", (lambda y, r0=r0, tq=tq: y.dma_start(
                                xres[tq % 2][:], src_dram[r0:r0 + 128, :])),
                                deps=b_xres[tq % 2].write_deps() + ([src_toks[gt]] if src_toks else []),
                                ch=f"xc{tq % 2}")
                            b_xres[tq % 2].wrote(drd)

                            ng = (nk + 3) // 4
                            items = [(hq, gi) for hq in range(16) for gi in range(ng)]
                            tok_exp = {}
                            tok_pv_last = None

                            def emit_pv(j):
                                nonlocal tok_pv_last
                                hq, gi = items[j]
                                h_ = hq % 4
                                hpar = hq % 3
                                c0 = 4 * gi
                                c1 = min(c0 + 4, nk)
                                off = (hq // 7) * 512 + (hq % 7) * 65
                                tk = sch.op("t", (lambda t, hq=hq, h_=h_, hpar=hpar,
                                                  c0=c0, c1=c1, nk=nk, off=off: [
                                    t.matmul(ao_ps[:, off:off + 65],
                                             pT_sb[hpar][:, c * 128:(c + 1) * 128],
                                             v_sb[:, c * 260 + h_ * 65:c * 260 + (h_ + 1) * 65],
                                             start=(c == 0), stop=(c == nk - 1))
                                    for c in range(c0, c1)][-1]),
                                    deps=[tok_exp[j], tok_ve[min(c1 - 1, nk - 1)]]
                                         + (b_aops.write_deps() if (hq == 0 and gi == 0) else []))
                                if c1 == nk:
                                    b_pT[hpar].read(tk)
                                tok_pv_last = tk

                            for j, (hq, gi) in enumerate(items):
                                if j == min(2, len(items) - 1):
                                    emit_p3b()
                                if j >= 2:
                                    emit_pv(j - 2)
                                h_ = hq % 4
                                hpar = hq % 3
                                c0 = 4 * gi
                                c1 = min(c0 + 4, nk)
                                gpar = gctr[0] % NSTB
                                gctr[0] += 1
                                tmm = sch.op("t", (lambda t, hq=hq, h_=h_, tq=tq, c0=c0, c1=c1, gpar=gpar: [
                                    t.matmul(sT[gpar][:, (c - c0) * 128:(c - c0 + 1) * 128],
                                             k_fm[0:65, h_ * S + c * 128:h_ * S + (c + 1) * 128],
                                             q_fm[0:65, hq * S + tq * 128:hq * S + (tq + 1) * 128],
                                             start=True, stop=True)
                                    for c in range(c0, c1)][-1]),
                                    deps=[tok_ke[c1 - 1], tok_qe[tq], tok_aug, tkon]
                                         + b_sT[gpar].write_deps())
                                b_sT[gpar].wrote(tmm)
                                tlast = tmm
                                if c0 <= tq < c1:
                                    tmk = sch.op("v", (lambda v, tq=tq, c0=c0, gpar=gpar: v.tensor_add(
                                        sT[gpar][:, (tq - c0) * 128:(tq - c0 + 1) * 128],
                                        sT[gpar][:, (tq - c0) * 128:(tq - c0 + 1) * 128],
                                        maskdT[:])),
                                        deps=[tmm] + tok_const)
                                    b_sT[gpar].wrote(tmk)
                                    tlast = tmk
                                texp = sch.op("s", (lambda s, hpar=hpar, c0=c0, c1=c1, gpar=gpar: s.activation(
                                    pT_sb[hpar][:, c0 * 128:c1 * 128],
                                    sT[gpar][:, 0:(c1 - c0) * 128], AF.Exp)),
                                    deps=[tlast] + (b_pT[hpar].write_deps() if gi == 0 else []))
                                b_sT[gpar].read(texp)
                                if gi == 0:
                                    b_pT[hpar].wrote(texp)
                                tok_exp[j] = texp
                            if len(items) >= 2:
                                emit_pv(len(items) - 2)
                            emit_pv(len(items) - 1)
                            b_aops.wrote(tok_pv_last)

                            trc = sch.op("v", (lambda v: [
                                v.reciprocal(
                                    rinv[:, g * 7:g * 7 + (7 if g < 2 else 2)],
                                    ao_ps[:, g * 512:g * 512 + (7 if g < 2 else 2) * 65]
                                    .rearrange("p (h c) -> p h c", c=65)[:, :, 64:65])
                                for g in range(3)][-1]),
                                deps=[tok_pv_last] + b_rinv.write_deps())
                            b_aops.read(trc)
                            b_rinv.wrote(trc)
                            def _aoevac(v):
                                last = None
                                for g, ng in ((0, 7), (1, 7), (2, 2)):
                                    last = v.tensor_tensor(
                                        ao_sb[:, g * 448:g * 448 + ng * 64].rearrange(
                                            "p (h c) -> p h c", c=64),
                                        ao_ps[:, g * 512:g * 512 + ng * 65].rearrange(
                                            "p (h c) -> p h c", c=65)[:, :, 0:64],
                                        rinv[:, g * 7:g * 7 + ng].rearrange(
                                            "p (h o) -> p h o", o=1).to_broadcast(
                                            [128, ng, 64]),
                                        op=ALU.mult)
                                return last
                            tae = sch.op("v", _aoevac,
                                deps=[tok_pv_last, trc] + b_ao.write_deps())
                            b_aops.read(tae)
                            b_rinv.read(tae)
                            b_ao.wrote(tae)
                            tln, _, lntoks = emit_ln(tq % 2, ao_sb[:], t1_sb[:],
                                                     [tae] + b_t1.write_deps())
                            for tk in lntoks[:3]:
                                b_ao.read(tk)
                            b_ao.read(tln)
                            b_t1.wrote(tln)
                            tg = sch.op("v", lambda v: v.tensor_mul(t1_sb[:], t1_sb[:], g_b[:]),
                                        deps=[tln])
                            b_t1.wrote(tg)
                            tb = sch.op("v", lambda v: v.tensor_add(t_bf[:], t1_sb[:], b_b[:]),
                                        deps=[tg] + b_tbf.write_deps())
                            b_t1.read(tb)
                            b_tbf.wrote(tb)
                            p3b_pend[0] = (tq, r0, gt, tb, drd)
                        emit_p3b()
                        sch.barrier()
                sch.barrier()

        if stage in ("f1", "f1dbg"):
            emit_ffn(x_in, out, "out", None, True, True)
        elif stage == "attn":
            emit_ffn(x_in, x1, "x1", None, True, True)
            emit_attn(x1, out, "out", store_toks["x1"])
        else:
            emit_ffn(x_in, x1, "x1", None, True, True)
            emit_attn(x1, x2, "x2", store_toks["x1"])
            emit_ffn(x2, out, "out", store_toks["x2"], False, False)

        with ExitStack() as semctx:
            semnames = sorted(set(list(sch.counts.keys()) + list(Sched.ENGS)))
            sems = {n: semctx.enter_context(nc.semaphore(f"sem_{n}")) for n in semnames}
            block = semctx.enter_context(nc.Block())

            def runner(name, eng):
                hw = {}
                for (e, waits, fn, amt, sem) in sch.recs:
                    if e != name:
                        continue
                    for (sname, val) in waits:
                        if hw.get(sname, 0) >= val:
                            continue
                        eng.wait_ge(sems[sname], val)
                        hw[sname] = val
                    if fn is not None:
                        inst = fn(eng)
                        if amt:
                            inst.then_inc(sems[sem], amt)
                if name == "y":
                    for sname, cnt in sorted(sch.counts.items()):
                        if sname not in Sched.ENGS and cnt:
                            eng.wait_ge(sems[sname], cnt)

            block.tensor(lambda t: runner("t", t))
            block.vector(lambda v: runner("v", v))
            block.scalar(lambda s: runner("s", s))
            block.gpsimd(lambda g: runner("g", g))
            block.sync(lambda y: runner("y", y))

    return nc


def kernel(L0_attn_x, L0_attn_Wq, L0_attn_bq, L0_attn_Wk, L0_attn_bk,
           L0_attn_Wv, L0_attn_bv, L0_attn_ln_g, L0_attn_ln_b, L0_attn_Wo,
           L0_attn_bo, L0_ffn_Wg, L0_ffn_W1, L0_ffn_W2, L0_ffn_Ws1, L0_ffn_Ws2,
           _stage="full"):
    x = np.ascontiguousarray(np.asarray(L0_attn_x, dtype=np.float32))
    shards = x.reshape(NCORES, TOK, D)
    common = {
        "Wq": np.ascontiguousarray(L0_attn_Wq, dtype=np.float32),
        "bq": np.ascontiguousarray(L0_attn_bq, dtype=np.float32).reshape(1, D),
        "Wk": np.ascontiguousarray(L0_attn_Wk, dtype=np.float32),
        "bk": np.ascontiguousarray(L0_attn_bk, dtype=np.float32).reshape(1, 256),
        "Wv": np.ascontiguousarray(L0_attn_Wv, dtype=np.float32),
        "bv": np.ascontiguousarray(L0_attn_bv, dtype=np.float32).reshape(1, 256),
        "ln_g": np.ascontiguousarray(L0_attn_ln_g, dtype=np.float32).reshape(1, D),
        "ln_b": np.ascontiguousarray(L0_attn_ln_b, dtype=np.float32).reshape(1, D),
        "Wo": np.ascontiguousarray(L0_attn_Wo, dtype=np.float32),
        "bo": np.ascontiguousarray(L0_attn_bo, dtype=np.float32).reshape(1, D),
        "Wg": np.ascontiguousarray(L0_ffn_Wg, dtype=np.float32),
        "W1": np.ascontiguousarray(L0_ffn_W1, dtype=np.float32),
        "W2": np.ascontiguousarray(L0_ffn_W2, dtype=np.float32),
        "Ws1": np.ascontiguousarray(L0_ffn_Ws1, dtype=np.float32),
        "Ws2": np.ascontiguousarray(L0_ffn_Ws2, dtype=np.float32),
    }
    nc = build(_stage)
    in_maps = [dict(common, x=np.ascontiguousarray(shards[c])) for c in range(NCORES)]
    res = run_bass_kernel_spmd(nc, in_maps, core_ids=list(range(NCORES)))
    outs = [res.results[c]["out"] for c in range(NCORES)]
    return np.concatenate(outs, axis=0).reshape(B, S, D)


# version 21
# speedup vs baseline: 1.1841x; 1.0005x over previous
from contextlib import ExitStack

import numpy as np
import concourse.bass as bass
import concourse.mybir as mybir
from concourse.bass_utils import run_bass_kernel_spmd
from concourse.bass_core_rust import IndirectOffsetOnAxis

F32 = mybir.dt.float32
BF16 = mybir.dt.bfloat16
I32 = mybir.dt.int32
RSQRT_MAGIC = 0x5F3759DF
AF = mybir.ActivationFunctionType
ALU = mybir.AluOpType
AX = mybir.AxisListType
f32 = np.float32

B, S, D = 16, 2048, 1024
NCORES = 8
SEQ_PER_CORE = B // NCORES
TOK = SEQ_PER_CORE * S
NT = TOK // 128
NEXP = 16
NPE = 18
ST = 2048
NT_ST = ST // 128
NST = TOK // ST
QT = S // 128
EPS = f32(1e-5)
KC = 8
CAP = 384
CAPT = CAP // 128
ZROW = 16 * CAP


class Sched:
    ENGS = ("t", "v", "s", "g", "y")

    def __init__(self):
        self.recs = []
        self.counts = {}
        self.barrier_level = {}

    def _chk(self, deps):
        waits = {}
        for d in deps:
            if d is None:
                continue
            sem, val = d
            assert val <= self.counts.get(sem, 0), (sem, val, self.counts.get(sem, 0))
            if waits.get(sem, 0) < val:
                waits[sem] = val
        for sem in [s for s, v in waits.items()
                    if s not in Sched.ENGS and v <= self.barrier_level.get(s, -1)]:
            del waits[sem]
        for sem, val in waits.items():
            if sem not in Sched.ENGS:
                assert val == self.counts.get(sem, 0), (
                    "unsafe intermediate DMA wait", sem, val, self.counts.get(sem, 0))
        return list(waits.items())

    def op(self, eng, fn, deps=()):
        waits = self._chk(deps)
        self.counts[eng] = self.counts.get(eng, 0) + 1
        tok = (eng, self.counts[eng])
        self.recs.append((eng, waits, fn, 1, eng))
        return tok

    def dma(self, q, fn, deps=(), ch="dy"):
        pass
        assert q in ("y", "g", "s")
        waits = self._chk(deps)
        sem = ch
        self.counts[sem] = self.counts.get(sem, 0) + 16
        tok = (sem, self.counts[sem])
        self.recs.append((q, waits, fn, 16, sem))
        return tok

    def wait_only(self, eng, deps):
        waits = self._chk(deps)
        if waits:
            self.recs.append((eng, waits, None, 0, None))

    def barrier(self):
        toks = [(s, v) for s, v in self.counts.items()]
        for e in self.ENGS:
            self.wait_only(e, toks)
        for s, v in self.counts.items():
            if s not in Sched.ENGS:
                self.barrier_level[s] = v


class Buf:

    def __init__(self):
        self.w = None
        self.r = {}

    def write_deps(self):
        return list(self.r.values()) or ([self.w] if self.w else [])

    def read_deps(self):
        return [self.w] if self.w else []

    def wrote(self, tok):
        self.w = tok
        self.r = {}

    def read(self, tok):
        self.r[tok[0]] = tok


def build(stage="full"):
    nc = bass.Bass("TRN2", target_bir_lowering=False)
    _uid = [0]

    def nm(base):
        _uid[0] += 1
        return f"{base}_{_uid[0]}"

    x_in = nc.dram_tensor("x", [TOK, D], F32, kind="ExternalInput").ap()
    wq = nc.dram_tensor("Wq", [D, D], F32, kind="ExternalInput").ap()
    bq = nc.dram_tensor("bq", [1, D], F32, kind="ExternalInput").ap()
    wk = nc.dram_tensor("Wk", [D, 256], F32, kind="ExternalInput").ap()
    bk = nc.dram_tensor("bk", [1, 256], F32, kind="ExternalInput").ap()
    wv = nc.dram_tensor("Wv", [D, 256], F32, kind="ExternalInput").ap()
    bv = nc.dram_tensor("bv", [1, 256], F32, kind="ExternalInput").ap()
    ln_g = nc.dram_tensor("ln_g", [1, D], F32, kind="ExternalInput").ap()
    ln_b = nc.dram_tensor("ln_b", [1, D], F32, kind="ExternalInput").ap()
    wo = nc.dram_tensor("Wo", [D, D], F32, kind="ExternalInput").ap()
    bo = nc.dram_tensor("bo", [1, D], F32, kind="ExternalInput").ap()
    wg = nc.dram_tensor("Wg", [NEXP, D], F32, kind="ExternalInput").ap()
    w1 = nc.dram_tensor("W1", [NEXP, D, 2 * 512], F32, kind="ExternalInput").ap()
    w2 = nc.dram_tensor("W2", [NEXP, 512, D], F32, kind="ExternalInput").ap()
    ws1 = nc.dram_tensor("Ws1", [D, 2048], F32, kind="ExternalInput").ap()
    ws2 = nc.dram_tensor("Ws2", [1024, D], F32, kind="ExternalInput").ap()
    out = nc.dram_tensor("out", [TOK, D], F32, kind="ExternalOutput").ap()
    x1 = nc.dram_tensor("x1", [TOK, D], F32, kind="Internal").ap()
    x2 = nc.dram_tensor("x2", [TOK, D], F32, kind="Internal").ap()
    xhat_dram = nc.dram_tensor("xhat_d", [TOK, D], BF16, kind="Internal").ap()
    xg_dram = [nc.dram_tensor(f"xg_d{i}", [ZROW + 1, D], BF16, kind="Internal").ap()
               for i in range(2)]
    zg_dram = [nc.dram_tensor(f"zg_d{i}", [ZROW + 1, D], F32, kind="Internal").ap()
               for i in range(2)]
    wbf1 = nc.dram_tensor("wbf1", [NPE, 128, KC * 1024], BF16, kind="Internal").ap()
    wbf2 = nc.dram_tensor("wbf2", [NPE, 128, 4 * 1024], BF16, kind="Internal").ap()

    sch = Sched()

    with ExitStack() as top:
        E = top.enter_context
        ones_row = E(nc.sbuf_tensor(nm("ones_row"), [1, 128], F32))
        ident = E(nc.sbuf_tensor(nm("ident"), [128, 128], BF16))
        maskdT = E(nc.sbuf_tensor(nm("maskdT"), [128, 128], F32))
        ident_f = E(nc.sbuf_tensor(nm("ident_f"), [128, 128], F32))
        wgT = E(nc.sbuf_tensor(nm("wgT"), [128, 128], BF16))
        ltri_bf = E(nc.sbuf_tensor(nm("ltri_bf"), [128, 128], BF16))
        ones_col_bf = E(nc.sbuf_tensor(nm("ones_col_bf"), [128, 1], BF16))
        ecb = E(nc.sbuf_tensor(nm("ecb"), [128, 16], F32))
        iotae = E(nc.sbuf_tensor(nm("iotae"), [128, 16], F32))


        with nc.psum_tensor(nm("set_ps"), [128, 512], F32) as set_ps, \
             nc.psum_tensor(nm("set_ps2"), [128, 128], BF16) as set_ps2, \
             nc.sbuf_tensor(nm("rows_sb"), [1, D], F32) as rows_sb, \
             nc.sbuf_tensor(nm("zero_row"), [1, D], F32) as zero_row:
            t0 = sch.op("g", lambda g: g.memset(ones_row[:], 1.0))
            t1 = sch.op("g", lambda g: g.memset(ident[:], 0.0))
            t2 = sch.op("g", lambda g: g.affine_select(
                out=ident[:], in_=ident[:], compare_op=ALU.not_equal, fill=1.0,
                base=0, pattern=[[-1, 128]], channel_multiplier=1), deps=[t1])
            t5 = sch.op("g", lambda g: g.memset(maskdT[:], 0.0))
            t6 = sch.op("g", lambda g: g.affine_select(
                out=maskdT[:], in_=maskdT[:], compare_op=ALU.is_ge, fill=-1e10,
                base=0, pattern=[[1, 128]], channel_multiplier=-1), deps=[t5])
            t7 = sch.op("g", lambda g: g.memset(ident_f[:], 0.0))
            t8 = sch.op("g", lambda g: g.affine_select(
                out=ident_f[:], in_=ident_f[:], compare_op=ALU.not_equal, fill=1.0,
                base=0, pattern=[[-1, 128]], channel_multiplier=1), deps=[t7])
            t9 = sch.op("g", lambda g: g.memset(ltri_bf[:], 0.0))
            t10 = sch.op("g", lambda g: g.affine_select(
                out=ltri_bf[:], in_=ltri_bf[:], compare_op=ALU.is_ge, fill=1.0,
                base=0, pattern=[[-1, 128]], channel_multiplier=1), deps=[t9])
            t11 = sch.op("g", lambda g: g.memset(ones_col_bf[:], 1.0))
            t12 = sch.op("g", lambda g: g.iota(
                ecb[:], pattern=[[CAP, 16]], base=0, channel_multiplier=0,
                allow_small_or_imprecise_dtypes=True))
            t13 = sch.op("g", lambda g: g.iota(
                iotae[:], pattern=[[1, 16]], base=0, channel_multiplier=0,
                allow_small_or_imprecise_dtypes=True))
            tok_const = [t0, t2, t6, t8, t10, t11, t12, t13]
            tz = sch.op("g", lambda g: g.memset(zero_row[:1, :], 0.0))
            for i in range(2):
                sch.dma("y", (lambda y, i=i: y.dma_start(
                    zg_dram[i][ZROW:ZROW + 1, :], zero_row[:1, :])),
                    deps=[tz], ch=f"zsink{i}")

            b_rows = Buf()
            b_setps = Buf()

            def _unused_bcast(dst, row_dram, n, scale=None):
                dld = sch.dma("y", (lambda y, n=n, row_dram=row_dram: y.dma_start(
                    rows_sb[:1, 0:n], row_dram)), deps=b_rows.write_deps(), ch="drw")
                b_rows.wrote(dld)
                if scale is not None:
                    dsc = sch.op("s", (lambda s, n=n, scale=scale: s.activation(
                        rows_sb[:1, 0:n], rows_sb[:1, 0:n], AF.Identity,
                        bias=f32(0.0), scale=f32(scale))), deps=[dld])
                    b_rows.wrote(dsc)
                    src_tok = dsc
                else:
                    src_tok = dld
                nj = (n + 511) // 512
                last = None
                for j in range(nj):
                    w = min(512, n - j * 512)
                    tmm = sch.op("t", (lambda t, j=j, w=w: t.matmul(
                        set_ps[:, 0:w], ones_row[:1, :],
                        rows_sb[:1, j * 512:j * 512 + w], start=True, stop=True)),
                        deps=[src_tok, t0] + b_setps.write_deps())
                    b_rows.read(tmm)
                    b_setps.wrote(tmm)
                    tev = sch.op("v", (lambda v, j=j, w=w, dst=dst: v.tensor_copy(
                        dst[:, j * 512:j * 512 + w], set_ps[:, 0:w])), deps=[tmm])
                    b_setps.read(tev)
                    last = tev
                return last

            tok_bcast = []

            with nc.sbuf_tensor(nm("wg_tmp"), [16, D], BF16) as wg_tmp:
                dwg = sch.dma("g", lambda g: g.dma_start(wg_tmp[:], wg), ch="dwg")
                tprev = None
                for c in range(KC):
                    ttr = sch.op("t", (lambda t, c=c: t.transpose(
                        set_ps2[:, 0:16], wg_tmp[:16, c * 128:(c + 1) * 128],
                        ident[0:16, 0:16])), deps=[dwg, t2] + ([tprev] if tprev else []))
                    tev = sch.op("v", (lambda v, c=c: v.tensor_copy(
                        wgT[:, c * 16:(c + 1) * 16], set_ps2[:, 0:16])), deps=[ttr])
                    tprev = tev
                tok_wgT = tprev
            sch.barrier()

        lnst = [E(nc.sbuf_tensor(nm(f"lnst{i}"), [128, 12], F32)) for i in range(2)]
        lnag = [E(nc.sbuf_tensor(nm(f"lnag{i}"), [128, 2], F32)) for i in range(2)]
        lnr = [E(nc.sbuf_tensor(nm(f"lnr{i}"), [128, 4], F32)) for i in range(2)]

        def emit_rsqrt(x_ap, xe_ap, y_ap, u_ap, eps, deps, iters=2):
            tk = sch.op("v", (lambda v: v.tensor_scalar_add(xe_ap, x_ap, eps)), deps=deps)
            tk = sch.op("v", (lambda v: v.tensor_scalar(
                out=y_ap.bitcast(I32), in0=xe_ap.bitcast(I32), scalar1=1,
                scalar2=None, op0=ALU.arith_shift_right)), deps=[tk])
            tk = sch.op("v", (lambda v: v.tensor_scalar(
                out=y_ap.bitcast(I32), in0=y_ap.bitcast(I32), scalar1=-1,
                scalar2=RSQRT_MAGIC, op0=ALU.mult, op1=ALU.add)), deps=[tk])
            for _ in range(iters):
                tk = sch.op("v", (lambda v: v.scalar_tensor_tensor(
                    out=u_ap, in0=y_ap, scalar=1.0, in1=y_ap,
                    op0=ALU.bypass, op1=ALU.mult)), deps=[tk])
                tk = sch.op("v", (lambda v: v.scalar_tensor_tensor(
                    out=u_ap, in0=u_ap, scalar=-0.5, in1=xe_ap,
                    op0=ALU.mult, op1=ALU.mult)), deps=[tk])
                tk = sch.op("v", (lambda v: v.scalar_tensor_tensor(
                    out=y_ap, in0=u_ap, scalar=1.5, in1=y_ap,
                    op0=ALU.add, op1=ALU.mult)), deps=[tk])
            return tk

        def emit_ln(par, src_ap, dst_ap, deps, scratch_r=None):
            stt = lnst[par]
            agg = lnag[par]
            rr = lnr[par] if scratch_r is None else scratch_r
            t_a = sch.op("v", lambda v: v.bn_stats(stt[:, 0:6], src_ap[:, 0:512]), deps=deps)
            t_b = sch.op("v", lambda v: v.bn_stats(stt[:, 6:12], src_ap[:, 512:1024]), deps=deps)
            t_c = sch.op("v", lambda v: v.bn_aggr(agg[:], stt[:]), deps=[t_a, t_b])
            t_e = emit_rsqrt(agg[:, 1:2], rr[:, 2:3], rr[:, 0:1], rr[:, 3:4],
                             float(EPS), [t_c], iters=2)
            t_f = sch.op("v", lambda v: v.scalar_tensor_tensor(
                out=rr[:, 1:2], in0=agg[:, 0:1], scalar=-1.0, in1=rr[:, 0:1],
                op0=ALU.mult, op1=ALU.mult), deps=[t_e, t_c])
            t_g = sch.op("s", lambda s: s.activation(
                dst_ap, src_ap, AF.Identity, bias=rr[:, 1:2], scale=rr[:, 0:1]),
                deps=[t_f, t_e] + list(deps))
            return t_g, (rr[:, 0:1], rr[:, 1:2]), [t_a, t_b, t_c, t_e, t_f, t_g]

        store_toks = {"x1": [None] * NT, "x2": [None] * NT, "out": [None] * NT}
        b_wbf1 = [Buf() for _ in range(NPE)]
        b_wbf2 = [Buf() for _ in range(NPE)]

        def emit_ffn(src_dram, dst_dram, dst_key, src_toks, first, sparse):
            with ExitStack() as ffn:
                FE = ffn.enter_context
                zacc = FE(nc.sbuf_tensor(nm("zacc"), [128, NT_ST * D], F32))
                cw_all = [FE(nc.sbuf_tensor(nm(f"cw_all{i}"), [128, NT_ST * 16], F32)) for i in range(2)]
                glog_all = cw_all
                w1s = [FE(nc.sbuf_tensor(nm(f"w1s{i}"), [128, KC * 1024], BF16)) for i in range(2)]
                w2s = [FE(nc.sbuf_tensor(nm(f"w2s{i}"), [128, 4 * 1024], BF16)) for i in range(2)]
                xtile = [FE(nc.sbuf_tensor(nm(f"xtile{i}"), [128, D], F32)) for i in range(2)]
                xhat_tm = [FE(nc.sbuf_tensor(nm(f"xhat_tm{i}"), [128, D], BF16)) for i in range(2)]
                if sparse:
                    xg_sb = [FE(nc.sbuf_tensor(nm(f"xg_sb{i}"), [128, D], BF16)) for i in range(3)]
                    xgT = [FE(nc.sbuf_tensor(nm(f"xgT{i}"), [128, KC * 128], BF16)) for i in range(2)]
                    _xfm = FE(nc.sbuf_tensor(nm("xhat_fm"), [128, KC * ST], BF16))
                    xhat_fm = [_xfm, _xfm]
                else:
                    xhat_fm = [FE(nc.sbuf_tensor(nm(f"xhat_fm{i}"), [128, KC * ST], BF16)) for i in range(2)]
                sg = [FE(nc.sbuf_tensor(nm(f"sg{i}"), [128, 512], BF16)) for i in range(2)]
                h_bf = [FE(nc.sbuf_tensor(nm(f"h_bf{i}"), [128, 512], BF16)) for i in range(2)]
                hT_sb = [FE(nc.sbuf_tensor(nm(f"hT_sb{i}"), [128, 512], BF16)) for i in range(3)]
                zg_sb = [FE(nc.sbuf_tensor(nm(f"zg_sb{i}"), [128, D], F32)) for i in range(2)] if sparse else None
                if sparse:
                    base_row = FE(nc.sbuf_tensor(nm("base_row"), [1, 16], F32))
                    maskf = [FE(nc.sbuf_tensor(nm(f"maskf{i}"), [128, 16], F32)) for i in range(2)]
                    mask_bf = [FE(nc.sbuf_tensor(nm(f"mask_bf{i}"), [128, 16], BF16)) for i in range(2)]
                    mask_all = [FE(nc.sbuf_tensor(nm(f"mask_all{i}"), [128, NT_ST * 16], BF16)) for i in range(2)]
                    rsc = FE(nc.sbuf_tensor(nm("rsc"), [128, 6 * 16], F32))
                    dk_all = [FE(nc.sbuf_tensor(nm(f"dk_all{i}"), [128, NT_ST * 2], I32)) for i in range(2)]
                    dkf = FE(nc.sbuf_tensor(nm("dkf"), [128, 2], F32))
                    cwk_all = [FE(nc.sbuf_tensor(nm(f"cwk_all{i}"), [128, NT_ST * 2], F32)) for i in range(2)]
                else:
                    base_row = maskf = mask_bf = mask_all = rsc = dk_all = dkf = cwk_all = None
                _gp = FE(nc.sbuf_tensor(nm("gp16"), [128, 16], F32))
                gp16 = [_gp, _gp]
                _s16 = FE(nc.sbuf_tensor(nm("s16"), [128, 16], F32))
                s16 = [_s16, _s16]
                _m8 = FE(nc.sbuf_tensor(nm("m8"), [128, 8], F32))
                m8 = [_m8, _m8]
                _gsc = FE(nc.sbuf_tensor(nm("gsc"), [128, 4], F32))
                gsc = [_gsc, _gsc]
                xctile = xtile
                zk_sb = [FE(nc.sbuf_tensor(nm(f"zk_sb{i}"), [128, D], F32)) for i in range(2)] if sparse else None
                xnew = [FE(nc.sbuf_tensor(nm(f"xnew{i}"), [128, D], F32)) for i in range(2)]

                y_ps = [FE(nc.psum_tensor(nm(f"y_ps{i}"), [128, 1024], F32)) for i in range(2)]
                hT_ps = [FE(nc.psum_tensor(nm(f"hT_ps{i}"), [128, 1024], BF16)) for i in range(2)]
                w2o = FE(nc.psum_tensor(nm("w2o"), [128, 1024], F32))
                glog_ps = w2o[:, 0:16]
                pfx_ps = w2o[:, 16:32]
                tot_ps = w2o[0:1, 32:48]
                baseb_ps = w2o[:, 48:64]

                b_xt = [Buf(), Buf()]
                b_xhtm = [Buf(), Buf()]
                b_xg = [Buf(), Buf(), Buf()]
                if sparse:
                    _bxf = [Buf() for _ in range(NT_ST)]
                    b_xfm = [_bxf, _bxf]
                else:
                    b_xfm = [[Buf() for _ in range(NT_ST)] for _ in range(2)]
                b_xgT = [Buf(), Buf()]
                b_cw = [[Buf() for _ in range(NT_ST)] for _ in range(2)]
                b_mall = [[Buf() for _ in range(NT_ST)] for _ in range(2)]
                b_z = [Buf() for _ in range(NT_ST)]
                b_w1 = [Buf(), Buf()]
                b_w2 = [Buf(), Buf()]
                b_yps = [Buf(), Buf()]
                b_hTps = [Buf(), Buf()]
                b_w2o = Buf()
                b_sg = [Buf(), Buf()]
                b_h = [Buf(), Buf()]
                b_hTsb = [Buf(), Buf(), Buf()]
                _bg = Buf()
                b_g16 = [_bg, _bg]
                b_zg = [Buf(), Buf()]
                b_base = Buf()
                b_baser = Buf()
                b_rsc = Buf()
                b_dk = [[Buf() for _ in range(NT_ST)] for _ in range(2)]
                b_cwk = [[Buf() for _ in range(NT_ST)] for _ in range(2)]
                b_xgd = [Buf(), Buf()]
                b_zgd = [Buf(), Buf()]
                scat_last = [[None, None], [None, None]]
                zg_last = [[None, None], [None, None]]
                b_xc = b_xt
                b_zk = [Buf(), Buf()]
                b_xn = [Buf(), Buf()]

                def wdma(e, cast):
                    sl = e % 2
                    d1 = b_w1[sl].write_deps()
                    d2 = b_w2[sl].write_deps()
                    if cast:
                        if e < NEXP:
                            tk1 = sch.dma("g", (lambda g, e=e, sl=sl: g.dma_start(
                                w1s[sl][:].rearrange("p (c n) -> p c n", c=KC),
                                w1[e].rearrange("(c p) n -> p c n", p=128))), deps=d1, ch=f"w1{sl}")
                            tk2 = sch.dma("g", (lambda g, e=e, sl=sl: g.dma_start(
                                w2s[sl][:].rearrange("p (c n) -> p c n", c=4),
                                w2[e].rearrange("(c p) n -> p c n", p=128))), deps=d2, ch=f"w2{sl}")
                        else:
                            j = e - NEXP
                            tk1a = sch.dma("g", (lambda g, j=j, sl=sl: g.dma_start(
                                w1s[sl][:].rearrange("p (c n) -> p c n", c=KC)[:, :, 0:512],
                                ws1.rearrange("(c p) n -> p c n", p=128)[:, :, j * 512:(j + 1) * 512])),
                                deps=d1, ch=f"w1{sl}")
                            tk1 = sch.dma("g", (lambda g, j=j, sl=sl: g.dma_start(
                                w1s[sl][:].rearrange("p (c n) -> p c n", c=KC)[:, :, 512:1024],
                                ws1.rearrange("(c p) n -> p c n", p=128)[:, :, 1024 + j * 512:1024 + (j + 1) * 512])),
                                deps=[tk1a], ch=f"w1{sl}")
                            tk2 = sch.dma("g", (lambda g, j=j, sl=sl: g.dma_start(
                                w2s[sl][:].rearrange("p (c n) -> p c n", c=4),
                                ws2[j * 512:(j + 1) * 512, :].rearrange("(c p) n -> p c n", p=128))),
                                deps=d2, ch=f"w2{sl}")
                        ts1 = sch.dma("s", (lambda y, e=e, sl=sl: y.dma_start(
                            wbf1[e], w1s[sl][:])), deps=[tk1], ch=f"wb1{e}")
                        ts2 = sch.dma("s", (lambda y, e=e, sl=sl: y.dma_start(
                            wbf2[e], w2s[sl][:])), deps=[tk2], ch=f"wb2{e}")
                        b_w1[sl].wrote(tk1)
                        b_w1[sl].read(ts1)
                        b_w2[sl].wrote(tk2)
                        b_w2[sl].read(ts2)
                        b_wbf1[e].wrote(ts1)
                        b_wbf2[e].wrote(ts2)
                    else:
                        tk1 = sch.dma("y", (lambda y, e=e, sl=sl: y.dma_start(
                            w1s[sl][:], wbf1[e])), deps=d1 + b_wbf1[e].read_deps(),
                            ch=f"wB1{sl}")
                        tk2 = sch.dma("y", (lambda y, e=e, sl=sl: y.dma_start(
                            w2s[sl][:], wbf2[e])), deps=d2 + b_wbf2[e].read_deps(),
                            ch=f"wB2{sl}")
                        b_wbf1[e].read(tk1)
                        b_wbf2[e].read(tk2)
                        b_w1[sl].wrote(tk1)
                        b_w2[sl].wrote(tk2)

                tln_pend = {}

                def passA_front(st_i, tt):
                    par = tt % 2
                    r0 = st_i * ST + tt * 128
                    dld = sch.dma("y", (lambda y, r0=r0, par=par: y.dma_start(
                        xtile[par][:], src_dram[r0:r0 + 128, :])),
                        deps=b_xt[par].write_deps() + ([src_toks[(r0 // 128)]] if src_toks else []),
                        ch=f"xa{par}")
                    b_xt[par].wrote(dld)
                    tln, _, lntoks = emit_ln(par, xtile[par][:], xhat_tm[par][:],
                                             [dld] + b_xhtm[par].write_deps())
                    for tk in lntoks[:3]:
                        b_xt[par].read(tk)
                    b_xt[par].read(tln)
                    b_xhtm[par].wrote(tln)
                    tln_pend[(st_i, tt)] = tln

                def passA_back(st_i, tt):
                    sp = st_i % 2
                    par = tt % 2
                    tln = tln_pend.pop((st_i, tt))
                    r0 = st_i * ST + tt * 128
                    if sparse and tt == 0 and st_i > 0:
                        trst = sch.op("v", (lambda v: v.memset(base_row[:1, :], 0.0)),
                                      deps=b_baser.read_deps() + b_baser.write_deps())
                        b_baser.wrote(trst)
                    ttr = sch.op("t", (lambda t, par=par: [
                        t.transpose(hT_ps[par][:, c * 128:(c + 1) * 128],
                                    xhat_tm[par][:, c * 128:(c + 1) * 128],
                                    ident[:]) for c in range(KC)][-1]),
                        deps=[tln] + b_hTps[par].write_deps() + tok_const)
                    b_xhtm[par].read(ttr)
                    b_hTps[par].wrote(ttr)
                    tev = sch.op("v", (lambda v, sp=sp, par=par, tt=tt: v.tensor_copy(
                        xhat_fm[sp][:].rearrange("p (c n) -> p c n", c=KC)[:, :, tt * 128:(tt + 1) * 128],
                        hT_ps[par][:].rearrange("p (c n) -> p c n", c=KC))),
                        deps=[ttr] + b_xfm[sp][tt].write_deps())
                    b_hTps[par].read(tev)
                    b_xfm[sp][tt].wrote(tev)
                    gate_lhsT = (lambda c, sp=sp, tt=tt: xhat_fm[sp][:, c * ST + tt * 128:c * ST + (tt + 1) * 128])
                    tgm_deps = [tev]
                    tgm = sch.op("t", (lambda t, gate_lhsT=gate_lhsT: [
                        t.matmul(glog_ps,
                                 gate_lhsT(c),
                                 wgT[:, c * 16:(c + 1) * 16],
                                 start=(c == 0), stop=(c == KC - 1)) for c in range(KC)][-1]),
                        deps=tgm_deps + [tok_wgT] + b_w2o.write_deps())
                    b_xfm[sp][tt].read(tgm)
                    b_w2o.wrote(tgm)
                    tge = sch.op("v", (lambda v, sp=sp, tt=tt: v.tensor_copy(
                        glog_all[sp][:, tt * 16:(tt + 1) * 16], glog_ps)),
                        deps=[tgm] + b_cw[sp][tt].write_deps())
                    b_w2o.read(tge)
                    b_cw[sp][tt].wrote(tge)
                    if not sparse:
                        return
                    tm8 = sch.op("v", (lambda v, par=par, sp=sp, tt=tt: v.max(
                        out=m8[par][:], in_=glog_all[sp][:, tt * 16:(tt + 1) * 16])),
                        deps=[tge] + b_g16[par].write_deps())
                    b_cw[sp][tt].read(tm8)
                    tmk = sch.op("v", (lambda v, par=par, sp=sp, tt=tt: v.tensor_scalar(
                        out=maskf[par][:], in0=glog_all[sp][:, tt * 16:(tt + 1) * 16],
                        scalar1=m8[par][:, 1:2], scalar2=None, op0=ALU.is_ge)),
                        deps=[tm8])
                    b_cw[sp][tt].read(tmk)
                    b_g16[par].wrote(tmk)
                    tmb = sch.op("v", (lambda v, par=par: v.tensor_copy(
                        mask_bf[par][:], maskf[par][:])), deps=[tmk])
                    tma = sch.op("v", (lambda v, par=par, sp=sp, tt=tt: v.tensor_copy(
                        mask_all[sp][:, tt * 16:(tt + 1) * 16], maskf[par][:])),
                        deps=[tmk] + b_mall[sp][tt].write_deps())
                    b_mall[sp][tt].wrote(tma)
                    tpf = sch.op("t", (lambda t, par=par: t.matmul(
                        pfx_ps, ltri_bf[:], mask_bf[par][:], start=True, stop=True)),
                        deps=[tmb, tge] + b_w2o.write_deps() + tok_const)
                    b_w2o.wrote(tpf)
                    ttot = sch.op("t", (lambda t, par=par: t.matmul(
                        tot_ps, ones_col_bf[:, 0:1], mask_bf[par][:], start=True, stop=True)),
                        deps=[tmb, tpf])
                    b_w2o.wrote(ttot)
                    tbb = sch.op("t", (lambda t: t.matmul(
                        baseb_ps, ones_row[:1, :], base_row[:1, :], start=True, stop=True)),
                        deps=[tpf] + b_baser.read_deps() + b_baser.write_deps() + [tok_const[0]])
                    b_baser.read(tbb)
                    b_w2o.wrote(tbb)
                    R = rsc
                    t_s = sch.op("v", (lambda v: v.tensor_copy(R[:, 80:96], baseb_ps)),
                                 deps=[tbb] + b_rsc.write_deps())
                    b_w2o.read(t_s)
                    t_s2 = sch.op("v", (lambda v: v.tensor_add(R[:, 0:16], pfx_ps, R[:, 80:96])),
                                  deps=[t_s, tpf])
                    b_w2o.read(t_s2)
                    t_bu = sch.op("v", (lambda v: v.tensor_add(
                        base_row[:1, :], base_row[:1, :], tot_ps[:1, :])),
                        deps=[ttot, tbb])
                    b_baser.wrote(t_bu)
                    b_w2o.read(t_bu)
                    t_ok = sch.op("v", (lambda v: v.tensor_scalar(
                        out=R[:, 16:32], in0=R[:, 0:16], scalar1=float(CAP),
                        scalar2=None, op0=ALU.is_lt)), deps=[t_s2])
                    t_t1 = sch.op("v", (lambda v: v.tensor_add(R[:, 32:48], R[:, 0:16], ecb[:])),
                                  deps=[t_s2, tok_const[0]])
                    t_c1 = sch.op("v", (lambda v: v.tensor_scalar(
                        out=R[:, 32:48], in0=R[:, 32:48], scalar1=float(ZROW),
                        scalar2=None, op0=ALU.subtract)), deps=[t_t1])
                    t_c2 = sch.op("v", (lambda v: v.tensor_mul(
                        R[:, 32:48], R[:, 32:48], R[:, 16:32])), deps=[t_c1, t_ok])
                    t_c3 = sch.op("v", (lambda v: v.tensor_scalar(
                        out=R[:, 32:48], in0=R[:, 32:48], scalar1=float(ZROW),
                        scalar2=None, op0=ALU.add)), deps=[t_c2])
                    t_m1 = sch.op("v", (lambda v: v.tensor_scalar(
                        out=R[:, 48:64], in0=R[:, 32:48], scalar1=float(ZROW),
                        scalar2=None, op0=ALU.subtract)), deps=[t_c3])
                    t_m2 = sch.op("v", (lambda v, par=par: v.tensor_mul(
                        R[:, 48:64], R[:, 48:64], maskf[par][:])), deps=[t_m1, tmk])
                    t_m3 = sch.op("v", (lambda v: v.tensor_scalar(
                        out=R[:, 48:64], in0=R[:, 48:64], scalar1=float(ZROW),
                        scalar2=None, op0=ALU.add)), deps=[t_m2])
                    t_x1 = sch.op("v", (lambda v: v.tensor_scalar(
                        out=R[:, 64:80], in0=R[:, 32:48], scalar1=1.0,
                        scalar2=None, op0=ALU.add)), deps=[t_c3])
                    t_x2 = sch.op("v", (lambda v, par=par: v.tensor_mul(
                        R[:, 64:80], R[:, 64:80], maskf[par][:])), deps=[t_x1, tmk])
                    t_x3 = sch.op("v", (lambda v: v.tensor_scalar(
                        out=R[:, 64:80], in0=R[:, 64:80], scalar1=1.0,
                        scalar2=None, op0=ALU.subtract)), deps=[t_x2])
                    t_d0 = sch.op("v", (lambda v: v.tensor_reduce(
                        dkf[:, 0:1], R[:, 48:64], axis=AX.X, op=ALU.min)),
                        deps=[t_m3])
                    t_d1 = sch.op("v", (lambda v: v.reduce_max(
                        dkf[:, 1:2], R[:, 64:80], axis=AX.X)), deps=[t_x3])
                    b_rsc.wrote(t_d1)
                    tci = sch.op("v", (lambda v, sp=sp, tt=tt: v.tensor_copy(
                        dk_all[sp][:, 2 * tt:2 * tt + 2], dkf[:, 0:2])),
                        deps=[t_d0, t_d1] + b_dk[sp][tt].write_deps())
                    b_dk[sp][tt].wrote(tci)
                    for k in range(2):
                        tsc_ = sch.dma("g", (lambda g, sp=sp, par=par, tt=tt, k=k: g.indirect_dma_start(
                            out=xg_dram[sp], out_offset=IndirectOffsetOnAxis(
                                dk_all[sp][:, 2 * tt + k:2 * tt + k + 1], 0),
                            in_=xhat_tm[par][:], in_offset=None)),
                            deps=[tci, tln] + (b_xgd[sp].write_deps() if tt == 0 and k == 0 else [])
                                 + ([b_xgd[sp].w] if b_xgd[sp].w else []),
                            ch=f"sx{sp}{par}")
                        b_xhtm[par].read(tsc_)
                        b_dk[sp][tt].read(tsc_)
                        b_xgd[sp].wrote(tsc_)
                        scat_last[sp][par] = tsc_

                def gate_burst(st_i):
                    for tt in range(NT_ST):
                        gate_tile(st_i, tt)

                def gate_tile(st_i, tt):
                    sp = st_i % 2
                    if True:
                        par = tt % 2
                        tmx = sch.op("v", (lambda v, sp=sp, par=par, tt=tt: v.reduce_max(
                            gsc[par][:, 0:1], glog_all[sp][:, tt * 16:(tt + 1) * 16],
                            axis=AX.X, negate=True)),
                            deps=b_cw[sp][tt].read_deps() + b_g16[par].write_deps())
                        b_cw[sp][tt].read(tmx)
                        texp = sch.op("s", (lambda s, sp=sp, par=par, tt=tt: s.activation(
                            gp16[par][:], glog_all[sp][:, tt * 16:(tt + 1) * 16],
                            AF.Exp, bias=gsc[par][:, 0:1],
                            accum_out=gsc[par][:, 1:2])), deps=[tmx])
                        b_cw[sp][tt].read(texp)
                        trc = sch.op("v", (lambda v, par=par: v.reciprocal(
                            gsc[par][:, 0:1], gsc[par][:, 1:2])), deps=[texp])
                        tsc = sch.op("v", (lambda v, par=par: v.tensor_scalar(
                            out=s16[par][:], in0=gp16[par][:], scalar1=gsc[par][:, 0:1],
                            scalar2=None, op0=ALU.mult)), deps=[trc])
                        if not sparse:
                            tdm = sch.op("v", (lambda v, par=par: v.max(
                                out=m8[par][:], in_=s16[par][:])), deps=[tsc])
                            tcw = sch.op("v", (lambda v, sp=sp, par=par, tt=tt: v.scalar_tensor_tensor(
                                out=cw_all[sp][:, tt * 16:(tt + 1) * 16], in0=s16[par][:],
                                scalar=m8[par][:, 1:2], in1=s16[par][:],
                                op0=ALU.is_ge, op1=ALU.mult)),
                                deps=[tdm] + b_cw[sp][tt].write_deps())
                            b_cw[sp][tt].wrote(tcw)
                            b_g16[par].wrote(tcw)
                            return
                        ms = (lambda sp=sp, tt=tt: mask_all[sp][:, tt * 16:(tt + 1) * 16])
                        te0 = sch.op("v", (lambda v, par=par: v.tensor_scalar(
                            out=gp16[par][:], in0=iotae[:], scalar1=16.0,
                            scalar2=None, op0=ALU.subtract)),
                            deps=[tsc, tok_const[0]])
                        te0b = sch.op("v", (lambda v, sp=sp, par=par, tt=tt: v.tensor_mul(
                            gp16[par][:], gp16[par][:],
                            mask_all[sp][:, tt * 16:(tt + 1) * 16])),
                            deps=[te0] + b_mall[sp][tt].read_deps())
                        b_mall[sp][tt].read(te0b)
                        te0c = sch.op("v", (lambda v, par=par: v.tensor_scalar(
                            out=gp16[par][:], in0=gp16[par][:], scalar1=16.0,
                            scalar2=None, op0=ALU.add)), deps=[te0b])
                        te1 = sch.op("v", (lambda v, par=par: v.tensor_reduce(
                            gsc[par][:, 2:3], gp16[par][:], axis=AX.X, op=ALU.min)),
                            deps=[te0c])
                        te2 = sch.op("v", (lambda v, par=par: v.tensor_scalar(
                            out=gp16[par][:], in0=iotae[:], scalar1=1.0,
                            scalar2=None, op0=ALU.add)), deps=[te1])
                        te2b = sch.op("v", (lambda v, sp=sp, par=par, tt=tt: v.tensor_mul(
                            gp16[par][:], gp16[par][:],
                            mask_all[sp][:, tt * 16:(tt + 1) * 16])),
                            deps=[te2])
                        b_mall[sp][tt].read(te2b)
                        te2c = sch.op("v", (lambda v, par=par: v.tensor_scalar(
                            out=gp16[par][:], in0=gp16[par][:], scalar1=1.0,
                            scalar2=None, op0=ALU.subtract)), deps=[te2b])
                        te3 = sch.op("v", (lambda v, par=par: v.reduce_max(
                            gsc[par][:, 3:4], gp16[par][:], axis=AX.X)), deps=[te2c])
                        tq1 = None
                        for k in range(2):
                            tq0 = sch.op("v", (lambda v, par=par, k=k: v.tensor_scalar(
                                out=gp16[par][:], in0=iotae[:],
                                scalar1=gsc[par][:, 2 + k:3 + k], scalar2=None,
                                op0=ALU.is_equal)),
                                deps=[te3] + ([tq1] if tq1 else []))
                            tq1 = sch.op("v", (lambda v, sp=sp, par=par, tt=tt, k=k: v.scalar_tensor_tensor(
                                out=m8[par][:, 0:8].to_broadcast([128, 8]) if False else gp16[par][:],
                                in0=gp16[par][:], scalar=1.0, in1=s16[par][:],
                                op0=ALU.bypass, op1=ALU.mult,
                                accum_out=cwk_all[sp][:, 2 * tt + k:2 * tt + k + 1])),
                                deps=[tq0] + b_cwk[sp][tt].write_deps())
                            b_cwk[sp][tt].wrote(tq1)
                        b_g16[par].wrote(tq1)

                def passC_tile(st_c, tt):
                    sp = st_c % 2
                    row0 = st_c * ST
                    par = tt % 2
                    r0 = row0 + tt * 128
                    gt = r0 // 128
                    dld = sch.dma("y", (lambda y, r0=r0, par=par: y.dma_start(
                        xctile[par][:], src_dram[r0:r0 + 128, :])),
                        deps=b_xc[par].write_deps() + ([src_toks[gt]] if src_toks else []),
                        ch=f"xc{par}")
                    b_xc[par].wrote(dld)
                    if not sparse:
                        tfin = sch.op("v", (lambda v, par=par, tt=tt: v.scalar_tensor_tensor(
                            out=xnew[par][:], in0=zacc[:, tt * D:(tt + 1) * D], scalar=0.5,
                            in1=xctile[par][:], op0=ALU.mult, op1=ALU.add)),
                            deps=[dld] + b_z[tt].read_deps() + b_z[tt].write_deps()
                                 + b_xn[par].write_deps())
                        b_xc[par].read(tfin)
                        b_z[tt].read(tfin)
                        b_xn[par].wrote(tfin)
                        dst = sch.dma("y", (lambda y, r0=r0, par=par: y.dma_start(
                            dst_dram[r0:r0 + 128, :], xnew[par][:])), deps=[tfin],
                            ch=f"so{par}")
                        b_xn[par].read(dst)
                        store_toks[dst_key][gt] = dst
                        return
                    tg0 = sch.dma("g", (lambda g, sp=sp, tt=tt, par=par: g.indirect_dma_start(
                        out=zk_sb[par][:], out_offset=None,
                        in_=zg_dram[sp], in_offset=IndirectOffsetOnAxis(
                            dk_all[sp][:, 2 * tt:2 * tt + 1], 0))),
                        deps=b_zk[par].write_deps()
                             + [t for t in zg_last[sp] if t]
                             + b_dk[sp][tt].read_deps(),
                        ch=f"zk{par}")
                    b_zgd[sp].read(tg0)
                    b_dk[sp][tt].read(tg0)
                    b_zk[par].wrote(tg0)
                    ta0 = sch.op("v", (lambda v, sp=sp, tt=tt, par=par: v.scalar_tensor_tensor(
                        out=xnew[par][:], in0=zk_sb[par][:],
                        scalar=cwk_all[sp][:, 2 * tt:2 * tt + 1],
                        in1=zacc[:, tt * D:(tt + 1) * D], op0=ALU.mult, op1=ALU.add)),
                        deps=[tg0] + b_z[tt].read_deps() + b_z[tt].write_deps()
                             + b_cwk[sp][tt].read_deps() + [b_cwk[sp][tt].w]
                             + b_xn[par].write_deps())
                    b_z[tt].read(ta0)
                    b_cwk[sp][tt].read(ta0)
                    b_zk[par].read(ta0)
                    b_xn[par].wrote(ta0)
                    tg1 = sch.dma("g", (lambda g, sp=sp, tt=tt, par=par: g.indirect_dma_start(
                        out=zk_sb[par][:], out_offset=None,
                        in_=zg_dram[sp], in_offset=IndirectOffsetOnAxis(
                            dk_all[sp][:, 2 * tt + 1:2 * tt + 2], 0))),
                        deps=[ta0] + [t for t in zg_last[sp] if t], ch=f"zk{par}")
                    b_zgd[sp].read(tg1)
                    b_zk[par].wrote(tg1)
                    ta1 = sch.op("v", (lambda v, sp=sp, tt=tt, par=par: v.scalar_tensor_tensor(
                        out=xnew[par][:], in0=zk_sb[par][:],
                        scalar=cwk_all[sp][:, 2 * tt + 1:2 * tt + 2],
                        in1=xnew[par][:], op0=ALU.mult, op1=ALU.add)),
                        deps=[tg1, ta0])
                    b_zk[par].read(ta1)
                    b_xn[par].wrote(ta1)
                    tfin = sch.op("v", (lambda v, par=par: v.scalar_tensor_tensor(
                        out=xnew[par][:], in0=xnew[par][:], scalar=0.5,
                        in1=xctile[par][:], op0=ALU.mult, op1=ALU.add)),
                        deps=[ta1, dld])
                    b_xc[par].read(tfin)
                    b_xn[par].wrote(tfin)
                    dst = sch.dma("y", (lambda y, r0=r0, par=par: y.dma_start(
                        dst_dram[r0:r0 + 128, :], xnew[par][:])), deps=[tfin],
                        ch=f"so{par}")
                    b_xn[par].read(dst)
                    store_toks[dst_key][gt] = dst

                if sparse:
                    tbi = sch.op("v", lambda v: v.memset(base_row[:1, :], 0.0))
                    b_baser.wrote(tbi)
                    if first:
                        tzb = sch.op("g", lambda g: g.memset(xg_sb[2][:], 0.0))
                        for bi in range(2):
                            tlast = None
                            for r in range(0, ZROW + 1, 128):
                                rows = min(128, ZROW + 1 - r)
                                tlast = sch.dma("s", (lambda y, bi=bi, r=r, rows=rows: y.dma_start(
                                    xg_dram[bi][r:r + rows, :], xg_sb[2][0:rows, :])),
                                    deps=[tzb], ch=f"zinit{bi}")
                            b_xgd[bi].wrote(tlast)
                            b_xg[2].read(tlast)
                    passA_front(0, 0)
                    passA_front(0, 1)
                    passA_back(0, 0)
                    passA_front(0, 2)
                    passA_back(0, 1)
                    passA_front(0, 3)
                else:
                    for tt in range(NT_ST):
                        passA_front(0, tt)
                        passA_back(0, tt)
                    gate_burst(0)

                for st_i in range(NST):
                    row0 = st_i * ST
                    sp = st_i % 2
                    if st_i == 0:
                        wdma(16 if sparse else 0, cast=first)
                        wdma(17 if sparse else 1, cast=first)

                    if sparse:
                        steps = ([(pe, tt) for pe in (16, 17) for tt in range(NT_ST)]
                                 + [(e, sl) for e in range(NEXP) for sl in range(CAPT)])
                    else:
                        steps = [(e, tt) for e in range(NPE) for tt in range(NT_ST)]
                    nsteps = len(steps)
                    tok_ld = {}
                    tok_trx = {}
                    tok_w1mm = {}
                    tok_hstt = {}
                    tok_hev = {}
                    tok_w2mm = {}

                    def src_rows(i):
                        e, sl = steps[i]
                        return xg_dram[sp][e * CAP + sl * 128: e * CAP + (sl + 1) * 128, :]

                    def emit_ldx(i):
                        e, sl = steps[i]
                        p = i % 3
                        deps = b_xg[p].write_deps() + [t for t in scat_last[sp] if t]
                        rows_ap = src_rows(i)
                        tk = sch.dma("y", (lambda y, p=p, rows_ap=rows_ap: y.dma_start(
                            xg_sb[p][:], rows_ap)), deps=deps, ch=f"xg{p}")
                        b_xg[p].wrote(tk)
                        b_xgd[sp].read(tk)
                        tok_ld[i] = tk

                    def emit_trx(i):
                        e, sl = steps[i]
                        p3 = i % 3
                        p = i % 2
                        tk = sch.op("t", (lambda t, p=p, p3=p3: [
                            t.transpose(hT_ps[p][:, c * 128:(c + 1) * 128],
                                        xg_sb[p3][:, c * 128:(c + 1) * 128],
                                        ident[:]) for c in range(KC)][-1]),
                            deps=[tok_ld[i]] + b_hTps[p].write_deps() + tok_const)
                        b_xg[p3].read(tk)
                        b_hTps[p].wrote(tk)
                        tev = sch.op("v", (lambda v, p=p: v.tensor_copy(
                            xgT[p][:], hT_ps[p][:])),
                            deps=[tk] + b_xgT[p].write_deps())
                        b_hTps[p].read(tev)
                        b_xgT[p].wrote(tev)
                        tok_trx[i] = tev

                    def emit_w1mm(i):
                        e, sl = steps[i]
                        p = i % 2
                        sl_w = e % 2
                        if sparse and e < NEXP:
                            deps = (b_yps[p].write_deps() + b_w1[sl_w].read_deps() + [tok_trx[i]])
                            tk = sch.op("t", (lambda t, p=p, sl_w=sl_w: [
                                t.matmul(y_ps[p][:, n * 512:(n + 1) * 512],
                                         xgT[p][:, k * 128:(k + 1) * 128],
                                         w1s[sl_w][:, k * 1024 + n * 512:k * 1024 + (n + 1) * 512],
                                         start=(k == 0), stop=(k == KC - 1))
                                for k in range(KC) for n in range(2)][-1]), deps=deps)
                            b_xgT[p].read(tk)
                        else:
                            tt = sl
                            deps = (b_yps[p].write_deps() + b_w1[sl_w].read_deps()
                                    + b_xfm[sp][tt].read_deps())
                            tk = sch.op("t", (lambda t, p=p, sl_w=sl_w, sp=sp, tt=tt: [
                                t.matmul(y_ps[p][:, n * 512:(n + 1) * 512],
                                         xhat_fm[sp][:, k * ST + tt * 128:k * ST + (tt + 1) * 128],
                                         w1s[sl_w][:, k * 1024 + n * 512:k * 1024 + (n + 1) * 512],
                                         start=(k == 0), stop=(k == KC - 1))
                                for k in range(KC) for n in range(2)][-1]), deps=deps)
                            b_xfm[sp][tt].read(tk)
                        tok_w1mm[i] = tk
                        b_yps[p].wrote(tk)
                        b_w1[sl_w].read(tk)

                    def emit_gating(i):
                        e, sl = steps[i]
                        p = i % 2
                        tsil = sch.op("s", (lambda s, p=p: s.activation(
                            sg[p][:], y_ps[p][:, 512:1024], AF.Silu)),
                            deps=[tok_w1mm[i]] + b_sg[p].write_deps())
                        b_yps[p].read(tsil)
                        b_sg[p].wrote(tsil)
                        use_cw = (not sparse) and e < NEXP
                        th = sch.op("v", (lambda v, p=p, sp=sp, e=e, sl=sl, use_cw=use_cw: v.scalar_tensor_tensor(
                            out=h_bf[p][:], in0=y_ps[p][:, 0:512],
                            scalar=(cw_all[sp][:, sl * 16 + e:sl * 16 + e + 1] if use_cw else 1.0),
                            in1=sg[p][:], op0=ALU.mult, op1=ALU.mult)),
                            deps=[tsil, tok_w1mm[i]] + b_h[p].write_deps()
                                 + (b_cw[sp][sl].read_deps() + [b_cw[sp][sl].w] if use_cw else []))
                        b_yps[p].read(th)
                        b_sg[p].read(th)
                        b_h[p].wrote(th)
                        if use_cw:
                            b_cw[sp][sl].read(th)
                        tok_hstt[i] = th

                    def emit_tr(i):
                        e, sl = steps[i]
                        p = i % 2
                        p3 = i % 3
                        tk = sch.op("t", (lambda t, p=p: [
                            t.transpose(hT_ps[p][:, j * 128:(j + 1) * 128],
                                        h_bf[p][:, j * 128:(j + 1) * 128], ident[:])
                            for j in range(4)][-1]),
                            deps=[tok_hstt[i]] + b_hTps[p].write_deps())
                        b_h[p].read(tk)
                        b_hTps[p].wrote(tk)
                        tev = sch.op("v", (lambda v, p=p, p3=p3: v.tensor_copy(
                            hT_sb[p3][:], hT_ps[p][:, 0:512])),
                            deps=[tk] + b_hTsb[p3].write_deps())
                        b_hTps[p].read(tev)
                        b_hTsb[p3].wrote(tev)
                        tok_hev[i] = tev

                    def emit_w2mm(i):
                        e, sl = steps[i]
                        p3 = i % 3
                        sl_w = e % 2
                        deps = ([tok_hev[i]] + b_w2o.write_deps() + b_w2[sl_w].read_deps())
                        tk = sch.op("t", (lambda t, p3=p3, sl_w=sl_w: [
                            t.matmul(w2o[:, n * 512:(n + 1) * 512],
                                     hT_sb[p3][:, k * 128:(k + 1) * 128],
                                     w2s[sl_w][:, k * 1024 + n * 512:k * 1024 + (n + 1) * 512],
                                     start=(k == 0), stop=(k == 3))
                            for k in range(4) for n in range(2)][-1]), deps=deps)
                        tok_w2mm[i] = tk
                        b_hTsb[p3].read(tk)
                        b_w2[sl_w].read(tk)
                        b_w2o.wrote(tk)

                    def emit_sink(i):
                        e, sl = steps[i]
                        p = i % 2
                        if sparse and e < NEXP:
                            tk = sch.op("v", (lambda v, p=p: v.tensor_copy(
                                zg_sb[p][:], w2o[:])),
                                deps=[tok_w2mm[i]] + b_zg[p].write_deps())
                            b_w2o.read(tk)
                            b_zg[p].wrote(tk)
                            td = sch.dma("s", (lambda y, p=p, e=e, sl=sl, sp=sp: y.dma_start(
                                zg_dram[sp][e * CAP + sl * 128:e * CAP + (sl + 1) * 128, :],
                                zg_sb[p][:])),
                                deps=[tk] + (b_zgd[sp].write_deps() if (e == 0 and sl == 0) else []),
                                ch=f"zg{sp}{p}")
                            b_zg[p].read(td)
                            b_zgd[sp].wrote(td)
                            zg_last[sp][p] = td
                        else:
                            tt = sl
                            zslv = zacc[:, tt * D:(tt + 1) * D]
                            if e == (16 if sparse else 0):
                                tk = sch.op("v", (lambda v, zslv=zslv: v.tensor_copy(zslv, w2o[:])),
                                            deps=[tok_w2mm[i]] + b_z[tt].write_deps())
                            else:
                                tk = sch.op("v", (lambda v, zslv=zslv: v.tensor_add(zslv, zslv, w2o[:])),
                                            deps=[tok_w2mm[i]] + b_z[tt].read_deps()
                                                 + b_z[tt].write_deps())
                            b_z[tt].wrote(tk)
                            b_w2o.read(tk)

                    fr_next = [0]
                    bk_next = [0]
                    cur_fr = [4]
                    cur_bk = [2]
                    for i in range(nsteps):
                        e, sl = steps[i]
                        if st_i > 0 and i < NT_ST:
                            passC_tile(st_i - 1, i)
                        if sparse and st_i == 0:
                            if cur_bk[0] < NT_ST:
                                passA_back(0, cur_bk[0])
                                cur_bk[0] += 1
                            if cur_fr[0] < NT_ST:
                                passA_front(0, cur_fr[0])
                                cur_fr[0] += 1
                        if sparse and 18 <= i < 18 + NT_ST:
                            gate_tile(st_i, i - 18)
                        if (not sparse) and st_i + 1 < NST and 250 <= i < 250 + NT_ST:
                            gate_tile(st_i + 1, i - 250)
                        if st_i + 1 < NST and i >= (16 if sparse else 2):
                            if i % 3 == 0 and fr_next[0] < NT_ST:
                                passA_front(st_i + 1, fr_next[0])
                                fr_next[0] += 1
                            if (i % 3 == 2 and bk_next[0] < NT_ST
                                    and (bk_next[0] <= fr_next[0] - 2
                                         or fr_next[0] == NT_ST)):
                                passA_back(st_i + 1, bk_next[0])
                                bk_next[0] += 1
                        if sparse and i + 2 < nsteps and steps[i + 2][0] < NEXP:
                            emit_ldx(i + 2)
                        emit_w1mm(i)
                        if sparse and i + 1 < nsteps and steps[i + 1][0] < NEXP:
                            emit_trx(i + 1)
                        if i >= 1:
                            emit_tr(i - 1)
                        if i >= 2:
                            emit_w2mm(i - 2)
                            emit_sink(i - 2)
                        emit_gating(i)
                        if sl == 1:
                            if sparse:
                                if e == 17:
                                    wdma(0, cast=(first and st_i == 0))
                                elif e < 15:
                                    wdma(e + 1, cast=(first and st_i == 0))
                            elif e >= 1 and e + 1 < NPE:
                                wdma(e + 1, cast=(first and st_i == 0))
                    emit_tr(nsteps - 1)
                    for i in (nsteps - 2, nsteps - 1):
                        emit_w2mm(i)
                        emit_sink(i)
                    if st_i + 1 < NST:
                        while fr_next[0] < NT_ST:
                            passA_front(st_i + 1, fr_next[0])
                            fr_next[0] += 1
                        while bk_next[0] < NT_ST:
                            passA_back(st_i + 1, bk_next[0])
                            bk_next[0] += 1
                        wdma(16 if sparse else 0, cast=False)
                        wdma(17 if sparse else 1, cast=False)

                for tt in range(NT_ST):
                    passC_tile(NST - 1, tt)
                sch.barrier()

        def emit_attn(src_dram, dst_dram, dst_key, src_toks):
            with ExitStack() as att:
                AE = att.enter_context
                wq_sb = AE(nc.sbuf_tensor(nm("wq_sb"), [128, KC * 1024], BF16))
                wk_sb = AE(nc.sbuf_tensor(nm("wk_sb"), [128, KC * 256], BF16))
                wv_sb = AE(nc.sbuf_tensor(nm("wv_sb"), [128, KC * 256], BF16))
                wo_sb = AE(nc.sbuf_tensor(nm("wo_sb"), [128, KC * 1024], BF16))
                bqs_b = AE(nc.sbuf_tensor(nm("bqs_b"), [128, D], F32))
                bk_b = AE(nc.sbuf_tensor(nm("bk_b"), [128, 256], F32))
                bv_b = AE(nc.sbuf_tensor(nm("bv_b"), [128, 256], F32))
                bo_b = AE(nc.sbuf_tensor(nm("bo_b"), [128, D], F32))
                g_b = AE(nc.sbuf_tensor(nm("g_b"), [128, D], F32))
                b_b = AE(nc.sbuf_tensor(nm("b_b"), [128, D], F32))
                gcolT = AE(nc.sbuf_tensor(nm("gcolT"), [128, KC], F32))
                bcolT = AE(nc.sbuf_tensor(nm("bcolT"), [128, KC], BF16))
                q_fm = AE(nc.sbuf_tensor(nm("q_fm"), [128, 16 * S], BF16))
                k_fm = AE(nc.sbuf_tensor(nm("k_fm"), [128, 4 * S], BF16))
                v_sb = AE(nc.sbuf_tensor(nm("v_sb"), [128, QT * 260], BF16))
                xtile = [AE(nc.sbuf_tensor(nm(f"axt{i}"), [128, D], F32)) for i in range(2)]
                xhat_tm = [AE(nc.sbuf_tensor(nm(f"axh{i}"), [128, D], BF16)) for i in range(2)]
                xaT = [AE(nc.sbuf_tensor(nm(f"xaT{i}"), [128, KC * 128], BF16)) for i in range(2)]
                q_tm = [AE(nc.sbuf_tensor(nm(f"q_tm{i}"), [128, D], BF16)) for i in range(2)]
                k_tm = [AE(nc.sbuf_tensor(nm(f"k_tm{i}"), [128, 256], BF16)) for i in range(2)]
                qn2_all = AE(nc.sbuf_tensor(nm("qn2_all"), [128, QT], F32))
                kn2_all = AE(nc.sbuf_tensor(nm("kn2_all"), [128, QT * 4], F32))
                qn8_all = AE(nc.sbuf_tensor(nm("qn8_all"), [128, QT], F32))
                qn8row = AE(nc.sbuf_tensor(nm("qn8row"), [1, S], BF16))
                knsc = AE(nc.sbuf_tensor(nm("knsc"), [128, 4], F32))
                pT_sb = [AE(nc.sbuf_tensor(nm(f"pT_sb{i}"), [128, S], BF16)) for i in range(3)]
                rinv = AE(nc.sbuf_tensor(nm("rinv"), [128, 16], F32))
                ao_sb = AE(nc.sbuf_tensor(nm("ao_sb"), [128, D], F32))
                t1_sb = AE(nc.sbuf_tensor(nm("t1_sb"), [128, D], F32))
                qsq = t1_sb
                t_bf = AE(nc.sbuf_tensor(nm("t_bf"), [128, D], BF16))
                tT_sb = AE(nc.sbuf_tensor(nm("tT_sb"), [128, D], BF16))
                xres = xtile
                xn1 = ao_sb
                xnew_s = AE(nc.sbuf_tensor(nm("axn"), [128, D], F32))
                xnew = [xnew_s, xnew_s]

                b_wq, b_wk, b_wv, b_wo = Buf(), Buf(), Buf(), Buf()
                sch.dma("g", lambda g: g.dma_start(
                    wq_sb[:].rearrange("p (c n) -> p c n", c=KC),
                    wq.rearrange("(c p) n -> p c n", p=128)), ch="wat")
                sch.dma("g", lambda g: g.dma_start(
                    wk_sb[:].rearrange("p (c n) -> p c n", c=KC),
                    wk.rearrange("(c p) n -> p c n", p=128)), ch="wat")
                sch.dma("g", lambda g: g.dma_start(
                    wv_sb[:].rearrange("p (c n) -> p c n", c=KC),
                    wv.rearrange("(c p) n -> p c n", p=128)), ch="wat")
                dw = sch.dma("g", lambda g: g.dma_start(
                    wo_sb[:].rearrange("p (c n) -> p c n", c=KC),
                    wo.rearrange("(c p) n -> p c n", p=128)), ch="wat")
                b_wq.wrote(dw); b_wk.wrote(dw); b_wv.wrote(dw); b_wo.wrote(dw)

                with nc.psum_tensor(nm("bc_ps"), [128, 512], F32) as bc_ps, \
                     nc.sbuf_tensor(nm("brow"), [1, D], F32) as brow:
                    b_rows2, b_bcps = Buf(), Buf()

                    def bcast2(dst, row_dram, n, scale=None):
                        dld = sch.dma("g", (lambda g, n=n, row_dram=row_dram: g.dma_start(
                            brow[:1, 0:n], row_dram)), deps=b_rows2.write_deps(), ch="drw")
                        b_rows2.wrote(dld)
                        if scale is not None:
                            dsc = sch.op("s", (lambda s, n=n, scale=scale: s.activation(
                                brow[:1, 0:n], brow[:1, 0:n], AF.Identity,
                                bias=f32(0.0), scale=f32(scale))), deps=[dld])
                            b_rows2.wrote(dsc)
                            src_tok = dsc
                        else:
                            src_tok = dld
                        last = None
                        for j in range((n + 511) // 512):
                            w = min(512, n - j * 512)
                            tmm = sch.op("t", (lambda t, j=j, w=w: t.matmul(
                                bc_ps[:, 0:w], ones_row[:1, :],
                                brow[:1, j * 512:j * 512 + w], start=True, stop=True)),
                                deps=[src_tok, tok_const[0]] + b_bcps.write_deps())
                            b_rows2.read(tmm)
                            b_bcps.wrote(tmm)
                            tev = sch.op("v", (lambda v, j=j, w=w, dst=dst: v.tensor_copy(
                                dst[:, j * 512:j * 512 + w], bc_ps[:, 0:w])), deps=[tmm])
                            b_bcps.read(tev)
                            last = tev
                        return last

                    bcast2(bqs_b, bq, D, scale=0.125)
                    bcast2(bk_b, bk, 256)
                    bcast2(bv_b, bv, 256)
                    tok_gb = bcast2(g_b, ln_g, D)
                    tok_bb = bcast2(b_b, ln_b, D)
                    tgc = [None] * KC
                    tbc = [None] * KC
                    for k in range(KC):
                        for src_b, colT, toks, tarr in ((g_b, gcolT, tok_gb, tgc),
                                                       (b_b, bcolT, tok_bb, tbc)):
                            ttr = sch.op("t", (lambda t, k=k, src_b=src_b: t.transpose(
                                bc_ps[:, 0:128], src_b[:, k * 128:(k + 1) * 128],
                                ident_f[:])),
                                deps=[toks, tok_const[3]] + b_bcps.write_deps())
                            b_bcps.wrote(ttr)
                            tev = sch.op("v", (lambda v, k=k, colT=colT: v.tensor_copy(
                                colT[:, k:k + 1], bc_ps[:, 0:1])), deps=[ttr])
                            b_bcps.read(tev)
                            tarr[k] = tev
                    dbo = sch.dma("g", lambda g: g.dma_start(
                        brow[:1, 0:D], bo), deps=b_rows2.write_deps(), ch="drw")
                    b_rows2.wrote(dbo)
                    tok_bor = None
                    for n in range(2):
                        tmmb = sch.op("t", (lambda t, n=n: [
                            t.matmul(bc_ps[0:1, 0:512], bcolT[:, k:k + 1],
                                     wo_sb[:, k * 1024 + n * 512:k * 1024 + (n + 1) * 512],
                                     start=(k == 0), stop=(k == KC - 1))
                            for k in range(KC)][-1]),
                            deps=tbc + [dw, dbo] + b_bcps.write_deps())
                        b_bcps.wrote(tmmb)
                        b_wo.read(tmmb)
                        tadd = sch.op("v", (lambda v, n=n: v.tensor_add(
                            brow[0:1, n * 512:(n + 1) * 512], bc_ps[0:1, 0:512],
                            brow[0:1, n * 512:(n + 1) * 512])), deps=[tmmb])
                        b_bcps.read(tadd)
                        b_rows2.wrote(tadd)
                        tok_bor = tadd
                    for j in range(2):
                        tmm = sch.op("t", (lambda t, j=j: t.matmul(
                            bc_ps[:, 0:512], ones_row[:1, :],
                            brow[:1, j * 512:j * 512 + 512], start=True, stop=True)),
                            deps=[tok_bor, tok_const[0]] + b_bcps.write_deps())
                        b_rows2.read(tmm)
                        b_bcps.wrote(tmm)
                        tok_bc = sch.op("v", (lambda v, j=j: v.tensor_copy(
                            bo_b[:, j * 512:j * 512 + 512], bc_ps[:, 0:512])), deps=[tmm])
                        b_bcps.read(tok_bc)
                    tok_wos = None
                    for k in range(KC):
                        tok_wos = sch.op("v", (lambda v, k=k: v.tensor_scalar(
                            out=wo_sb[:, k * 1024:(k + 1) * 1024],
                            in0=wo_sb[:, k * 1024:(k + 1) * 1024],
                            scalar1=gcolT[:, k:k + 1], scalar2=None, op0=ALU.mult)),
                            deps=[tgc[k], dw] + b_wo.read_deps() + b_wo.write_deps())
                    b_wo.wrote(tok_wos)

                b_vsb = Buf()
                b_qfm = Buf()
                b_kfm = Buf()
                gctr = [0]

                for seq in range(SEQ_PER_CORE):
                    row0 = seq * S
                    b_xt = [Buf(), Buf()]
                    b_xhtm = [Buf(), Buf()]
                    b_xaT = [Buf(), Buf()]
                    b_qtm = [Buf(), Buf()]
                    b_ktm = [Buf(), Buf()]
                    b_qsq = Buf()
                    tok_ke = [None] * QT
                    tok_ve = [None] * QT
                    tok_qe = [None] * QT
                    with nc.psum_tensor(nm("xT_ps0"), [128, 1024], BF16) as xT_ps0, \
                         nc.psum_tensor(nm("xT_ps1"), [128, 1024], BF16) as xT_ps1, \
                         nc.psum_tensor(nm("q_ps"), [128, 1024], F32) as q_ps, \
                         nc.psum_tensor(nm("qT_ps"), [128, 1024], BF16) as qT_ps, \
                         nc.psum_tensor(nm("kv_ps"), [128, 512], F32) as kv_ps, \
                         nc.psum_tensor(nm("kT_ps"), [128, 256], BF16) as kT_ps, \
                         nc.psum_tensor(nm("sp_ps"), [128, 128], F32) as sp_ps:
                        xT_ps = [xT_ps0, xT_ps1]
                        b_xTps = [Buf(), Buf()]
                        b_qps, b_qTps, b_kvps, b_kTps, b_spps = \
                            Buf(), Buf(), Buf(), Buf(), Buf()
                        tko = sch.op("g", (lambda g: g.memset(
                            v_sb[:].rearrange("p (t h c) -> p t h c", h=4, c=65)[:, :, :, 64:65],
                            1.0)), deps=b_vsb.write_deps())
                        b_vsb.wrote(tko)
                        tkon = sch.op("g", (lambda g: g.memset(k_fm[64:65, :], 1.0)),
                                      deps=b_kfm.write_deps())
                        b_kfm.wrote(tkon)
                        tok_ln = [None] * QT

                        def p1_ln(tt):
                            par = tt % 2
                            r0 = row0 + tt * 128
                            gt = r0 // 128
                            dld = sch.dma("y", (lambda y, r0=r0, par=par: y.dma_start(
                                xtile[par][:], src_dram[r0:r0 + 128, :])),
                                deps=b_xt[par].write_deps() + ([src_toks[gt]] if src_toks else []),
                                ch=f"xa{par}")
                            b_xt[par].wrote(dld)
                            tln, _, lntoks = emit_ln(par, xtile[par][:], xhat_tm[par][:],
                                                     [dld] + b_xhtm[par].write_deps())
                            for tk in lntoks[:3]:
                                b_xt[par].read(tk)
                            b_xt[par].read(tln)
                            b_xhtm[par].wrote(tln)
                            tok_ln[tt] = tln

                        tok_ev = [None] * QT

                        def p1_stage1(tt):
                            par = tt % 2
                            tln = tok_ln[tt]
                            ttr = sch.op("t", (lambda t, par=par: [
                                t.transpose(xT_ps[par][:, c * 128:(c + 1) * 128],
                                            xhat_tm[par][:, c * 128:(c + 1) * 128],
                                            ident[:]) for c in range(KC)][-1]),
                                deps=[tln] + b_xTps[par].write_deps() + tok_const)
                            b_xhtm[par].read(ttr)
                            b_xTps[par].wrote(ttr)
                            tev = sch.op("v", (lambda v, par=par: v.tensor_copy(
                                xaT[par][:], xT_ps[par][:])),
                                deps=[ttr] + b_xaT[par].write_deps())
                            b_xTps[par].read(tev)
                            b_xaT[par].wrote(tev)
                            tok_ev[tt] = tev

                        p1_ln(0)
                        p1_ln(1)
                        p1_stage1(0)
                        for tt in range(QT):
                            par = tt % 2
                            r0 = row0 + tt * 128
                            gt = r0 // 128
                            if tt + 2 < QT:
                                p1_ln(tt + 2)
                            if tt + 1 < QT:
                                p1_stage1(tt + 1)
                            tev = tok_ev[tt]
                            tqm = sch.op("t", (lambda t, par=par: [
                                t.matmul(q_ps[:, n * 512:(n + 1) * 512],
                                         xaT[par][:, k * 128:(k + 1) * 128],
                                         wq_sb[:, k * 1024 + n * 512:k * 1024 + (n + 1) * 512],
                                         start=(k == 0), stop=(k == KC - 1))
                                for k in range(KC) for n in range(2)][-1]),
                                deps=[tev] + b_qps.write_deps() + b_wq.read_deps())
                            b_xaT[par].read(tqm)
                            b_wq.read(tqm)
                            b_qps.wrote(tqm)
                            tqs = sch.op("v", (lambda v, par=par: v.scalar_tensor_tensor(
                                out=q_tm[par][:], in0=q_ps[:], scalar=0.125,
                                in1=bqs_b[:], op0=ALU.mult, op1=ALU.add)),
                                deps=[tqm, tok_bc] + b_qtm[par].write_deps())
                            b_qps.read(tqs)
                            b_qtm[par].wrote(tqs)
                            tqn = sch.op("v", (lambda v, par=par, tt=tt: v.scalar_tensor_tensor(
                                out=qsq[:], in0=q_tm[par][:], scalar=1.0, in1=q_tm[par][:],
                                op0=ALU.bypass, op1=ALU.mult,
                                accum_out=qn2_all[:, tt:tt + 1])),
                                deps=[tqs] + b_qsq.write_deps())
                            b_qtm[par].read(tqn)
                            b_qsq.wrote(tqn)
                            tqt = sch.op("t", (lambda t, par=par: [
                                t.transpose(qT_ps[64 * (hq % 2):64 * (hq % 2) + 64,
                                                  (hq // 2) * 128:(hq // 2) * 128 + 128],
                                            q_tm[par][:, hq * 64:(hq + 1) * 64],
                                            ident[:]) for hq in range(16)][-1]),
                                deps=[tqs] + b_qTps.write_deps())
                            b_qtm[par].read(tqt)
                            b_qTps.wrote(tqt)
                            tqe = sch.op("v", (lambda v, tt=tt: [
                                v.tensor_copy(
                                    q_fm[0:64, :].rearrange("p (h n) -> p h n", n=S)[:, 0::2, tt * 128:(tt + 1) * 128],
                                    qT_ps[0:64, :].rearrange("p (h n) -> p h n", n=128)),
                                v.tensor_copy(
                                    q_fm[0:64, :].rearrange("p (h n) -> p h n", n=S)[:, 1::2, tt * 128:(tt + 1) * 128],
                                    qT_ps[64:128, :].rearrange("p (h n) -> p h n", n=128))][-1]),
                                deps=[tqt] + b_qfm.write_deps())
                            b_qTps.read(tqe)
                            tok_qe[tt] = tqe
                            tkm = sch.op("t", (lambda t, par=par: [
                                t.matmul(kv_ps[:, 0:256],
                                         xaT[par][:, k * 128:(k + 1) * 128],
                                         wk_sb[:, k * 256:(k + 1) * 256],
                                         start=(k == 0), stop=(k == KC - 1))
                                for k in range(KC)][-1]),
                                deps=[tev] + b_kvps.write_deps() + b_wk.read_deps())
                            b_xaT[par].read(tkm)
                            b_wk.read(tkm)
                            b_kvps.wrote(tkm)
                            tks = sch.op("v", (lambda v, par=par: v.scalar_tensor_tensor(
                                out=k_tm[par][:], in0=kv_ps[:, 0:256], scalar=1.0,
                                in1=bk_b[:], op0=ALU.mult, op1=ALU.add)),
                                deps=[tkm] + b_ktm[par].write_deps())
                            b_kvps.read(tks)
                            b_ktm[par].wrote(tks)
                            tkn = None
                            for h in range(4):
                                tkn = sch.op("v", (lambda v, par=par, tt=tt, h=h: v.scalar_tensor_tensor(
                                    out=qsq[:, h * 64:(h + 1) * 64],
                                    in0=k_tm[par][:, h * 64:(h + 1) * 64], scalar=1.0,
                                    in1=k_tm[par][:, h * 64:(h + 1) * 64],
                                    op0=ALU.bypass, op1=ALU.mult,
                                    accum_out=kn2_all[:, tt * 4 + h:tt * 4 + h + 1])),
                                    deps=[tks] + ([] if h else b_qsq.write_deps()))
                                b_ktm[par].read(tkn)
                            b_qsq.wrote(tkn)
                            tkt = sch.op("t", (lambda t, par=par: [
                                t.transpose(kT_ps[64 * (h % 2):64 * (h % 2) + 64,
                                                  (h // 2) * 128:(h // 2) * 128 + 128],
                                            k_tm[par][:, h * 64:(h + 1) * 64],
                                            ident[:]) for h in range(4)][-1]),
                                deps=[tks] + b_kTps.write_deps())
                            b_ktm[par].read(tkt)
                            b_kTps.wrote(tkt)
                            tke = sch.op("v", (lambda v, tt=tt: [
                                v.tensor_copy(
                                    k_fm[0:64, :].rearrange("p (h n) -> p h n", n=S)[:, 0::2, tt * 128:(tt + 1) * 128],
                                    kT_ps[0:64, 0:256].rearrange("p (h n) -> p h n", n=128)),
                                v.tensor_copy(
                                    k_fm[0:64, :].rearrange("p (h n) -> p h n", n=S)[:, 1::2, tt * 128:(tt + 1) * 128],
                                    kT_ps[64:128, 0:256].rearrange("p (h n) -> p h n", n=128))][-1]),
                                deps=[tkt] + b_kfm.write_deps())
                            b_kTps.read(tke)
                            tok_ke[tt] = tke
                            tvm = sch.op("t", (lambda t, par=par: [
                                t.matmul(kv_ps[:, 256:512],
                                         xaT[par][:, k * 128:(k + 1) * 128],
                                         wv_sb[:, k * 256:(k + 1) * 256],
                                         start=(k == 0), stop=(k == KC - 1))
                                for k in range(KC)][-1]),
                                deps=[tev, tks] + b_wv.read_deps())
                            b_xaT[par].read(tvm)
                            b_wv.read(tvm)
                            tvs = sch.op("v", (lambda v, tt=tt: v.scalar_tensor_tensor(
                                out=v_sb[:, tt * 260:(tt + 1) * 260].rearrange(
                                    "p (h c) -> p h c", c=65)[:, :, 0:64],
                                in0=kv_ps[:, 256:512].rearrange("p (h c) -> p h c", c=64),
                                scalar=1.0,
                                in1=bv_b[:].rearrange("p (h c) -> p h c", c=64),
                                op0=ALU.mult, op1=ALU.add)),
                                deps=[tvm, tko])
                            b_kvps.read(tvs)
                            b_kvps.wrote(tvm)
                            b_vsb.read(tvs)
                            tok_ve[tt] = tvs

                        tl1 = sch.op("s", (lambda s: s.activation(
                            qn8_all[:], qn2_all[:], AF.Ln)),
                            deps=[b_qsq.w, tok_qe[QT - 1]] + [t for t in [b_qsq.w] if t])
                        tl2 = sch.op("s", (lambda s: s.activation(
                            qn8_all[:], qn8_all[:], AF.Exp, bias=f32(-2.0794415416798357),
                            scale=f32(0.5))), deps=[tl1])
                        tok_qn8r = None
                        for tt in range(QT):
                            ttr2 = sch.op("t", (lambda t, tt=tt: t.transpose(
                                sp_ps[0:1, 0:128], qn8_all[:, tt:tt + 1], ident_f[:])),
                                deps=[tl2] + b_spps.write_deps() + tok_const)
                            b_spps.wrote(ttr2)
                            tev2 = sch.op("v", (lambda v, tt=tt: v.tensor_copy(
                                qn8row[0:1, tt * 128:(tt + 1) * 128], sp_ps[0:1, 0:128])),
                                deps=[ttr2])
                            b_spps.read(tev2)
                            tok_qn8r = tev2
                        tk1 = sch.op("v", (lambda v: v.reduce_max(
                            knsc[:, 0:1], kn2_all[:], axis=AX.X)), deps=[b_qsq.w])
                        tk2 = sch.op("t", (lambda t: t.transpose(
                            sp_ps[0:1, 0:128], knsc[:, 0:1], ident_f[:])),
                            deps=[tk1] + b_spps.write_deps())
                        b_spps.wrote(tk2)
                        tk4 = sch.op("v", (lambda v: v.reduce_max(
                            knsc[0:1, 1:2], sp_ps[0:1, 0:128], axis=AX.X)),
                            deps=[tk2])
                        b_spps.read(tk4)
                        tk5 = sch.op("s", (lambda s: s.activation(
                            knsc[0:1, 2:3], knsc[0:1, 1:2], AF.Ln)), deps=[tk4])
                        tk6 = sch.op("s", (lambda s: s.activation(
                            knsc[0:1, 2:3], knsc[0:1, 2:3], AF.Exp, bias=f32(0.0),
                            scale=f32(0.5))), deps=[tk5])
                        tk7 = sch.op("v", (lambda v: v.tensor_scalar_mul(
                            knsc[0:1, 3:4], knsc[0:1, 2:3], -1.0)), deps=[tk6])
                        tok_aug = None
                        for hq in range(16):
                            tok_aug = sch.op("v", (lambda v, hq=hq: v.tensor_scalar(
                                out=q_fm[64:65, hq * S:(hq + 1) * S], in0=qn8row[0:1, :],
                                scalar1=knsc[0:1, 3:4], scalar2=None, op0=ALU.mult)),
                                deps=[tk7, tok_qn8r] + (b_qfm.write_deps() if hq == 0 else []))
                        b_qfm.wrote(tok_aug)
                        for tt in range(QT):
                            b_qfm.read(tok_qe[tt])
                        sch.barrier()

                    with nc.psum_tensor(nm("sT_ps"), [128, 2048], F32) as sT_ps, \
                         nc.psum_tensor(nm("ao_ps"), [128, 3 * 512], F32) as ao_ps, \
                         nc.psum_tensor(nm("tT_ps"), [128, 1024], BF16) as tT_ps:
                        NSTB = 4
                        sT = [sT_ps[:, i * 512:(i + 1) * 512] for i in range(NSTB)]
                        wo_ps = ao_ps[:, 0:1024]
                        b_sT = [Buf() for _ in range(NSTB)]
                        b_aops, b_tTps = Buf(), Buf()
                        b_wops = b_aops
                        b_pT = [Buf(), Buf(), Buf()]
                        b_rinv, b_ao = Buf(), Buf()
                        b_t1, b_t2, b_tbf, b_tT = Buf(), Buf(), Buf(), Buf()
                        b_xres = [Buf(), Buf()]
                        b_xn1 = Buf()
                        _bxn = Buf()
                        b_xn = [_bxn, _bxn]

                        def _ao_data_view():
                            return ao_ps[:].rearrange(
                                "p (b c) -> p b c", c=512).rearrange(
                                "p b (h c) -> p b h c", c=65)[:, :, :, 0:64].rearrange(
                                "p b h c -> p (b h) c")[:, 0:16, :] if False else None

                        p3b_pend = [None]

                        def emit_p3b():
                            if p3b_pend[0] is None:
                                return
                            tq, r0, gt, tb, drd = p3b_pend[0]
                            p3b_pend[0] = None
                            ttt = sch.op("t", (lambda t: [
                                t.transpose(tT_ps[:, c * 128:(c + 1) * 128],
                                            t_bf[:, c * 128:(c + 1) * 128], ident[:])
                                for c in range(KC)][-1]),
                                deps=[tb] + b_tTps.write_deps())
                            b_tbf.read(ttt)
                            b_tTps.wrote(ttt)
                            tte = sch.op("v", (lambda v: v.tensor_copy(
                                tT_sb[:], tT_ps[:])),
                                deps=[ttt] + b_tT.write_deps())
                            b_tTps.read(tte)
                            b_tT.wrote(tte)
                            twm = sch.op("t", (lambda t: [
                                t.matmul(wo_ps[:, n * 512:(n + 1) * 512],
                                         tT_sb[:, k * 128:(k + 1) * 128],
                                         wo_sb[:, k * 1024 + n * 512:k * 1024 + (n + 1) * 512],
                                         start=(k == 0), stop=(k == KC - 1))
                                for k in range(KC) for n in range(2)][-1]),
                                deps=[tte] + b_wops.write_deps() + b_wo.read_deps())
                            b_tT.read(twm)
                            b_wo.read(twm)
                            b_wops.wrote(twm)
                            tr1 = sch.op("v", (lambda v: v.scalar_tensor_tensor(
                                out=xn1[:], in0=wo_ps[:], scalar=1.0, in1=bo_b[:],
                                op0=ALU.mult, op1=ALU.add)),
                                deps=[twm] + b_xn1.write_deps() + b_ao.write_deps())
                            b_wops.read(tr1)
                            b_xn1.wrote(tr1)
                            b_ao.wrote(tr1)
                            tr2 = sch.op("v", (lambda v, tq=tq: v.tensor_add(
                                xnew[tq % 2][:], xn1[:], xres[tq % 2][:])),
                                deps=[tr1, drd] + b_xn[tq % 2].write_deps())
                            b_xn1.read(tr2)
                            b_xres[tq % 2].read(tr2)
                            b_xn[tq % 2].wrote(tr2)
                            dstk = sch.dma("y", (lambda y, r0=r0, tq=tq: y.dma_start(
                                dst_dram[r0:r0 + 128, :], xnew[tq % 2][:])), deps=[tr2],
                                ch=f"so{tq % 2}")
                            b_xn[tq % 2].read(dstk)
                            store_toks[dst_key][gt] = dstk

                        for tq in range(QT):
                            r0 = row0 + tq * 128
                            gt = r0 // 128
                            nk = tq + 1
                            drd = sch.dma("y", (lambda y, r0=r0, tq=tq: y.dma_start(
                                xres[tq % 2][:], src_dram[r0:r0 + 128, :])),
                                deps=b_xres[tq % 2].write_deps() + ([src_toks[gt]] if src_toks else []),
                                ch=f"xc{tq % 2}")
                            b_xres[tq % 2].wrote(drd)

                            ng = (nk + 3) // 4
                            items = [(hq, gi) for hq in range(16) for gi in range(ng)]
                            tok_exp = {}
                            tok_pv_last = None

                            def emit_pv(j):
                                nonlocal tok_pv_last
                                hq, gi = items[j]
                                h_ = hq % 4
                                hpar = hq % 3
                                c0 = 4 * gi
                                c1 = min(c0 + 4, nk)
                                off = (hq // 7) * 512 + (hq % 7) * 65
                                tk = sch.op("t", (lambda t, hq=hq, h_=h_, hpar=hpar,
                                                  c0=c0, c1=c1, nk=nk, off=off: [
                                    t.matmul(ao_ps[:, off:off + 65],
                                             pT_sb[hpar][:, c * 128:(c + 1) * 128],
                                             v_sb[:, c * 260 + h_ * 65:c * 260 + (h_ + 1) * 65],
                                             start=(c == 0), stop=(c == nk - 1))
                                    for c in range(c0, c1)][-1]),
                                    deps=[tok_exp[j], tok_ve[min(c1 - 1, nk - 1)]]
                                         + (b_aops.write_deps() if (hq == 0 and gi == 0) else []))
                                if c1 == nk:
                                    b_pT[hpar].read(tk)
                                tok_pv_last = tk

                            for j, (hq, gi) in enumerate(items):
                                if j == min(2, len(items) - 1):
                                    emit_p3b()
                                if j >= 2:
                                    emit_pv(j - 2)
                                h_ = hq % 4
                                hpar = hq % 3
                                c0 = 4 * gi
                                c1 = min(c0 + 4, nk)
                                gpar = gctr[0] % NSTB
                                gctr[0] += 1
                                tmm = sch.op("t", (lambda t, hq=hq, h_=h_, tq=tq, c0=c0, c1=c1, gpar=gpar: [
                                    t.matmul(sT[gpar][:, (c - c0) * 128:(c - c0 + 1) * 128],
                                             k_fm[0:65, h_ * S + c * 128:h_ * S + (c + 1) * 128],
                                             q_fm[0:65, hq * S + tq * 128:hq * S + (tq + 1) * 128],
                                             start=True, stop=True)
                                    for c in range(c0, c1)][-1]),
                                    deps=[tok_ke[c1 - 1], tok_qe[tq], tok_aug, tkon]
                                         + b_sT[gpar].write_deps())
                                b_sT[gpar].wrote(tmm)
                                tlast = tmm
                                if c0 <= tq < c1:
                                    tmk = sch.op("v", (lambda v, tq=tq, c0=c0, gpar=gpar: v.tensor_add(
                                        sT[gpar][:, (tq - c0) * 128:(tq - c0 + 1) * 128],
                                        sT[gpar][:, (tq - c0) * 128:(tq - c0 + 1) * 128],
                                        maskdT[:])),
                                        deps=[tmm] + tok_const)
                                    b_sT[gpar].wrote(tmk)
                                    tlast = tmk
                                texp = sch.op("s", (lambda s, hpar=hpar, c0=c0, c1=c1, gpar=gpar: s.activation(
                                    pT_sb[hpar][:, c0 * 128:c1 * 128],
                                    sT[gpar][:, 0:(c1 - c0) * 128], AF.Exp)),
                                    deps=[tlast] + (b_pT[hpar].write_deps() if gi == 0 else []))
                                b_sT[gpar].read(texp)
                                if gi == 0:
                                    b_pT[hpar].wrote(texp)
                                tok_exp[j] = texp
                            if len(items) >= 2:
                                emit_pv(len(items) - 2)
                            emit_pv(len(items) - 1)
                            b_aops.wrote(tok_pv_last)

                            trc = sch.op("v", (lambda v: [
                                v.reciprocal(
                                    rinv[:, g * 7:g * 7 + (7 if g < 2 else 2)],
                                    ao_ps[:, g * 512:g * 512 + (7 if g < 2 else 2) * 65]
                                    .rearrange("p (h c) -> p h c", c=65)[:, :, 64:65])
                                for g in range(3)][-1]),
                                deps=[tok_pv_last] + b_rinv.write_deps())
                            b_aops.read(trc)
                            b_rinv.wrote(trc)
                            def _aoevac(v):
                                last = None
                                for g, ng in ((0, 7), (1, 7), (2, 2)):
                                    last = v.tensor_tensor(
                                        ao_sb[:, g * 448:g * 448 + ng * 64].rearrange(
                                            "p (h c) -> p h c", c=64),
                                        ao_ps[:, g * 512:g * 512 + ng * 65].rearrange(
                                            "p (h c) -> p h c", c=65)[:, :, 0:64],
                                        rinv[:, g * 7:g * 7 + ng].rearrange(
                                            "p (h o) -> p h o", o=1).to_broadcast(
                                            [128, ng, 64]),
                                        op=ALU.mult)
                                return last
                            tae = sch.op("v", _aoevac,
                                deps=[tok_pv_last, trc] + b_ao.write_deps())
                            b_aops.read(tae)
                            b_rinv.read(tae)
                            b_ao.wrote(tae)
                            tln, _, lntoks = emit_ln(tq % 2, ao_sb[:], t_bf[:],
                                                     [tae] + b_tbf.write_deps())
                            for tk in lntoks[:3]:
                                b_ao.read(tk)
                            b_ao.read(tln)
                            b_tbf.wrote(tln)
                            p3b_pend[0] = (tq, r0, gt, tln, drd)
                        emit_p3b()
                        sch.barrier()
                sch.barrier()

        if stage in ("f1", "f1dbg"):
            emit_ffn(x_in, out, "out", None, True, True)
        elif stage == "attn":
            emit_ffn(x_in, x1, "x1", None, True, True)
            emit_attn(x1, out, "out", store_toks["x1"])
        else:
            emit_ffn(x_in, x1, "x1", None, True, True)
            emit_attn(x1, x2, "x2", store_toks["x1"])
            emit_ffn(x2, out, "out", store_toks["x2"], False, False)

        with ExitStack() as semctx:
            semnames = sorted(set(list(sch.counts.keys()) + list(Sched.ENGS)))
            sems = {n: semctx.enter_context(nc.semaphore(f"sem_{n}")) for n in semnames}
            block = semctx.enter_context(nc.Block())

            def runner(name, eng):
                hw = {}
                for (e, waits, fn, amt, sem) in sch.recs:
                    if e != name:
                        continue
                    for (sname, val) in waits:
                        if hw.get(sname, 0) >= val:
                            continue
                        eng.wait_ge(sems[sname], val)
                        hw[sname] = val
                    if fn is not None:
                        inst = fn(eng)
                        if amt:
                            inst.then_inc(sems[sem], amt)
                if name == "y":
                    for sname, cnt in sorted(sch.counts.items()):
                        if sname not in Sched.ENGS and cnt:
                            eng.wait_ge(sems[sname], cnt)

            block.tensor(lambda t: runner("t", t))
            block.vector(lambda v: runner("v", v))
            block.scalar(lambda s: runner("s", s))
            block.gpsimd(lambda g: runner("g", g))
            block.sync(lambda y: runner("y", y))

    return nc


def kernel(L0_attn_x, L0_attn_Wq, L0_attn_bq, L0_attn_Wk, L0_attn_bk,
           L0_attn_Wv, L0_attn_bv, L0_attn_ln_g, L0_attn_ln_b, L0_attn_Wo,
           L0_attn_bo, L0_ffn_Wg, L0_ffn_W1, L0_ffn_W2, L0_ffn_Ws1, L0_ffn_Ws2,
           _stage="full"):
    x = np.ascontiguousarray(np.asarray(L0_attn_x, dtype=np.float32))
    shards = x.reshape(NCORES, TOK, D)
    common = {
        "Wq": np.ascontiguousarray(L0_attn_Wq, dtype=np.float32),
        "bq": np.ascontiguousarray(L0_attn_bq, dtype=np.float32).reshape(1, D),
        "Wk": np.ascontiguousarray(L0_attn_Wk, dtype=np.float32),
        "bk": np.ascontiguousarray(L0_attn_bk, dtype=np.float32).reshape(1, 256),
        "Wv": np.ascontiguousarray(L0_attn_Wv, dtype=np.float32),
        "bv": np.ascontiguousarray(L0_attn_bv, dtype=np.float32).reshape(1, 256),
        "ln_g": np.ascontiguousarray(L0_attn_ln_g, dtype=np.float32).reshape(1, D),
        "ln_b": np.ascontiguousarray(L0_attn_ln_b, dtype=np.float32).reshape(1, D),
        "Wo": np.ascontiguousarray(L0_attn_Wo, dtype=np.float32),
        "bo": np.ascontiguousarray(L0_attn_bo, dtype=np.float32).reshape(1, D),
        "Wg": np.ascontiguousarray(L0_ffn_Wg, dtype=np.float32),
        "W1": np.ascontiguousarray(L0_ffn_W1, dtype=np.float32),
        "W2": np.ascontiguousarray(L0_ffn_W2, dtype=np.float32),
        "Ws1": np.ascontiguousarray(L0_ffn_Ws1, dtype=np.float32),
        "Ws2": np.ascontiguousarray(L0_ffn_Ws2, dtype=np.float32),
    }
    nc = build(_stage)
    in_maps = [dict(common, x=np.ascontiguousarray(shards[c])) for c in range(NCORES)]
    res = run_bass_kernel_spmd(nc, in_maps, core_ids=list(range(NCORES)))
    outs = [res.results[c]["out"] for c in range(NCORES)]
    return np.concatenate(outs, axis=0).reshape(B, S, D)
